# version 05
# baseline (speedup 1.0000x reference)
import numpy as np
from contextlib import ExitStack

import concourse.bass as bass
import concourse.mybir as mybir
from concourse.bass_utils import run_bass_kernel_spmd

F32 = mybir.dt.float32
BF16 = mybir.dt.bfloat16
ALU = mybir.AluOpType
ACT = mybir.ActivationFunctionType
AX = mybir.AxisListType.X

T = 2048
D = 1024
DIN = 2048
NST = 16
DTR = 64
DFF = 4096
CH = 512
NCH = T // CH
E1 = 2144
E1P = 2176
CONVD = 1056
CONVDP = 1152
NH = 64
HP = 16
SSDCH = 256

MAGIC = float(1.5 * 2.0 ** 23)

N_CORES = 8
ENGINES = ("sync", "tensor", "vector", "scalar", "gpsimd")

PHASES = 7
TAPS = False


class PB:

    def __init__(self, nc, es):
        self.nc = nc
        self.es = es
        self.q = {k: [] for k in ENGINES}
        self.sems = {}
        for name in ("t", "v", "s", "cc"):
            self.sems[name] = [es.enter_context(nc.semaphore(name)), 0]
        self.waited = {k: {} for k in ENGINES}
        self.psmap, self.bank_w, self.bank_r = {}, {}, {}

    def _slot(self, name):
        if name not in self.sems:
            self.sems[name] = [self.es.enter_context(self.nc.semaphore(name)), 0]
        return name

    def tok(self, sem):
        return (sem, self.sems[sem][1])

    def emit(self, eng, fn, waits=(), inc=None, n=1):
        ws = []
        for w in waits:
            if w is None:
                continue
            sname, v = w
            if v <= 0:
                continue
            if v > self.waited[eng].get(sname, 0):
                self.waited[eng][sname] = v
                ws.append((sname, v))
        if inc is not None:
            self.sems[inc][1] += n

        def g(e, ws=tuple(ws), fn=fn, inc=inc, n=n):
            for sname, v in ws:
                e.wait_ge(self.sems[sname][0], v)
            r = fn(e)
            if inc is not None:
                r.then_inc(self.sems[inc][0], n)

        self.q[eng].append(g)
        return (inc, self.sems[inc][1]) if inc is not None else None

    def dma(self, q, out, in_, waits=(), slot="bulk"):
        sem = self._slot(("q." if q == "sync" else "g.") + slot)
        return self.emit(q, lambda e, o=out, i=in_: e.dma_start(out=o, in_=i), waits, inc=sem, n=16)

    def dma_acc(self, out, in_, waits=(), slot="acc"):
        sem = self._slot("g." + slot)
        return self.emit("gpsimd", lambda e, o=out, i=in_: e.dma_start(out=o, in_=i, accum_op=ALU.add),
                         waits, inc=sem, n=16)

    def drain(self, eng):
        self.q[eng].append(lambda e: e.drain())

    DEF_SEM = {"tensor": "t", "vector": "v", "scalar": "s"}

    def set_psum(self, psums):
        self.psmap = {g.ap().tensor.name: i for i, g in enumerate(psums)}
        self.bank_w = {}
        self.bank_r = {}

    def _banks_of(self, obj):
        if isinstance(obj, bass.AP):
            n = obj.tensor.name
            if n in self.psmap:
                return (self.psmap[n],)
        return ()

    def _psum_scan(self, args, kw):
        wr, rd = set(), set()
        for i, a in enumerate(args):
            (wr if i == 0 else rd).update(self._banks_of(a))
        for a in kw.values():
            rd.update(self._banks_of(a))
        return wr, rd

    def _psum_pre(self, eng, wr, rd):
        extra = []
        for b in rd | wr:
            w = self.bank_w.get(b)
            if w is not None and w[1] != eng:
                extra.append(w[0])
        for b in wr:
            for tk, e2 in self.bank_r.get(b, ()):
                if e2 != eng:
                    extra.append(tk)
        return extra

    def _psum_post(self, eng, wr, rd, tok):
        for b in wr:
            self.bank_w[b] = (tok, eng)
            self.bank_r[b] = []
        for b in rd:
            self.bank_r.setdefault(b, []).append((tok, eng))

    def op(self, eng, method, *args, waits=(), inc=None, n=1, **kw):
        wr, rd = self._psum_scan(args, kw)
        if (wr or rd) and inc is None:
            assert eng in self.DEF_SEM, (eng, method)
            inc = self.DEF_SEM[eng]
        extra = self._psum_pre(eng, wr, rd)

        def fn(e, method=method, args=args, kw=kw):
            return getattr(e, method)(*args, **kw)
        tok = self.emit(eng, fn, waits=tuple(waits) + tuple(extra), inc=inc, n=n)
        if wr or rd:
            self._psum_post(eng, wr, rd, tok)
        return tok

    def mm(self, out_ap, pairs, waits=(), inc="t", start=True, stop=True):
        tok = None
        npairs = len(pairs)
        for i, (st, mv) in enumerate(pairs):
            last = i == npairs - 1
            tok = self.op("tensor", "matmul", out_ap, st, mv, start=(start and i == 0), stop=(stop and last),
                          waits=(waits if i == 0 else ()), inc=(inc if last else None))
        return tok


class Banks:
    def __init__(self, tensors):
        self.t = list(tensors)
        self.free = [None] * len(tensors)
        self.rr = 0

    def get(self, k=1):
        out = []
        for _ in range(k):
            i = self.rr % len(self.t)
            self.rr += 1
            out.append(i)
        return out


def build(nc, num_cores=N_CORES, phases=PHASES, taps=TAPS):
    es = ExitStack()
    p = PB(nc, es)

    def din(name, shape, dtype=F32):
        return nc.dram_tensor(name, list(shape), dtype, kind="ExternalInput").ap()

    x0t = din("x0t", [D, T])
    w_inproj = din("w_inproj", [D, 2 * DIN])
    cdg0 = din("cdg0", [64, 128, 128])
    w_xproj = din("w_xproj", [DIN, 96])
    w_dtproj = din("w_dtproj", [DTR, DIN])
    w_outproj = din("w_outproj", [DIN, D])
    w_gate = din("w_gate", [D, DFF])
    w_up = din("w_up", [D, DFF])
    w_down = din("w_down", [DFF, D])
    w_ffn_in = din("w_ffn_in", [D, DFF])
    w_ffn_out = din("w_ffn_out", [DFF, D])
    w1_inT = din("w1_inT", [D, E1P])
    cdg1 = din("cdg1", [36, 128, 128])
    w1_outT = din("w1_outT", [DIN, D])
    w1_rT = din("w1_rT", [D, D])
    w1_kT = din("w1_kT", [D, D])
    w1_vT = din("w1_vT", [D, D])
    c_dtlnw = din("c_dtlnw", [64, 1])
    c_blnw = din("c_blnw", [16, 1])
    c_clnw = din("c_clnw", [16, 1])
    c_bclnw = din("c_bclnw", [32, 1])
    c_dtprojb = din("c_dtprojb", [128, 16])
    c_asc = din("c_asc", [128, 256])
    c_dsk = din("c_dsk", [128, 16])
    c_convb0 = din("c_convb0", [128, 16])
    c_premoe = din("c_premoe", [128, 8])
    c_bin = din("c_bin", [128, 32])
    c_bout = din("c_bout", [128, 8])
    c_dtbias1 = din("c_dtbias1", [64, 1])
    c_alog1 = din("c_alog1", [64, 1])
    c_dsk1 = din("c_dsk1", [128, 8])
    c_lnw1 = din("c_lnw1", [128, 16])
    c_lnb1 = din("c_lnb1", [128, 16])
    c_mur = din("c_mur", [128, 8])
    c_muk = din("c_muk", [128, 8])
    c_convb1 = din("c_convb1", [128, 9])
    k_ones = din("k_ones", [1, 128])
    k_mones = din("k_mones", [1, 128])
    k_ident = din("k_ident", [128, 128])
    k_c3 = din("k_c3", [96, 3])
    k_b3 = din("k_b3", [3, 96])
    k_sel8 = din("k_sel8", [8, 128])
    k_masktri = din("k_masktri", [128, 512])
    k_maskneg = din("k_maskneg", [128, 512])
    k_r1024 = din("k_r1024", [128, 1])
    k_r2048 = din("k_r2048", [128, 1])
    k_onescol = din("k_onescol", [128, 1])
    k_eps = din("k_eps", [128, 3])
    k_selhp = din("k_selhp", [64, 1024])
    k_insel = din("k_insel", [16, 1024])

    out = nc.dram_tensor("out", [D, T], F32, kind="ExternalOutput").ap()

    sp_hid = nc.dram_tensor("sp_hid", [DFF, T], F32, kind="Internal").ap()
    cc1i = nc.dram_tensor("cc1i", [1, 1], F32, kind="Internal").ap()
    cc1o = nc.dram_tensor("cc1o", [1, 1], F32, kind="Internal").ap()
    cc2i = nc.dram_tensor("cc2i", [1, 1], F32, kind="Internal").ap()
    cc2o = nc.dram_tensor("cc2o", [1, 1], F32, kind="Internal").ap()

    tap_list = []
    def tap(name, shape, dtype=F32):
        t = nc.dram_tensor(name, list(shape), dtype, kind="ExternalOutput").ap()
        tap_list.append(name)
        return t

    sb = lambda name, shape, dtype=F32: es.enter_context(nc.sbuf_tensor(name, list(shape), dtype))

    bufA = sb("bufA", [128, 16384])
    bufB = sb("bufB", [128, 32768], BF16)
    bcall = sb("bcall", [128, 16384], BF16)
    hstrip = sb("hstrip", [128, 8192], BF16)
    wsl = sb("wsl", [128, 6144], BF16)
    smallA = sb("smallA", [128, 1792])
    smallB = sb("smallB", [128, 1664])
    consts = sb("consts", [128, 384])
    constb = sb("constb", [128, 768], BF16)
    colsA = sb("colsA", [128, 512])
    carry = sb("carry", [128, 256])

    bufA_bf = bufA.ap().bitcast(BF16)
    bufB_f32 = bufB.ap().bitcast(F32)
    hstrip_f32 = hstrip.ap().bitcast(F32)
    bcall_f32 = bcall.ap().bitcast(F32)

    psum = [es.enter_context(nc.psum_tensor(f"ps{i}", [128, 512], F32)) for i in range(8)]
    p.set_psum(psum)
    banks = Banks(psum[:6])

    blk = es.enter_context(nc.Block())

    ONES = consts[0:1, 0:128]
    ONES64 = consts[64:65, 0:128]
    ONES32 = consts[32:33, 0:128]
    IDENT = consts.ap()[:, 128:256]
    C3 = consts[0:96, 256:259]
    B3 = consts[0:3, 260:356]
    R1024 = consts.ap()[:, 356:357]
    R2048 = consts.ap()[:, 357:358]
    ONESCOL = consts.ap()[:, 358:359]
    IDENTB = constb.ap()[:, 0:128]
    MASKTRI = constb.ap()[:, 128:640]
    SEL8 = constb[0:8, 640:768]

    DTLNW = colsA[0:64, 0:1]
    BLNW = colsA[0:16, 1:2]
    CLNW = colsA[0:16, 2:3]
    DTPROJB = colsA.ap()[:, 3:19]
    ASC = colsA.ap()[:, 19:275]
    DSK = colsA.ap()[:, 275:291]
    CONVB0 = colsA.ap()[:, 291:307]
    PREMOE = colsA.ap()[:, 307:315]
    BIN = colsA.ap()[:, 315:347]
    BOUT = colsA.ap()[:, 347:355]
    DTBIAS1 = colsA[0:64, 355:356]
    ALOG1 = colsA[0:64, 356:357]
    DSK1 = colsA.ap()[:, 357:365]
    LNW1 = colsA.ap()[:, 365:381]
    LNB1 = colsA.ap()[:, 381:397]
    MUR = colsA.ap()[:, 397:405]
    MUK = colsA.ap()[:, 405:413]
    ONEMMUR = colsA.ap()[:, 413:421]
    ONEMMUK = colsA.ap()[:, 421:429]
    CONVB1 = colsA.ap()[:, 429:438]
    AMAX1 = colsA.ap()[:, 438:439]
    S1COL = colsA.ap()[:, 439:440]
    INV1COL = colsA.ap()[:, 440:441]
    S2COL = colsA.ap()[:, 441:442]
    AMAX2 = colsA.ap()[:, 442:443]
    COLAMAX = colsA.ap()[:, 443:475]
    COLSC = colsA.ap()[:, 475:507]
    EPS5 = colsA.ap()[:, 507:508]
    EPS6 = colsA.ap()[:, 508:509]
    EPSG = colsA.ap()[:, 509:510]

    t_sy = []
    p.dma("sync", consts[0:1, 0:128], k_ones, slot="boot")
    p.dma("sync", consts[64:65, 0:128], k_ones, slot="boot")
    p.dma("sync", consts[32:33, 0:128], k_ones, slot="boot")
    p.dma("sync", consts.ap()[:, 128:256], k_ident, slot="boot")
    p.dma("sync", consts[0:96, 256:259], k_c3, slot="boot")
    p.dma("sync", consts[0:3, 260:356], k_b3, slot="boot")
    p.dma("sync", consts.ap()[:, 356:357], k_r1024, slot="boot")
    p.dma("sync", consts.ap()[:, 357:358], k_r2048, slot="boot")
    p.dma("sync", consts.ap()[:, 358:359], k_onescol, slot="boot")
    p.dma("gpsimd", constb.ap()[:, 0:128], k_ident, slot="bootg")
    p.dma("gpsimd", constb.ap()[:, 128:640], k_masktri, slot="bootg")
    p.dma("gpsimd", constb[0:8, 640:768], k_sel8, slot="bootg")
    p.dma("sync", colsA[0:64, 0:1], c_dtlnw, slot="boot")
    p.dma("sync", colsA[0:16, 1:2], c_blnw, slot="boot")
    p.dma("sync", colsA[0:16, 2:3], c_clnw, slot="boot")
    p.dma("sync", colsA[64:96, 1:2], c_bclnw, slot="boot")
    p.dma("sync", colsA.ap()[:, 3:19], c_dtprojb, slot="boot")
    p.dma("sync", colsA.ap()[:, 19:275], c_asc, slot="boot")
    p.dma("sync", colsA.ap()[:, 275:291], c_dsk, slot="boot")
    p.dma("sync", colsA.ap()[:, 291:307], c_convb0, slot="boot")
    p.dma("sync", colsA.ap()[:, 307:315], c_premoe, slot="boot")
    p.dma("sync", colsA.ap()[:, 315:347], c_bin, slot="boot")
    p.dma("sync", colsA.ap()[:, 347:355], c_bout, slot="boot")
    p.dma("sync", colsA[0:64, 355:356], c_dtbias1, slot="boot")
    p.dma("sync", colsA[0:64, 356:357], c_alog1, slot="boot")
    p.dma("sync", colsA.ap()[:, 357:365], c_dsk1, slot="boot")
    p.dma("sync", colsA.ap()[:, 365:381], c_lnw1, slot="boot")
    p.dma("sync", colsA.ap()[:, 381:397], c_lnb1, slot="boot")
    p.dma("sync", colsA.ap()[:, 397:405], c_mur, slot="boot")
    p.dma("sync", colsA.ap()[:, 405:413], c_muk, slot="boot")
    p.dma("sync", colsA.ap()[:, 429:438], c_convb1, slot="boot")
    p.dma("sync", colsA.ap()[:, 507:510], k_eps, slot="boot")

    tok_consts = p.tok("q.boot")
    tok_constb = p.tok("g.bootg")

    p.emit("scalar", lambda e: e.activation(ASC, ASC, ACT.Exp, bias=0.0, scale=1.0), waits=[tok_consts], inc="s")
    p.emit("scalar", lambda e: e.activation(colsA[0:64, 356:357], colsA[0:64, 356:357], ACT.Exp, bias=0.0, scale=1.0), inc="s")

    p.drain("scalar")
    p.emit("vector", lambda e: e.tensor_scalar(colsA.ap()[:, 3:19], colsA.ap()[:, 3:19], -1.0, None, op0=ALU.mult),
           waits=[tok_consts], inc="v")
    p.emit("vector", lambda e: e.tensor_scalar(colsA[0:64, 355:356], colsA[0:64, 355:356], -1.0, None, op0=ALU.mult), inc="v")
    p.emit("vector", lambda e: e.tensor_scalar(ONEMMUR, MUR, -1.0, 1.0, op0=ALU.mult, op1=ALU.add), inc="v")
    p.emit("vector", lambda e: e.tensor_scalar(ONEMMUK, MUK, -1.0, 1.0, op0=ALU.mult, op1=ALU.add), inc="v")
    p.emit("vector", lambda e: e.tensor_scalar(colsA[64:80, 1:2], colsA[64:80, 1:2], -1.0, None, op0=ALU.mult), inc="v")

    p.drain("vector")
    tok_setup_v = p.tok("v")
    tok_setup_s = p.tok("s")

    def xn_t(d, c0, c1):
        return bufA.ap()[:, d * T + c0: d * T + c1]

    x0sb = bufB_f32
    def x0_t(d, c0, c1):
        return x0sb[:, d * T + c0: d * T + c1]

    tok_x0 = p.dma("sync",
                   bass.AP(x0sb.tensor, x0sb.offset, [[16384, 128], [2048, 8], [1, 2048]]),
                   x0t.rearrange("(a p) t -> p a t", p=128), slot="x0")

    redbuf = smallA.ap()[:, 0:64]
    sqst = [smallB.ap()[:, i * 512:(i + 1) * 512] for i in range(2)]
    musb = smallA[0:1, 64:576]
    rrow = smallA[0:1, 576:1088]

    def emit_layernorm(src_t, dst_t, red_off, src_tok, do_red=True):
        lastv = None
        prev_chunk = None
        for c in range(NCH):
            c0, c1 = c * CH, (c + 1) * CH
            bmu, bsq, brb, bab = banks.get(4)
            pairs_mu = [(R1024, src_t(d, c0, c1)) for d in range(8)]
            tmu = p.mm(psum[bmu][0:1, 0:CH], pairs_mu, waits=[src_tok, tok_consts, banks.free[bmu]])
            sq_mm = [prev_chunk, prev_chunk]
            for d in range(8):
                stok = p.op("scalar", "activation", sqst[d % 2], src_t(d, c0, c1), ACT.Square,
                            bias=0.0, scale=1.0, waits=[src_tok, sq_mm[d % 2]], inc="s")
                mtok = p.op("tensor", "matmul", psum[bsq][0:1, 0:CH], R1024, sqst[d % 2],
                            start=(d == 0), stop=(d == 7),
                            waits=[stok] + ([banks.free[bsq]] if d == 0 else []), inc="t")
                sq_mm[d % 2] = mtok
            tsq = sq_mm[1]
            tmu2 = p.op("vector", "tensor_copy", musb, psum[bmu][0:1, 0:CH], waits=[tmu], inc="v")
            banks.free[bmu] = tmu2
            p.drain("vector")
            tm2 = p.op("vector", "tensor_tensor", rrow, musb, musb, op=ALU.mult, inc="v")
            p.drain("vector")
            tvar = p.op("vector", "tensor_tensor", rrow, psum[bsq][0:1, 0:CH], rrow, op=ALU.subtract,
                        waits=[tsq], inc="v")
            banks.free[bsq] = tvar
            ts1 = p.op("scalar", "activation", rrow, rrow, ACT.Sqrt, bias=EPS5[0:1, 0:1], scale=1.0,
                       waits=[tvar], inc="s")
            tr = p.op("vector", "reciprocal", rrow, rrow, waits=[ts1], inc="v")
            p.drain("vector")
            trb = p.mm(psum[brb][:, 0:CH], [(ONES, rrow)], waits=[tr, banks.free[brb]])
            tab = p.mm(psum[bab][:, 0:CH], [(ONES, musb)], waits=[tmu2, tr, banks.free[bab]])
            xn_toks = []
            for d in range(8):
                t1 = p.op("vector", "tensor_tensor", sqst[d % 2], src_t(d, c0, c1), psum[bab][:, 0:CH],
                          op=ALU.subtract, waits=[trb, tab, (None if d < 2 else xn_toks[d - 2])], inc="v")
                p.drain("vector")
                t2 = p.op("vector", "tensor_tensor", dst_t(d, c0, c1), sqst[d % 2], psum[brb][:, 0:CH],
                          op=ALU.mult, waits=[tab], inc="v")
                xn_toks.append(t2)
                if do_red:
                    p.drain("vector")
                    t3 = p.op("vector", "tensor_reduce",
                              redbuf[:, red_off + c * 8 + d: red_off + c * 8 + d + 1],
                              dst_t(d, c0, c1), axis=AX, op=ALU.max, apply_absolute_value=True, inc="v")
                    lastv = t3
                else:
                    lastv = t2
            banks.free[brb] = lastv
            banks.free[bab] = lastv
            prev_chunk = xn_toks[-1]
        return lastv

    tok_xn = emit_layernorm(x0_t, xn_t, 0, tok_x0)

    am_col = smallA.ap()[:, 1664:1665]
    am1 = smallA[0:1, 1671:1672]
    t1 = p.emit("vector", lambda e: e.tensor_reduce(am_col, redbuf[:, 0:32], axis=AX, op=ALU.max), waits=[tok_xn], inc="v")
    btr = banks.get(1)[0]
    t2 = p.op("tensor", "transpose", psum[btr][0:1, 0:128], am_col, IDENT,
              waits=[t1, banks.free[btr], tok_consts], inc="t")
    t3 = p.op("vector", "tensor_reduce", am1, psum[btr][0:1, 0:128], axis=AX, op=ALU.max, waits=[t2], inc="v")
    banks.free[btr] = t3
    tok_am1_dma = p.dma("sync", cc1i, am1, waits=[t3], slot="cc1w")
    tok_cc1 = p.emit("gpsimd",
                     lambda e: e.collective_compute("AllReduce", ALU.max, replica_groups=[list(range(num_cores))],
                                                    ins=[cc1i], outs=[cc1o]),
                     waits=[tok_am1_dma], inc="cc")
    tok_am1_back = p.dma("gpsimd", am1, cc1o, waits=[tok_cc1], slot="cc1r")
    s1row = smallA[0:1, 1672:1673]
    inv1row = smallA[0:1, 1673:1674]
    t4 = p.emit("vector", lambda e: e.reciprocal(s1row, am1), waits=[tok_am1_back], inc="v")
    p.drain("vector")
    t5 = p.emit("vector", lambda e: e.tensor_scalar(s1row, s1row, 32767.0, None, op0=ALU.mult), inc="v")
    t6 = p.emit("vector", lambda e: e.tensor_scalar(inv1row, am1, 1.0 / 32767.0, None, op0=ALU.mult),
                waits=[tok_am1_back], inc="v")
    bq = banks.get(1)[0]
    t7a = p.mm(psum[bq][:, 0:1], [(ONES, am1)], waits=[tok_am1_back, banks.free[bq]])
    t7b = p.mm(psum[bq][:, 1:2], [(ONES, s1row)], waits=[t5])
    t7c = p.mm(psum[bq][:, 2:3], [(ONES, inv1row)], waits=[t6])
    t8 = p.op("vector", "tensor_copy", AMAX1, psum[bq][:, 0:1], waits=[t7a], inc="v")
    t8b = p.op("vector", "tensor_copy", S1COL, psum[bq][:, 1:2], waits=[t7b], inc="v")
    t8c = p.op("vector", "tensor_copy", INV1COL, psum[bq][:, 2:3], waits=[t7c], inc="v")
    banks.free[bq] = t8c
    tok_s1 = t8b
    tok_inv1 = t8c

    if taps:
        tp_xn = tap("tp_xn", [D, T])
        p.dma("sync", tp_xn.rearrange("(a p) t -> p a t", p=128),
              bass.AP(bufA.ap().tensor, 0, [[16384, 128], [2048, 8], [1, 2048]]), waits=[tok_xn], slot="tap")
        tp_am = tap("tp_am", [1, 1])
        p.dma("sync", tp_am, am1, waits=[tok_am1_back], slot="tap")

    tok_hid_spill = None
    tok_cc2 = None
    if phases >= 1:
        K_in = bufB.ap()
        def kin_ap(k, m0, m1):
            return bufB.ap()[:, k * 4096 + m0: k * 4096 + m1]
        wfa = hstrip_f32
        qtmp = smallB.ap()[:, 0:512]
        qsrowA = smallA[0:1, 64:576]
        cmx = smallA.ap()[:, 1666:1670]

        tok_prev_use = [None] * 8
        for ms in range(8):
            m0 = ms * 512
            ldt = []
            for k in range(8):
                tok = p.dma("sync", wfa[:, k * 512:(k + 1) * 512],
                            w_ffn_in[k * 128:(k + 1) * 128, m0:m0 + 512],
                            waits=[tok_prev_use[k]], slot="wf%d" % k)
                ldt.append(tok)
            lv1 = [smallB.ap()[:, 0:512], smallB.ap()[:, 512:1024]]
            lvm = smallB.ap()[:, 1024:1536]
            p.op("vector", "tensor_tensor", lv1[0], wfa[:, 0:512], wfa[:, 512:1024], op=ALU.max,
                 waits=[ldt[0], ldt[1]], inc="v")
            p.op("vector", "tensor_tensor", lvm, wfa[:, 0:512], wfa[:, 512:1024], op=ALU.min, inc="v")
            for k2 in (1, 2, 3):
                c0 = k2 * 1024
                p.op("vector", "tensor_tensor", lv1[1], wfa[:, c0:c0 + 512], wfa[:, c0 + 512:c0 + 1024],
                     op=ALU.max, waits=[ldt[2 * k2], ldt[2 * k2 + 1]], inc="v")
                p.drain("vector")
                p.op("vector", "tensor_tensor", lv1[0], lv1[0], lv1[1], op=ALU.max, inc="v")
                p.drain("vector")
                p.op("vector", "tensor_tensor", lv1[1], wfa[:, c0:c0 + 512], wfa[:, c0 + 512:c0 + 1024],
                     op=ALU.min, inc="v")
                p.drain("vector")
                p.op("vector", "tensor_tensor", lvm, lvm, lv1[1], op=ALU.min, inc="v")
                p.drain("vector")
            tG = p.op("vector", "scalar_tensor_tensor", lv1[0], lvm, -1.0, lv1[0],
                      op0=ALU.mult, op1=ALU.max, inc="v")
            last_tt = None
            for j in range(4):
                bt = banks.get(1)[0]
                tt = p.op("tensor", "transpose", psum[bt][:, 0:128], lv1[0][:, j * 128:(j + 1) * 128], IDENT,
                          waits=[tG, banks.free[bt]], inc="t")
                last_tt = tt
                tr2 = p.op("vector", "tensor_reduce", cmx[:, j:j + 1], psum[bt][:, 0:128], axis=AX, op=ALU.max,
                           waits=[tt], inc="v")
                banks.free[bt] = tr2
            p.drain("vector")
            p.op("vector", "tensor_scalar", COLAMAX[:, ms * 4:ms * 4 + 4], cmx, 1e-8, None, op0=ALU.max, inc="v")
            p.drain("vector")
            p.op("vector", "tensor_scalar", COLSC[:, ms * 4:ms * 4 + 4], COLAMAX[:, ms * 4:ms * 4 + 4],
                 1.0 / 127.0, None, op0=ALU.mult, inc="v")
            p.op("vector", "reciprocal", cmx, COLAMAX[:, ms * 4:ms * 4 + 4], inc="v")
            p.drain("vector")
            t12 = p.op("vector", "tensor_scalar", cmx, cmx, 127.0, None, op0=ALU.mult, inc="v")
            t14s = []
            bt = banks.get(1)[0]
            for j in range(4):
                t13 = p.op("tensor", "transpose", psum[bt][0:1, j * 128:(j + 1) * 128], cmx[:, j:j + 1], IDENT,
                           waits=[t12] + ([banks.free[bt]] if j == 0 else []), inc="t")
                t14 = p.op("vector", "tensor_copy", qsrowA[:, j * 128:(j + 1) * 128],
                           psum[bt][0:1, j * 128:(j + 1) * 128], waits=[t13, tok_inv1], inc="v")
                t14s.append(t14)
            banks.free[bt] = t14s[-1]
            bqb = banks.get(1)[0]
            t15 = p.mm(psum[bqb][:, 0:512], [(ONES, qsrowA)], waits=t14s + [banks.free[bqb]])
            lastq = None
            for k in range(8):
                tq1 = p.op("vector", "tensor_tensor", qtmp, wfa[:, k * 512:(k + 1) * 512], psum[bqb][:, 0:512],
                           op=ALU.mult, waits=[t15, last_tt], inc="v")
                p.drain("vector")
                p.op("vector", "tensor_scalar", qtmp, qtmp, MAGIC, None, op0=ALU.add, inc="v")
                p.drain("vector")
                p.op("vector", "tensor_scalar", qtmp, qtmp, -MAGIC, -128.0, op0=ALU.add, op1=ALU.max, inc="v")
                p.drain("vector")
                tq4 = p.op("vector", "tensor_scalar", kin_ap(k, m0, m0 + 512), qtmp, 127.0, None,
                           op0=ALU.min, inc="v")
                p.drain("vector")
                lastq = tq4
                tok_prev_use[k] = tq1
            banks.free[bqb] = lastq
        tok_kin = p.tok("v")

        hred = smallB.ap()[:, 1024:1152]
        hstage = [smallB.ap()[:, 1152:1664], smallB.ap()[:, 0:512]]
        qpre = [smallB.ap()[:, 0:512], smallB.ap()[:, 512:1024]]
        hred_n = 0
        tok_hstage = [[], []]
        tok_xsim_use = [None] * 8
        p.drain("vector")
        for c in range(NCH):
            c0 = c * CH
            xs_toks = []
            for d in range(8):
                xs = wsl.ap()[:, d * 512:(d + 1) * 512]
                p.op("vector", "tensor_scalar", qpre[d % 2], xn_t(d, c0, c0 + CH), S1COL, MAGIC,
                     op0=ALU.mult, op1=ALU.add,
                     waits=[tok_s1, tok_xn, tok_xsim_use[d]] + tok_hstage[0] + tok_hstage[1], inc="v")
                p.drain("vector")
                p.op("vector", "tensor_scalar", qpre[d % 2], qpre[d % 2], -MAGIC, -32768.0,
                     op0=ALU.add, op1=ALU.max, inc="v")
                p.drain("vector")
                tq3 = p.op("vector", "tensor_scalar", xs, qpre[d % 2], 32767.0, INV1COL,
                           op0=ALU.min, op1=ALU.mult, waits=[tok_inv1], inc="v")
                xs_toks.append(tq3)
            for m in range(32):
                bmm = banks.get(1)[0]
                pairs = [(kin_ap(k, m * 128, (m + 1) * 128), wsl.ap()[:, k * 512:(k + 1) * 512]) for k in range(8)]
                tmm = p.mm(psum[bmm][:, 0:CH], pairs, waits=xs_toks + [tok_kin, banks.free[bmm]])
                tok_xsim_use = [tmm] * 8
                hs = hstage[m % 2]
                tg = p.op("scalar", "activation", hs, psum[bmm][:, 0:CH], ACT.Gelu,
                          bias=BIN[:, m:m + 1], scale=COLSC[:, m:m + 1],
                          waits=[tmm, tok_kin] + tok_hstage[m % 2], inc="s")
                banks.free[bmm] = tg
                tr = p.op("vector", "tensor_reduce", hred[:, hred_n:hred_n + 1], hs, axis=AX,
                          op=ALU.max, apply_absolute_value=True, waits=[tg], inc="v")
                hred_n += 1
                tspill = p.dma("sync", sp_hid[m * 128:(m + 1) * 128, c0:c0 + CH], hs, waits=[tg],
                               slot="hs%d" % (m % 2))
                tok_hstage[m % 2] = [tspill, tr]
        tok_hid_spill = [p.tok("q.hs0"), p.tok("q.hs1")]
        am2loc = smallA.ap()[:, 1665:1666]
        am2rowT = smallA[0:1, 1674:1675]
        t1 = p.emit("vector", lambda e: e.tensor_reduce(am2loc, hred, axis=AX, op=ALU.max), waits=[p.tok("v"), p.tok("s")], inc="v")
        bt = banks.get(1)[0]
        t2 = p.op("tensor", "transpose", psum[bt][0:1, 0:128], am2loc, IDENT,
                  waits=[t1, banks.free[bt]], inc="t")
        t3 = p.op("vector", "tensor_reduce", am2rowT[0:1, 0:1], psum[bt][0:1, 0:128],
                  axis=AX, op=ALU.max, waits=[t2], inc="v")
        banks.free[bt] = t3
        tok_am2_dma = p.dma("sync", cc2i, am2rowT[0:1, 0:1], waits=[t3], slot="cc2w")
        tok_cc2 = p.emit("gpsimd",
                         lambda e: e.collective_compute("AllReduce", ALU.max, replica_groups=[list(range(num_cores))],
                                                        ins=[cc2i], outs=[cc2o]),
                         waits=[tok_am2_dma], inc="cc")

        if taps:
            tp_h = tap("tp_hid", [DFF, CH])
            p.dma("sync", tp_h, sp_hid[:, 0:CH], waits=tok_hid_spill, slot="tap")



    if phases >= 2:
        p.drain("vector")
        p.drain("scalar")
        gate = [p.tok("v"), p.tok("s"), p.tok("t")] + (tok_hid_spill or [])
        if "q.tap" in p.sems:
            gate.append(p.tok("q.tap"))
        xc_t = lambda d: bufB.ap()[:, d * 512:(d + 1) * 512]
        zsil_t = lambda d: bufB.ap()[:, 8192 + d * 512: 8192 + (d + 1) * 512]
        mdt_t = lambda d: bufB.ap()[:, 16384 + d * 512: 16384 + (d + 1) * 512]
        dtxy_t = lambda d: bufB.ap()[:, 24576 + d * 512: 24576 + (d + 1) * 512]
        bbc_t = lambda n: bcall.ap()[:, n * 512:(n + 1) * 512]
        cbc_t = lambda n: bcall.ap()[:, 8192 + n * 512: 8192 + (n + 1) * 512]
        xnbf_t = lambda k: hstrip.ap()[:, k * 512:(k + 1) * 512]
        h_t = lambda n: hstrip.ap()[:, n * 512:(n + 1) * 512]
        cdsl_t = lambda si: bass.AP(hstrip.ap().tensor, 4096 + si * 512, [[8192, 128], [128, 4], [1, 128]])
        cdblk = lambda si, kk: bass.AP(hstrip.ap().tensor, 4096 + si * 512 + kk * 128, [[8192, 128], [1, 128]])
        wslab_blk = lambda si, k: bass.AP(wsl.ap().tensor, si * 1024 + k * 128, [[6144, 128], [1, 128]])
        xpw_blk = lambda k: bass.AP(wsl.ap().tensor, 2048 + k * 96, [[6144, 128], [1, 96]])
        dtw_blk = lambda m: bass.AP(wsl.ap().tensor, (m % 2) * 64 * 6144 + 3584 + (m // 2) * 128, [[6144, 64], [1, 128]])
        stage = wsl.ap()[:, 4608:5123]
        halo = wsl.ap()[:, 5123:5171]
        dtn0 = wsl[0:64, 5171:5683]
        dtn64 = wsl[64:128, 5171:5683]
        proj96 = smallA[0:96, 0:512]
        projBC = smallA[64:96, 0:512]
        sq96 = smallA[0:96, 512:1024]
        r3 = smallA[0:3, 1024:1536]
        BCm = smallB[64:96, 0:512]
        t1buf = smallB.ap()[:, 1024:1536]

        def selBC(row):
            return bass.AP(consts.ap().tensor, 64 * 384 + 128 + 64 + row, [[384, 32], [0, 128]])

        tok_xpw = p.dma("gpsimd", bass.AP(wsl.ap().tensor, 2048, [[6144, 128], [96, 16], [1, 96]]),
                        bass.AP(w_xproj.tensor, 0, [[96, 128], [96 * 128, 16], [1, 96]]),
                        waits=gate, slot="wxp")
        tok_dtw = []
        for hh in range(2):
            dst = bass.AP(wsl.ap().tensor, hh * 64 * 6144 + 3584, [[6144, 64], [128, 8], [1, 128]])
            srcap = bass.AP(w_dtproj.tensor, hh * 128, [[2048, 64], [256, 8], [1, 128]])
            tok_dtw.append(p.dma("gpsimd", dst, srcap, waits=gate, slot="wdt%d" % hh))
        th0 = p.op("vector", "memset", halo, 0.0, waits=gate, inc="v")
        p.drain("vector")

        slab_free = [None, None]
        cd_free = [None, None]
        prev_h_user = None
        prev_h_user_t = None
        halo_tok = [th0] * 16
        stage_free = None
        carry_tok = [None] * 16
        prev_scan_user = None
        dA_free = [None, None]

        for c in range(NCH):
            c0 = c * CH
            xnbf_toks = []
            for k in range(8):
                tkx = p.op("vector", "tensor_copy", xnbf_t(k), xn_t(k, c0, c0 + CH),
                           waits=gate + [prev_h_user, prev_h_user_t, tok_xn], inc="v")
                xnbf_toks.append(tkx)
            p.drain("vector")
            last_conv_mm = None
            for half in range(2):
                coff = 0 if half == 0 else DIN
                for m in range(16):
                    si = m % 2
                    srcap = bass.AP(w_inproj.tensor, coff + m * 128, [[4096, 128], [4096 * 128, 8], [1, 128]])
                    dstap = bass.AP(wsl.ap().tensor, si * 1024, [[6144, 128], [128, 8], [1, 128]])
                    twl = p.dma("gpsimd", dstap, srcap, waits=[slab_free[si]], slot="ws%d" % si)
                    bmm = banks.get(1)[0]
                    pairs = [(wslab_blk(si, k), xnbf_t(k)) for k in range(8)]
                    tmm = p.mm(psum[bmm][:, 0:CH], pairs, waits=xnbf_toks + [twl, banks.free[bmm]])
                    slab_free[si] = tmm
                    if half == 1:
                        tz = p.op("scalar", "activation", zsil_t(m), psum[bmm][:, 0:CH], ACT.Silu,
                                  bias=0.0, scale=1.0, waits=[tmm, prev_h_user, prev_h_user_t], inc="s")
                        banks.free[bmm] = tz
                        continue
                    thalo = p.op("vector", "tensor_copy", stage[:, 0:3], halo[:, m * 3:m * 3 + 3],
                                 waits=[halo_tok[m], stage_free], inc="v")
                    p.drain("vector")
                    tev = p.op("vector", "tensor_copy", stage[:, 3:515], psum[bmm][:, 0:CH],
                               waits=[tmm], inc="v")
                    banks.free[bmm] = tev
                    p.drain("vector")
                    thl = p.op("vector", "tensor_copy", halo[:, m * 3:m * 3 + 3], stage[:, 512:515], inc="v")
                    halo_tok[m] = thl
                    csrc = bass.AP(cdg0.tensor, m * 4 * 128 * 128, [[128, 128], [128 * 128, 4], [1, 128]])
                    tcd = p.dma("gpsimd", cdsl_t(si), csrc, waits=[cd_free[si], prev_h_user, prev_h_user_t], slot="cd%d" % si)
                    bcv = banks.get(1)[0]
                    cpairs = [(cdblk(si, kk), stage[:, kk:kk + 512]) for kk in range(4)]
                    tcv = p.mm(psum[bcv][:, 0:CH], cpairs, waits=[tcd, tev, thl, thalo, banks.free[bcv]])
                    cd_free[si] = tcv
                    stage_free = tcv
                    last_conv_mm = tcv
                    txc = p.op("scalar", "activation", xc_t(m), psum[bcv][:, 0:CH], ACT.Silu,
                               bias=CONVB0[:, m:m + 1], scale=1.0, waits=[tcv, prev_h_user, prev_h_user_t], inc="s")
                    banks.free[bcv] = txc
            tok_xc = p.tok("s")
            bxp = banks.get(1)[0]
            xpairs = [(xpw_blk(k), xc_t(k)) for k in range(16)]
            txp = p.mm(psum[bxp][0:96, 0:CH], xpairs, waits=[tok_xpw, tok_xc, banks.free[bxp]])
            tpe = p.op("vector", "tensor_copy", proj96, psum[bxp][0:96, 0:CH], waits=[txp], inc="v")
            banks.free[bxp] = tpe
            tsqp = p.op("scalar", "activation", sq96, proj96, ACT.Square, bias=0.0, scale=1.0,
                        waits=[tpe], inc="s")
            bms = banks.get(1)[0]
            tms = p.mm(psum[bms][0:3, 0:CH], [(C3, sq96)], waits=[tsqp, banks.free[bms]])
            tr3 = p.op("scalar", "activation", r3, psum[bms][0:3, 0:CH], ACT.Sqrt,
                       bias=EPS6[0:3, 0:1], scale=1.0, waits=[tms], inc="s")
            banks.free[bms] = tr3
            trc = p.op("vector", "reciprocal", r3, r3, waits=[tr3], inc="v")
            p.drain("vector")
            brb = banks.get(1)[0]
            trb = p.mm(psum[brb][0:96, 0:CH], [(B3, r3)], waits=[trc, banks.free[brb]])
            tdtn = p.op("vector", "scalar_tensor_tensor", dtn0, smallA[0:64, 0:512], DTLNW,
                        psum[brb][0:64, 0:CH], op0=ALU.mult, op1=ALU.mult, waits=[trb, tok_dtw[0], tok_dtw[1]], inc="v")
            tbcm = p.op("vector", "scalar_tensor_tensor", BCm, projBC, colsA[64:96, 1:2],
                        psum[brb][64:96, 0:CH], op0=ALU.mult, op1=ALU.mult, waits=[trb, tok_setup_v], inc="v")
            banks.free[brb] = tbcm
            tdtn64 = p.dma("gpsimd", dtn64, dtn0, waits=[tdtn], slot="dtnc")
            tln_prev = None
            for m in range(16):
                bdt = banks.get(1)[0]
                tdm = p.mm(psum[bdt][:, 0:CH], [(dtw_blk(m), dtn0 if m % 2 == 0 else dtn64)],
                           waits=[tdtn if m % 2 == 0 else tdtn64, banks.free[bdt]])
                tsg = p.op("scalar", "activation", t1buf, psum[bdt][:, 0:CH], ACT.Sigmoid,
                           bias=DTPROJB[:, m:m + 1], scale=-1.0, waits=[tdm, tln_prev, tok_setup_v], inc="s")
                banks.free[bdt] = tsg
                p.drain("scalar")
                tln = p.op("scalar", "activation", mdt_t(m), t1buf, ACT.Ln,
                           bias=EPSG, scale=1.0, waits=[prev_h_user, prev_h_user_t], inc="s")
                p.drain("scalar")
                tln_prev = tln
            tok_mdt = p.tok("s")
            dtx_toks = []
            for m in range(16):
                td = p.op("vector", "tensor_tensor", dtxy_t(m), mdt_t(m), xc_t(m), op=ALU.mult,
                          waits=[tok_mdt, tok_xc, prev_h_user, prev_h_user_t], inc="v")
                dtx_toks.append(td)
            bc_toks = []
            for n in range(16):
                bb = banks.get(1)[0]
                tb = p.mm(psum[bb][:, 0:CH], [(selBC(n), BCm)], waits=[tbcm, tok_consts, banks.free[bb]])
                te = p.op("scalar", "activation", bbc_t(n), psum[bb][:, 0:CH], ACT.Copy,
                          bias=0.0, scale=1.0, waits=[tb, prev_scan_user], inc="s")
                banks.free[bb] = te
                bc = banks.get(1)[0]
                tb2 = p.mm(psum[bc][:, 0:CH], [(selBC(16 + n), BCm)], waits=[tbcm, banks.free[bc]])
                te2 = p.op("scalar", "activation", cbc_t(n), psum[bc][:, 0:CH], ACT.Copy,
                           bias=0.0, scale=1.0, waits=[tb2, prev_scan_user], inc="s")
                banks.free[bc] = te2
                bc_toks.append((te, te2))
            bbuf = [wsl.ap()[:, 4608:5120], wsl.ap()[:, 5171:5683]]
            bbuf_free = [last_conv_mm, (tdtn64 if True else None)]
            bbuf_free[1] = p.tok("t")
            sum_evac = None
            new_h_user = None
            for dt_ in range(16):
                bsum = banks.get(1)[0]
                scan_toks = []
                for n in range(16):
                    dab = 6 + (n % 2)
                    tda = p.op("scalar", "activation", psum[dab][:, 0:CH], mdt_t(dt_), ACT.Exp,
                               scale=ASC[:, n * 16 + dt_:n * 16 + dt_ + 1], bias=0.0,
                               waits=[tok_mdt, dA_free[n % 2], tok_setup_s], inc="s")
                    tb = p.op("vector", "tensor_tensor", bbuf[n % 2], dtxy_t(dt_), bbc_t(n), op=ALU.mult,
                              waits=[dtx_toks[dt_], bc_toks[n][0], bbuf_free[n % 2]], inc="v")
                    p.drain("vector")
                    init = 0.0 if c == 0 else carry[:, dt_ * 16 + n: dt_ * 16 + n + 1]
                    tsc = p.op("vector", "tensor_tensor_scan", h_t(n), psum[dab][:, 0:CH], bbuf[n % 2],
                               init, op0=ALU.mult, op1=ALU.add,
                               waits=[tda, tb, carry_tok[dt_], prev_h_user, last_conv_mm,
                                      xnbf_toks[-1], p.tok("t") if (dt_ == 0 and n == 0) else None], inc="v")
                    p.drain("vector")
                    dA_free[n % 2] = tsc
                    bbuf_free[n % 2] = tsc
                    scan_toks.append(tsc)
                if c < NCH - 1:
                    tcar = p.op("vector", "tensor_copy", carry[:, dt_ * 16:dt_ * 16 + 16],
                                bass.AP(hstrip.ap().tensor, 511, [[8192, 128], [512, 16]]),
                                waits=scan_toks, inc="v")
                    carry_tok[dt_] = tcar
                    p.drain("vector")
                tsum = None
                for n in range(16):
                    tp_ = p.op("vector", "tensor_tensor", h_t(n), h_t(n), cbc_t(n), op=ALU.mult,
                               waits=[scan_toks[n], bc_toks[n][1], carry_tok[dt_]], inc="v")
                    p.drain("vector")
                    tsum = p.op("tensor", "matmul", psum[bsum][:, 0:CH], IDENTB, h_t(n),
                                start=(n == 0), stop=(n == 15),
                                waits=[tp_, tok_constb] + ([banks.free[bsum]] if n == 0 else []),
                                inc=("t" if n == 15 else None))
                    new_h_user = tsum
                ty1 = p.op("vector", "scalar_tensor_tensor", t1buf, xc_t(dt_), DSK[:, dt_:dt_ + 1],
                           psum[bsum][:, 0:CH], op0=ALU.mult, op1=ALU.add,
                           waits=[tsum, p.tok("s")], inc="v")
                banks.free[bsum] = ty1
                p.drain("vector")
                ty2 = p.op("vector", "tensor_tensor", dtxy_t(dt_), t1buf, zsil_t(dt_), op=ALU.mult, inc="v")
                p.drain("vector")
            prev_scan_user = p.tok("v")
            prev_h_user_next = p.tok("v")
            tok_y = p.tok("v")
            for m in range(8):
                tw1 = p.dma("gpsimd", bass.AP(wsl.ap().tensor, 0, [[6144, 128], [128, 8], [1, 128]]),
                            bass.AP(w_outproj.tensor, m * 128, [[1024, 128], [1024 * 128, 8], [1, 128]]),
                            waits=[slab_free[0]], slot="ws0")
                tw2 = p.dma("gpsimd", bass.AP(wsl.ap().tensor, 1024, [[6144, 128], [128, 8], [1, 128]]),
                            bass.AP(w_outproj.tensor, 8 * 128 * 1024 + m * 128, [[1024, 128], [1024 * 128, 8], [1, 128]]),
                            waits=[slab_free[1]], slot="ws1")
                bop = banks.get(1)[0]
                pairs = [(wslab_blk(0, k), dtxy_t(k)) for k in range(8)] +                         [(wslab_blk(1, k), dtxy_t(8 + k)) for k in range(8)]
                tmo = p.mm(psum[bop][:, 0:CH], pairs, waits=[tw1, tw2, tok_y, banks.free[bop]])
                slab_free[0] = tmo
                slab_free[1] = tmo
                thid = p.op("vector", "tensor_tensor", xn_t(m, c0, c0 + CH), psum[bop][:, 0:CH],
                            xn_t(m, c0, c0 + CH), op=ALU.add, waits=[tmo], inc="v")
                banks.free[bop] = thid
            prev_h_user = p.tok("v")
            prev_h_user_t = p.tok("t")
        tok_hid = p.tok("v")
        dA_user = p.tok("v")

        if taps:
            tp_hd = tap("tp_hidmx", [D, T])
            p.dma("sync", tp_hd.rearrange("(a p) t -> p a t", p=128),
                  bass.AP(bufA.ap().tensor, 0, [[16384, 128], [2048, 8], [1, 2048]]), waits=[tok_hid], slot="tap")


    tok_mlp_out = None
    if phases >= 3:
        p.drain("vector")
        p.drain("scalar")
        gate2 = [p.tok("v"), p.tok("s"), p.tok("t"), tok_hid]
        MCH = 1024
        hn_t = lambda d, cc: bcall.ap()[:, d * 1024 + cc * 512: d * 1024 + cc * 512 + 512]
        ff_t = lambda m, cc: bufB.ap()[:, m * 1024 + cc * 512: m * 1024 + cc * 512 + 512]
        stgb = wsl.ap()[:, 4608:5120]
        x0stg = smallB.ap()[:, 1024:1536]
        stg1 = smallA.ap()[:, 1152:1664]
        hid_t = xn_t
        prev_ff_user = None
        stgw_free = None
        tok_out_w = []
        for mc in range(2):
            m0 = mc * MCH
            hn_toks = []
            for cc in range(2):
                cb_, cs_ = m0 + cc * 512, 512
                bsq2 = banks.get(1)[0]
                sqmm = [None, None]
                for d in range(8):
                    st = p.op("scalar", "activation", sqst[d % 2], hid_t(d, cb_, cb_ + 512), ACT.Square,
                              bias=0.0, scale=1.0, waits=gate2 + [sqmm[d % 2]], inc="s")
                    mt = p.op("tensor", "matmul", psum[bsq2][0:1, 0:512], R1024, sqst[d % 2],
                              start=(d == 0), stop=(d == 7),
                              waits=[st] + ([banks.free[bsq2]] if d == 0 else []), inc="t")
                    sqmm[d % 2] = mt
                trr = p.op("scalar", "activation", rrow, psum[bsq2][0:1, 0:512], ACT.Sqrt,
                           bias=EPS6[0:1, 0:1], scale=1.0, waits=[sqmm[1]], inc="s")
                banks.free[bsq2] = trr
                trc = p.op("vector", "reciprocal", rrow, rrow, waits=[trr, gate2[0]], inc="v")
                brb2 = banks.get(1)[0]
                trb = p.mm(psum[brb2][:, 0:512], [(ONES, rrow)], waits=[trc, banks.free[brb2]])
                lastt = None
                for d in range(8):
                    th = p.op("vector", "scalar_tensor_tensor", hn_t(d, cc) if mc == 0 else hn_t(d, cc),
                              hid_t(d, cb_, cb_ + 512), PREMOE[:, d:d + 1], psum[brb2][:, 0:512],
                              op0=ALU.mult, op1=ALU.mult, waits=[trb, prev_ff_user], inc="v")
                    lastt = th
                    hn_toks.append(th)
                banks.free[brb2] = lastt
            prev_tff = None
            for m in range(32):
                twg = p.dma("gpsimd", bass.AP(wsl.ap().tensor, 0, [[6144, 128], [128, 8], [1, 128]]),
                            bass.AP(w_gate.tensor, m * 128, [[4096, 128], [4096 * 128, 8], [1, 128]]),
                            waits=[slab_free[0]], slot="ws0")
                twu = p.dma("gpsimd", bass.AP(wsl.ap().tensor, 1024, [[6144, 128], [128, 8], [1, 128]]),
                            bass.AP(w_up.tensor, m * 128, [[4096, 128], [4096 * 128, 8], [1, 128]]),
                            waits=[slab_free[1]], slot="ws1")
                lg = lu = None
                for cc in range(2):
                    bg = banks.get(1)[0]
                    lg = p.mm(psum[bg][:, 0:512], [(wslab_blk(0, k), hn_t(k, cc)) for k in range(8)],
                              waits=hn_toks + [twg, banks.free[bg]])
                    bu = banks.get(1)[0]
                    lu = p.mm(psum[bu][:, 0:512], [(wslab_blk(1, k), hn_t(k, cc)) for k in range(8)],
                              waits=[twu, banks.free[bu]])
                    tgs = p.op("scalar", "activation", stgb, psum[bg][:, 0:512], ACT.Silu,
                               bias=0.0, scale=1.0, waits=[lg, prev_ff_user, prev_tff], inc="s")
                    banks.free[bg] = tgs
                    tff = p.op("vector", "tensor_tensor", ff_t(m, cc), stgb, psum[bu][:, 0:512],
                               op=ALU.mult, waits=[tgs, lu, prev_ff_user], inc="v")
                    banks.free[bu] = tff
                    prev_tff = tff
                slab_free[0] = lg
                slab_free[1] = lu
            tok_ff = p.tok("v")
            for md in range(8):
                bks = banks.get(2)
                tdm_last = [None, None]
                for kb in range(4):
                    sl = kb % 2
                    tws = p.dma("gpsimd", bass.AP(wsl.ap().tensor, sl * 1024, [[6144, 128], [128, 8], [1, 128]]),
                                bass.AP(w_down.tensor, kb * 8 * 128 * 1024 + md * 128,
                                        [[1024, 128], [1024 * 128, 8], [1, 128]]),
                                waits=[slab_free[sl]], slot="ws%d" % sl)
                    for cc in range(2):
                        pairs = [(wslab_blk(sl, k), ff_t(kb * 8 + k, cc)) for k in range(8)]
                        tdm_last[cc] = p.mm(psum[bks[cc]][:, 0:512], pairs,
                                            waits=[tws, tok_ff] + ([banks.free[bks[cc]]] if kb == 0 else []),
                                            start=(kb == 0), stop=(kb == 3))
                    slab_free[sl] = tdm_last[1]
                for cc in range(2):
                    cb_ = m0 + cc * 512
                    tx0 = p.dma("sync", x0stg, x0t[md * 128:(md + 1) * 128, cb_:cb_ + 512],
                                waits=([prev_ff_user] if prev_ff_user is not None else gate2), slot="x0r")
                    t1 = p.op("vector", "tensor_tensor", stg1, psum[bks[cc]][:, 0:512],
                              hid_t(md, cb_, cb_ + 512), op=ALU.add,
                              waits=[tdm_last[cc], stgw_free], inc="v")
                    banks.free[bks[cc]] = t1
                    p.drain("vector")
                    t2 = p.op("vector", "tensor_tensor", stg1, stg1, x0stg, op=ALU.add, waits=[tx0], inc="v")
                    p.drain("vector")
                    tws_ = p.dma("sync", out[md * 128:(md + 1) * 128, cb_:cb_ + 512], stg1, waits=[t2], slot="outw")
                    tok_out_w.append(tws_)
                    stgw_free = tws_
                    prev_ff_user = t2
        tok_mlp_out = p.tok("q.outw")

    if phases >= 4:
        p.drain("vector")
        p.drain("scalar")
        gate3 = [p.tok("v"), p.tok("s"), p.tok("t")]
        if "q.tap" in p.sems:
            gate3.append(p.tok("q.tap"))
        am2 = smallA[0:1, 1675:1676]
        s2row = smallA[0:1, 1676:1677]
        inv2row = smallA[0:1, 1677:1678]
        tok_am2_back = p.dma("gpsimd", am2, cc2o, waits=[tok_cc2], slot="cc2r")
        tt4 = p.op("vector", "reciprocal", s2row, am2, waits=[tok_am2_back] + gate3, inc="v")
        p.drain("vector")
        tt5 = p.op("vector", "tensor_scalar", s2row, s2row, 32767.0, None, op0=ALU.mult, inc="v")
        tt6 = p.op("vector", "tensor_scalar", inv2row, am2, 1.0 / 32767.0, None, op0=ALU.mult,
                   waits=[tok_am2_back], inc="v")
        bq2 = banks.get(1)[0]
        tb1 = p.mm(psum[bq2][:, 0:1], [(ONES, s2row)], waits=[tt5, banks.free[bq2]])
        tb2 = p.mm(psum[bq2][:, 1:2], [(ONES, inv2row)], waits=[tt6])
        tc1 = p.op("vector", "tensor_copy", S2COL, psum[bq2][:, 0:1], waits=[tb1], inc="v")
        tc2 = p.op("vector", "tensor_copy", AMAX2, psum[bq2][:, 1:2], waits=[tb2], inc="v")
        banks.free[bq2] = tc2
        INV2COL = AMAX2
        ROWSC = colsA.ap()[:, 443:475]
        kout_ap = lambda k, q0, q1: bufA_bf[:, k * 1024 + q0: k * 1024 + q1]
        wfa2 = [hstrip_f32[:, 0:1024], hstrip_f32[:, 1024:2048]]
        rmaxc = smallA.ap()[:, 1670:1671]
        prev_kq = [gate3[2], gate3[2]]
        for k in range(32):
            sl = k % 2
            tld = p.dma("sync", wfa2[sl], w_ffn_out[k * 128:(k + 1) * 128, :], waits=[prev_kq[sl]], slot="wo%d" % sl)
            trm = p.op("vector", "tensor_reduce", rmaxc, wfa2[sl], axis=AX, op=ALU.max,
                       apply_absolute_value=True, waits=[tld] + gate3, inc="v")
            p.drain("vector")
            trm2 = p.op("vector", "tensor_scalar", ROWSC[:, k:k + 1], rmaxc, 1e-8, 1.0 / 127.0,
                        op0=ALU.max, op1=ALU.mult, inc="v")
            p.drain("vector")
            trm3 = p.op("vector", "tensor_scalar", rmaxc, rmaxc, 1e-8, None, op0=ALU.max, inc="v")
            p.drain("vector")
            trm4 = p.op("vector", "reciprocal", rmaxc, rmaxc, inc="v")
            p.drain("vector")
            for q in range(2):
                tq1 = p.op("vector", "tensor_scalar", qpre[q], wfa2[sl][:, q * 512:(q + 1) * 512],
                           rmaxc, None, op0=ALU.mult, waits=[p.tok("s")] + gate3, inc="v")
                p.drain("vector")
                tq2 = p.op("vector", "tensor_scalar", qpre[q], qpre[q], 127.0, MAGIC, op0=ALU.mult, op1=ALU.add, inc="v")
                p.drain("vector")
                tq3 = p.op("vector", "tensor_scalar", qpre[q], qpre[q], -MAGIC, -128.0, op0=ALU.add, op1=ALU.max, inc="v")
                p.drain("vector")
                tq4 = p.op("vector", "tensor_scalar", kout_ap(k, q * 512, (q + 1) * 512), qpre[q],
                           127.0, None, op0=ALU.min, inc="v")
                prev_kq[sl] = tq4
        p.drain("vector")
        tcmb = p.op("vector", "tensor_scalar", ROWSC, ROWSC, INV2COL, None, op0=ALU.mult, waits=[tc2], inc="v")
        tok_kout = p.tok("v")
        hs_t = lambda k: bufB.ap()[:, k * 512:(k + 1) * 512]
        hstg = [smallB.ap()[:, 0:512], smallB.ap()[:, 512:1024]]
        prev_hs_mm = None
        prev_ffn2 = None
        prev_hs_q3 = [tok_kout, tok_kout]
        for c in range(NCH):
            c0 = c * CH
            hs_toks = []
            for k in range(32):
                sl = k % 2
                tld = p.dma("sync", hstg[sl], sp_hid[k * 128:(k + 1) * 128, c0:c0 + CH],
                            waits=[prev_hs_q3[sl]], slot="hr%d" % sl)
                tq1 = p.op("vector", "tensor_scalar", hstg[sl], hstg[sl], S2COL, MAGIC,
                           op0=ALU.mult, op1=ALU.add, waits=[tld, tc1, prev_hs_mm], inc="v")
                p.drain("vector")
                tq2 = p.op("vector", "tensor_scalar", hstg[sl], hstg[sl], -MAGIC, -32768.0,
                           op0=ALU.add, op1=ALU.max, inc="v")
                p.drain("vector")
                tq3 = p.op("vector", "tensor_scalar", hs_t(k), hstg[sl], 32767.0, ROWSC[:, k:k + 1],
                           op0=ALU.min, op1=ALU.mult, waits=[tcmb], inc="v")
                prev_hs_q3[sl] = tq3
                hs_toks.append(tq3)
            for md in range(8):
                bmd = banks.get(1)[0]
                pairs = [(kout_ap(k, md * 128, (md + 1) * 128), hs_t(k)) for k in range(32)]
                tdm = p.mm(psum[bmd][:, 0:CH], pairs, waits=hs_toks + [tok_kout, banks.free[bmd]])
                prev_hs_mm = tdm
                tev = p.op("vector", "tensor_scalar", stg1, psum[bmd][:, 0:CH], BOUT[:, md:md + 1], None,
                           op0=ALU.add, waits=[tdm, tok_mlp_out, prev_ffn2], inc="v")
                banks.free[bmd] = tev
                p.drain("vector")
                tac = p.dma_acc(out[md * 128:(md + 1) * 128, c0:c0 + CH], stg1,
                                waits=[tev, tok_mlp_out], slot="acc")
                prev_ffn2 = tac
        tok_ffn2 = p.tok("g.acc")

    if phases >= 5:
        p.drain("vector")
        p.drain("scalar")
        gate4 = [p.tok("v"), p.tok("s"), p.tok("t")]
        x1sb = bufB_f32
        x1_t = lambda d, a, b: x1sb[:, d * T + a: d * T + b]
        tok_x1 = p.dma("sync",
                       bass.AP(x1sb.tensor, x1sb.offset, [[16384, 128], [2048, 8], [1, 2048]]),
                       out.rearrange("(a p) t -> p a t", p=128), waits=[tok_ffn2] + gate4, slot="x1r")
        tok_xn1 = emit_layernorm(x1_t, xn_t, 32, tok_x1, do_red=False)
        if taps:
            tp_x1 = tap("tp_x1", [D, T])
            p.dma("sync", tp_x1.rearrange("(a p) t -> p a t", p=128),
                  bass.AP(bufB_f32.tensor, bufB_f32.offset, [[16384, 128], [2048, 8], [1, 2048]]),
                  waits=[tok_x1], slot="tap")
            tp_xn1 = tap("tp_xn1", [D, T])
            p.dma("sync", tp_xn1.rearrange("(a p) t -> p a t", p=128),
                  bass.AP(bufA.ap().tensor, 0, [[16384, 128], [2048, 8], [1, 2048]]),
                  waits=[tok_xn1], slot="tap")


    if phases >= 6:
        p.drain("vector")
        p.drain("scalar")
        gate5 = [p.tok("v"), p.tok("s"), p.tok("t")]
        if "q.tap" in p.sems:
            gate5.append(p.tok("q.tap"))
        FC = 256
        xn1b_t = lambda k: bcall.ap()[:, k * 2048:(k + 1) * 2048]
        z_t = lambda d: bufB_f32[:, d * 256:(d + 1) * 256]
        y_t = lambda hg: bufB_f32[:, 2048 + hg * 256: 2048 + (hg + 1) * 256]
        xbcc_t = lambda ct: bufB.ap()[:, 8192 + ct * 256: 8192 + (ct + 1) * 256]
        xbct_t = lambda tt: bufB.ap()[:, 10496 + tt * 1152: 10496 + (tt + 1) * 1152]
        xdt_t = lambda tt: bufB.ap()[:, 12800 + tt * 1024: 12800 + (tt + 1) * 1024]
        xdec_t = lambda tt: bufB.ap()[:, 14848 + tt * 1024: 14848 + (tt + 1) * 1024]
        catn_t = lambda i: bufB.ap()[:, 16896 + i * 256: 16896 + (i + 1) * 256]
        states_ab = [bufB_f32[0:16, 10496:11520], bufB_f32[0:16, 11520:12544]]
        SIa = bufB_f32[0:16, 12544:13568]
        SI2a = bufB_f32[0:16, 13568:14592]
        cm0b = bufB[0:16, 29184:29440]
        bm0b = bufB[0:16, 29440:29696]
        cm0f = bufB_f32[0:16, 14848:15104]
        stage1 = wsl.ap()[:, 3072:3331]
        halo1 = wsl.ap()[:, 3331:3358]
        GmS = wsl.ap()[:, 3358:3870]
        WS = wsl.ap()[:, 3870:4382]
        LamS = wsl.ap()[:, 4382:4894]
        ydsb = wsl[0:16, 4894:5150]
        odsb = wsl.ap()[:, 5150:5406]
        CSELHP = hstrip.ap()[:, 0:1024]
        CINSEL = hstrip.ap()[:, 1024:2048]
        odxb = hstrip[0:64, 2048:2304]
        m1b = smallB[0:64, 0:256]
        adtb = smallB[0:64, 256:512]
        acsb = smallB[0:64, 512:768]
        sqf = [smallB.ap()[:, 1024:1280], smallB.ap()[:, 1280:1536]]
        acst = [smallA.ap()[:, 0:64], smallA.ap()[:, 64:128]]
        nacst = [smallA.ap()[:, 128:192], smallA.ap()[:, 192:256]]
        m1t = [smallA.ap()[:, 256:320], smallA.ap()[:, 320:384]]
        e2sb = smallA[0:16, 384:448]
        dect = [smallA.ap()[:, 448:512], smallA.ap()[:, 512:576]]
        mu2 = smallA[0:1, 576:832]
        v2 = smallA[0:1, 832:1088]
        ID64 = bass.AP(consts.ap().tensor, 128, [[384, 64], [1, 64]])
        onesrep = bass.AP(colsA.ap().tensor, 358, [[512, 64], [0, 256]])

        MASKNEG = hstrip_f32[:, 1152:1664]
        tok_mn = p.dma("sync", MASKNEG, k_maskneg, waits=gate5, slot="fmn")
        tok_shp = p.dma("gpsimd", CSELHP[0:64, :], k_selhp, waits=gate5, slot="fsel")
        tok_ins = p.dma("gpsimd", CINSEL[0:16, :], k_insel, waits=gate5, slot="fsel2")
        xn1b_toks = []
        for k in range(8):
            tkx = p.op("vector", "tensor_copy", xn1b_t(k), xn_t(k, 0, 2048), waits=gate5 + [tok_xn1], inc="v")
            xn1b_toks.append(tkx)
        tsi0 = p.op("vector", "memset", SIa, 0.0, waits=gate5, inc="v")
        th1 = p.op("vector", "memset", halo1, 0.0, waits=gate5, inc="v")
        p.drain("vector")

        slabf = [None, None]
        cdf = [None, None]
        stage1_free = None
        halo1_tok = [th1] * 9
        prev_acc = None
        prev_yd_acc = None
        tok_si = tsi0
        tok_e2 = None
        tok_states_prev = None
        prev_loop_v = None
        prev_loop_t = None
        prev_loop_s = None
        for c in range(8):
            tc0 = c * FC
            states_cur = states_ab[c % 2]
            states_prev = states_ab[(c + 1) % 2]
            if c > 0:
                gexp = bass.AP(smallA.ap().tensor, 384, [[1792, 16], [1, 64], [0, 16]])
                tu = p.op("vector", "tensor_tensor", SI2a, SIa, gexp, op=ALU.mult,
                          waits=[tok_e2, tok_si, tok_states_prev], inc="v")
                p.drain("vector")
                tok_si = p.op("vector", "tensor_tensor", SIa, SI2a, states_prev, op=ALU.add, inc="v")
                p.drain("vector")
            tsg_raw = None
            for m in range(17):
                si = m % 2
                srcap = bass.AP(w1_inT.tensor, m * 128, [[E1P, 128], [E1P * 128, 8], [1, 128]])
                dstap = bass.AP(wsl.ap().tensor, si * 1024, [[6144, 128], [128, 8], [1, 128]])
                twl = p.dma("gpsimd", dstap, srcap, waits=[slabf[si]], slot="fws%d" % si)
                bmm = banks.get(1)[0]
                pairs = [(wslab_blk(si, k), bcall.ap()[:, k * 2048 + tc0: k * 2048 + tc0 + FC]) for k in range(8)]
                tmm = p.mm(psum[bmm][:, 0:FC], pairs, waits=xn1b_toks + [twl, banks.free[bmm]])
                slabf[si] = tmm
                if m < 8:
                    tzv = p.op("vector", "tensor_copy", z_t(m), psum[bmm][:, 0:FC],
                               waits=[tmm, prev_loop_t, prev_loop_v], inc="v")
                    banks.free[bmm] = tzv
                    continue
                ct = m - 8
                if m == 16:
                    tsg_raw = p.op("scalar", "activation", m1b, psum[bmm][0:64, 0:FC], ACT.Sigmoid,
                                   bias=colsA[0:64, 355:356], scale=-1.0,
                                   waits=[tmm, tok_setup_s, tok_setup_v, prev_loop_v, prev_loop_s], inc="s")
                    p.drain("scalar")
                thalo = p.op("vector", "tensor_copy", stage1[:, 0:3], halo1[:, ct * 3:ct * 3 + 3],
                             waits=[halo1_tok[ct], stage1_free], inc="v")
                p.drain("vector")
                tev = p.op("vector", "tensor_copy", stage1[:, 3:259], psum[bmm][:, 0:FC],
                           waits=[tmm, (tsg_raw if m == 16 else None)], inc="v")
                banks.free[bmm] = tev
                p.drain("vector")
                thl = p.op("vector", "tensor_copy", halo1[:, ct * 3:ct * 3 + 3], stage1[:, 256:259], inc="v")
                halo1_tok[ct] = thl
                csrc = bass.AP(cdg1.tensor, ct * 4 * 128 * 128, [[128, 128], [128 * 128, 4], [1, 128]])
                cdst = bass.AP(wsl.ap().tensor, 2048 + si * 512, [[6144, 128], [128, 4], [1, 128]])
                tcd = p.dma("gpsimd", cdst, csrc, waits=[cdf[si]], slot="fcd%d" % si)
                bcv = banks.get(1)[0]
                cpairs = [(bass.AP(wsl.ap().tensor, 2048 + si * 512 + kk * 128, [[6144, 128], [1, 128]]),
                           stage1[:, kk:kk + 256]) for kk in range(4)]
                tcv = p.mm(psum[bcv][:, 0:FC], cpairs, waits=[tcd, tev, thl, thalo, banks.free[bcv]])
                cdf[si] = tcv
                stage1_free = tcv
                txb = p.op("scalar", "activation", xbcc_t(ct), psum[bcv][:, 0:FC], ACT.Silu,
                           bias=CONVB1[:, ct:ct + 1], scale=1.0,
                           waits=[tcv, prev_loop_v, prev_loop_t], inc="s")
                banks.free[bcv] = txb
            tok_xbc = p.tok("s")
            tm1 = p.op("scalar", "activation", m1b, m1b, ACT.Ln, bias=EPSG[0:64, 0:1], scale=1.0, inc="s")
            p.drain("scalar")
            tadt = p.op("vector", "tensor_scalar", adtb, m1b, colsA[0:64, 356:357], None, op0=ALU.mult,
                        waits=[tm1, prev_loop_v], inc="v")
            p.drain("vector")
            tacs = p.op("vector", "tensor_tensor_scan", acsb, onesrep, adtb, 0.0,
                        op0=ALU.mult, op1=ALU.add, waits=[tok_consts], inc="v")
            p.drain("vector")
            tacst = []
            for tt in range(2):
                btt = banks.get(1)[0]
                t_tr = p.op("tensor", "transpose", psum[btt][:, 0:64], smallB[0:64, 512 + tt * 128: 512 + (tt + 1) * 128],
                            ID64, waits=[tacs, banks.free[btt]], inc="t")
                t_e1 = p.op("vector", "tensor_copy", acst[tt], psum[btt][:, 0:64], waits=[t_tr, prev_loop_v], inc="v")
                t_e2 = p.op("vector", "tensor_scalar", nacst[tt], psum[btt][:, 0:64], -1.0, None, op0=ALU.mult,
                            waits=[t_tr], inc="v")
                banks.free[btt] = t_e2
                tacst.append((t_e1, t_e2))
                bt2 = banks.get(1)[0]
                t_tr2 = p.op("tensor", "transpose", psum[bt2][:, 0:64], smallB[0:64, tt * 128:(tt + 1) * 128],
                             ID64, waits=[tm1, tadt, banks.free[bt2]], inc="t")
                t_e3 = p.op("vector", "tensor_copy", m1t[tt], psum[bt2][:, 0:64], waits=[t_tr2], inc="v")
                banks.free[bt2] = t_e3
                tacst.append((t_e3, t_e3))
            tod = []
            for tt in range(2):
                bto = banks.get(1)[0]
                t_tr3 = p.op("tensor", "transpose", psum[bto][0:64, 0:128], acst[tt], IDENT,
                             waits=[tacst[2 * tt][0], banks.free[bto]], inc="t")
                t_oe = p.op("scalar", "activation", odxb[:, tt * 128:(tt + 1) * 128], psum[bto][0:64, 0:128],
                            ACT.Exp, bias=0.0, scale=1.0, waits=[t_tr3, prev_loop_s, prev_loop_t], inc="s")
                banks.free[bto] = t_oe
                tod.append(t_oe)
            sel127 = bass.AP(consts.ap().tensor, 128 + 127, [[384, 128], [0, 128]])
            tcsl = p.op("tensor", "matmul", psum[7][:, 0:64], sel127, acst[1], start=True, stop=True,
                        waits=[tacst[2][0] if False else tacst[2 * 1][0], tok_consts, prev_loop_v], inc="t")
            tdec = []
            for tt in range(2):
                t_d1 = p.op("vector", "tensor_tensor", dect[tt], psum[7][:, 0:64], acst[tt], op=ALU.subtract,
                            waits=[tcsl, tacst[2 * tt][0], prev_loop_s], inc="v")
                p.drain("vector")
                t_d2 = p.op("scalar", "activation", dect[tt], dect[tt], ACT.Exp, bias=0.0, scale=1.0,
                            waits=[t_d1], inc="s")
                p.drain("scalar")
                tdec.append(t_d2)
            tok_e2 = p.op("scalar", "activation", e2sb, psum[7][0:16, 0:64], ACT.Exp, bias=0.0, scale=1.0,
                          waits=[tcsl, tok_si], inc="s")
            tbm0 = p.dma("gpsimd", bm0b, bufB[64:80, 8192 + 8 * 256: 8192 + 9 * 256], waits=[tok_xbc, prev_loop_t], slot="fbm")
            tcm0 = p.dma("gpsimd", cm0b, bufB[80:96, 8192 + 8 * 256: 8192 + 9 * 256], waits=[tok_xbc, prev_loop_t], slot="fcm")
            tcm0f = p.dma("gpsimd", cm0f, bufB[80:96, 8192 + 8 * 256: 8192 + 9 * 256], waits=[tok_xbc], slot="fcm2")
            txbt_toks = []
            for ct in range(9):
                for tt in range(2):
                    btx = banks.get(1)[0]
                    pbf = psum[btx].ap().bitcast(BF16)
                    t_tr = p.op("tensor", "transpose", pbf[:, 0:128],
                                bufB[0:128, 8192 + ct * 256 + tt * 128: 8192 + ct * 256 + (tt + 1) * 128],
                                IDENTB, waits=[tok_xbc, banks.free[btx], tok_constb], inc="t")
                    t_ev = p.op("vector", "tensor_copy", bufB.ap()[:, 10496 + tt * 1152 + ct * 128: 10496 + tt * 1152 + (ct + 1) * 128],
                                pbf[:, 0:128], waits=[t_tr, prev_loop_t], inc="v")
                    banks.free[btx] = t_ev
                    txbt_toks.append(t_ev)
            txd = []
            for tt in range(2):
                m1exp = bass.AP(smallA.ap().tensor, 256 + tt * 64, [[1792, 128], [1, 64], [0, 16]])
                t_x1 = p.op("vector", "scalar_tensor_tensor", xdt_t(tt),
                            bufB.ap()[:, 10496 + tt * 1152: 10496 + tt * 1152 + 1024], -1.0, m1exp,
                            op0=ALU.mult, op1=ALU.mult, waits=txbt_toks + [tacst[2 * tt + 1][0], prev_loop_t], inc="v")
                p.drain("vector")
                dexp = bass.AP(smallA.ap().tensor, 448 + tt * 64, [[1792, 128], [1, 64], [0, 16]])
                t_x2 = p.op("vector", "tensor_tensor", xdec_t(tt), xdt_t(tt), dexp, op=ALU.mult,
                            waits=[tdec[tt]], inc="v")
                p.drain("vector")
                txd.append((t_x1, t_x2))
            st_toks = []
            for hg in range(8):
                bst = banks.get(1)[0]
                pairs = [(bufB.ap()[:, 10496 + tt * 1152 + 1088: 10496 + tt * 1152 + 1104],
                          bufB.ap()[:, 14848 + tt * 1024 + hg * 128: 14848 + tt * 1024 + (hg + 1) * 128])
                         for tt in range(2)]
                tsm = p.mm(psum[bst][0:16, 0:128], pairs, waits=[txd[0][1], txd[1][1]] + txbt_toks + [banks.free[bst]])
                sc0 = (10496 if c % 2 == 0 else 11520) + hg * 128
                tse = p.op("vector", "tensor_copy", bufB_f32[0:16, sc0:sc0 + 128],
                           psum[bst][0:16, 0:128], waits=[tsm, tok_si], inc="v")
                banks.free[bst] = tse
                st_toks.append(tse)
            tok_states_prev = st_toks[-1]
            tG = None
            for sh in range(2):
                tG = p.op("tensor", "matmul", psum[7][:, sh * 256:(sh + 1) * 256],
                          bm0b[0:16, sh * 128:(sh + 1) * 128],
                          cm0b, start=True, stop=True,
                          waits=[tbm0, tcm0, (tcsl if sh == 0 else None), p.tok("s") if sh == 0 else None], inc="t")
            tGm = p.op("vector", "tensor_tensor", GmS, psum[7][:, 0:512], MASKTRI, op=ALU.mult,
                       waits=[tG, tok_constb, prev_loop_s], inc="v")
            y_toks = []
            prev_w = None
            prev_lam = None
            prev_od = None
            prev_yd = None
            for hg in range(8):
                byo = banks.get(1)[0]
                si_st = bass.AP(bufB_f32.tensor, bufB_f32.offset + 12544 + hg * 128, [[16384, 16], [1, 128]])
                tyo = p.op("tensor", "matmul", psum[byo][:, 0:FC], si_st, cm0f, start=True, stop=True,
                           waits=[tok_si, tcm0f, banks.free[byo]], inc="t")
                bod = banks.get(1)[0]
                tob = p.op("tensor", "matmul", psum[bod][:, 0:FC], CSELHP[0:64, hg * 128:(hg + 1) * 128],
                           odxb, start=True, stop=True, waits=[tod[0], tod[1], tok_shp, banks.free[bod]], inc="t")
                tos = p.op("scalar", "activation", odsb, psum[bod][:, 0:FC], ACT.Copy, bias=0.0, scale=1.0,
                           waits=[tob, prev_od], inc="s")
                banks.free[bod] = tos
                prev_od = None
                tyv = p.op("vector", "tensor_tensor", y_t(hg), psum[byo][:, 0:FC], odsb, op=ALU.mult,
                           waits=[tyo, tos, prev_loop_t], inc="v")
                banks.free[byo] = tyv
                prev_od = tyv
                tacc = None
                for h8 in range(8):
                    h = hg * 8 + h8
                    blam = banks.get(1)[0]
                    selh = bass.AP(consts.ap().tensor, 128 + h, [[384, 64], [0, 128]])
                    acsap = bass.AP(smallB.ap().tensor, 512, [[1664, 64], [1, 256]])
                    p.op("tensor", "matmul", psum[blam][:, 0:256], selh, acsap, start=True, stop=False,
                         waits=[tacs, tok_consts, tok_mn, banks.free[blam]])
                    p.op("tensor", "matmul", psum[blam][:, 256:512], selh, acsap, start=False, stop=False)
                    tlb = p.op("tensor", "matmul", psum[blam][:, 0:512], IDENT, MASKNEG, start=False, stop=True,
                               inc="t")
                    tl1 = p.op("scalar", "activation", LamS[:, 0:256], psum[blam][:, 0:256], ACT.Exp,
                               bias=nacst[0][:, h:h + 1], scale=1.0,
                               waits=[tlb, tacst[1][1], prev_w, prev_lam], inc="s")
                    tl2 = p.op("scalar", "activation", LamS[:, 256:512], psum[blam][:, 256:512], ACT.Exp,
                               bias=nacst[1][:, h:h + 1], scale=1.0, waits=[tacst[3][1] if False else None], inc="s")
                    banks.free[blam] = tl2
                    prev_lam = tl2
                    tw = p.op("vector", "scalar_tensor_tensor", WS, LamS, 3.0e38, GmS,
                              op0=ALU.min, op1=ALU.mult, waits=[tl1, tl2, tGm, prev_w], inc="v")
                    byd = banks.get(1)[0]
                    ydpairs = [(bufB.ap()[:, 12800 + tt * 1024 + h * 16: 12800 + tt * 1024 + (h + 1) * 16],
                                WS[:, tt * 256:(tt + 1) * 256]) for tt in range(2)]
                    tyd = p.mm(psum[byd][0:16, 0:FC], ydpairs, waits=[tw, banks.free[byd]])
                    prev_w = tyd
                    tye = p.op("vector", "tensor_copy", ydsb, psum[byd][0:16, 0:FC],
                               waits=[tyd, prev_yd], inc="v")
                    banks.free[byd] = tye
                    tacc = p.op("tensor", "matmul", psum[6][:, 0:FC], CINSEL[0:16, h8 * 128:(h8 + 1) * 128],
                                ydsb, start=(h8 == 0), stop=(h8 == 7),
                                waits=[tye, tok_ins] + ([prev_yd_acc] if h8 == 0 else []), inc="t")
                    prev_yd = tacc
                ty2 = p.op("vector", "tensor_tensor", y_t(hg), y_t(hg), psum[6][:, 0:FC], op=ALU.add,
                           waits=[tacc, tyv], inc="v")
                prev_yd_acc = ty2
                p.drain("vector")
                ty3 = p.op("vector", "scalar_tensor_tensor", y_t(hg), xbcc_t(hg), DSK1[:, hg:hg + 1],
                           y_t(hg), op0=ALU.mult, op1=ALU.add, waits=[tok_xbc, tok_setup_v], inc="v")
                p.drain("vector")
                y_toks.append(ty3)
            tiles = [("y", i) for i in range(8)] + [("z", i) for i in range(8)]
            bmu2, bsq3 = banks.get(2)
            mu_mm = None
            sq_mm2 = [None, None]
            for i, (kind, d) in enumerate(tiles):
                vt = y_t(d) if kind == "y" else z_t(d)
                wv = y_toks + [prev_loop_v]
                mu_mm = p.op("tensor", "matmul", psum[bmu2][0:1, 0:FC], R2048, vt,
                             start=(i == 0), stop=(i == 15),
                             waits=wv + ([banks.free[bmu2]] if i == 0 else []), inc="t")
                tqs = p.op("scalar", "activation", sqf[i % 2], vt, ACT.Square, bias=0.0, scale=1.0,
                           waits=wv + [sq_mm2[i % 2]], inc="s")
                sq_mm2[i % 2] = p.op("tensor", "matmul", psum[bsq3][0:1, 0:FC], R2048, sqf[i % 2],
                                     start=(i == 0), stop=(i == 15),
                                     waits=[tqs] + ([banks.free[bsq3]] if i == 0 else []), inc="t")
            tmu3 = p.op("vector", "tensor_copy", mu2, psum[bmu2][0:1, 0:FC], waits=[mu_mm], inc="v")
            banks.free[bmu2] = tmu3
            p.drain("vector")
            tm23 = p.op("vector", "tensor_tensor", v2, mu2, mu2, op=ALU.mult, inc="v")
            p.drain("vector")
            tvar3 = p.op("vector", "tensor_tensor", v2, psum[bsq3][0:1, 0:FC], v2, op=ALU.subtract,
                         waits=[sq_mm2[1]], inc="v")
            banks.free[bsq3] = tvar3
            ts13 = p.op("scalar", "activation", v2, v2, ACT.Sqrt, bias=EPS5[0:1, 0:1], scale=1.0,
                        waits=[tvar3], inc="s")
            tr13 = p.op("vector", "reciprocal", v2, v2, waits=[ts13], inc="v")
            p.drain("vector")
            brb3, bab3 = banks.get(2)
            trb3 = p.mm(psum[brb3][:, 0:FC], [(ONES, v2)], waits=[tr13, banks.free[brb3]])
            tab3 = p.mm(psum[bab3][:, 0:FC], [(ONES, mu2)], waits=[tmu3, tr13, banks.free[bab3]])
            catn_toks = []
            prev_cn = None
            for i, (kind, d) in enumerate(tiles):
                vt = y_t(d) if kind == "y" else z_t(d)
                t1c = p.op("vector", "tensor_tensor", sqf[i % 2], vt, psum[bab3][:, 0:FC], op=ALU.subtract,
                           waits=[trb3, tab3, p.tok("t"), catn_toks[i - 2] if i >= 2 else None], inc="v")
                p.drain("vector")
                t2c = p.op("vector", "tensor_tensor", sqf[i % 2], sqf[i % 2], psum[brb3][:, 0:FC], op=ALU.mult,
                           waits=[tab3], inc="v")
                p.drain("vector")
                p.drain("vector")
                t3c = p.op("vector", "tensor_scalar", catn_t(i), sqf[i % 2], LNW1[:, i:i + 1], LNB1[:, i:i + 1],
                           op0=ALU.mult, op1=ALU.add, waits=[prev_loop_t, prev_cn], inc="v")
                catn_toks.append(t3c)
                prev_cn = t3c
            banks.free[brb3] = catn_toks[-1]
            banks.free[bab3] = catn_toks[-1]
            for m in range(8):
                tw1 = p.dma("gpsimd", bass.AP(wsl.ap().tensor, 0, [[6144, 128], [128, 8], [1, 128]]),
                            bass.AP(w1_outT.tensor, m * 128, [[1024, 128], [1024 * 128, 8], [1, 128]]),
                            waits=[slabf[0]], slot="fws0")
                tw2 = p.dma("gpsimd", bass.AP(wsl.ap().tensor, 1024, [[6144, 128], [128, 8], [1, 128]]),
                            bass.AP(w1_outT.tensor, 8 * 128 * 1024 + m * 128, [[1024, 128], [1024 * 128, 8], [1, 128]]),
                            waits=[slabf[1]], slot="fws1")
                bwo = banks.get(1)[0]
                pairs = [(wslab_blk(0, k), catn_t(k)) for k in range(8)] + \
                        [(wslab_blk(1, k), catn_t(8 + k)) for k in range(8)]
                tmo = p.mm(psum[bwo][:, 0:FC], pairs, waits=catn_toks + [tw1, tw2, banks.free[bwo]])
                slabf[0] = tmo
                slabf[1] = tmo
                tev2 = p.op("vector", "tensor_copy", stg1[:, 0:FC], psum[bwo][:, 0:FC],
                            waits=[tmo, prev_acc], inc="v")
                banks.free[bwo] = tev2
                tac2 = p.dma_acc(out[m * 128:(m + 1) * 128, tc0:tc0 + FC], stg1[:, 0:FC],
                                 waits=[tev2, tok_ffn2], slot="acc")
                prev_acc = tac2
            prev_loop_v = p.tok("v")
            prev_loop_t = p.tok("t")
            prev_loop_s = p.tok("s")
        tok_attn1 = p.tok("g.acc")

    if phases >= 7:
        p.drain("vector")
        p.drain("scalar")
        gate6 = [p.tok("v"), p.tok("s"), p.tok("t")]
        xr_t = lambda d: bcall.ap()[:, d * 512:(d + 1) * 512]
        xk_t = lambda d: bcall.ap()[:, 4096 + d * 512: 4096 + (d + 1) * 512]
        ka_t = lambda d: bcall.ap()[:, 8192 + d * 512: 8192 + (d + 1) * 512]
        r_t = lambda d: bufB_f32[:, d * 512:(d + 1) * 512]
        gtmp = smallB.ap()[:, 1024:1536]
        prev_g_acc = None
        prev_gmm = None
        prev_ka = None
        for c in range(NCH):
            c0 = c * CH
            mix_toks = []
            for d in range(8):
                for which, MU, OMU, dstf in ((0, MUR, ONEMMUR, xr_t), (1, MUK, ONEMMUK, xk_t)):
                    t1 = p.op("vector", "tensor_scalar", gtmp, xn_t(d, c0, c0 + CH), MU[:, d:d + 1], None,
                              op0=ALU.mult, waits=gate6 + [tok_xn1, tok_setup_v, prev_gmm], inc="v")
                    p.drain("vector")
                    if c == 0:
                        t2 = p.op("vector", "scalar_tensor_tensor", dstf(d)[:, 1:CH],
                                  xn_t(d, 0, CH - 1), OMU[:, d:d + 1], gtmp[:, 1:CH],
                                  op0=ALU.mult, op1=ALU.add, inc="v")
                        p.drain("vector")
                        t3 = p.op("vector", "tensor_copy", dstf(d)[:, 0:1], gtmp[:, 0:1], inc="v")
                        mix_toks.append(t3)
                    else:
                        t2 = p.op("vector", "scalar_tensor_tensor", dstf(d),
                                  xn_t(d, c0 - 1, c0 + CH - 1), OMU[:, d:d + 1], gtmp,
                                  op0=ALU.mult, op1=ALU.add, inc="v")
                        mix_toks.append(t2)
                    p.drain("vector")
            rk_toks = []
            for which, wt, name in ((0, w1_rT, "r"), (1, w1_kT, "k")):
                for m in range(8):
                    si = m % 2
                    twl = p.dma("gpsimd", bass.AP(wsl.ap().tensor, si * 1024, [[6144, 128], [128, 8], [1, 128]]),
                                bass.AP(wt.tensor, m * 128, [[1024, 128], [1024 * 128, 8], [1, 128]]),
                                waits=[slabf[si]], slot="fws%d" % si)
                    bmg = banks.get(1)[0]
                    src_t = xr_t if which == 0 else xk_t
                    pairs = [(wslab_blk(si, k), src_t(k)) for k in range(8)]
                    tmm = p.mm(psum[bmg][:, 0:CH], pairs, waits=mix_toks + [twl, banks.free[bmg]])
                    slabf[si] = tmm
                    prev_gmm = tmm
                    if which == 0:
                        te = p.op("scalar", "activation", r_t(m), psum[bmg][:, 0:CH], ACT.Sigmoid,
                                  bias=0.0, scale=1.0, waits=[tmm, p.tok("v")], inc="s")
                        banks.free[bmg] = te
                    else:
                        te1 = p.op("vector", "tensor_relu", gtmp, psum[bmg][:, 0:CH],
                                   waits=[tmm, p.tok("v"), prev_ka], inc="v")
                        banks.free[bmg] = te1
                        te = p.op("scalar", "activation", ka_t(m), gtmp, ACT.Square, bias=0.0, scale=1.0,
                                  waits=[te1, p.tok("s")], inc="s")
                        prev_ka = te
                        prev_gmm = te
                    rk_toks.append(te)
            for m in range(8):
                si = m % 2
                twl = p.dma("gpsimd", bass.AP(wsl.ap().tensor, si * 1024, [[6144, 128], [128, 8], [1, 128]]),
                            bass.AP(w1_vT.tensor, m * 128, [[1024, 128], [1024 * 128, 8], [1, 128]]),
                            waits=[slabf[si]], slot="fws%d" % si)
                bmv = banks.get(1)[0]
                pairs = [(wslab_blk(si, k), ka_t(k)) for k in range(8)]
                tmm = p.mm(psum[bmv][:, 0:CH], pairs, waits=rk_toks + [twl, banks.free[bmv]])
                slabf[si] = tmm
                prev_gmm = tmm
                tev = p.op("vector", "tensor_tensor", stg1[:, 0:CH], psum[bmv][:, 0:CH], r_t(m),
                           op=ALU.mult, waits=[tmm, prev_g_acc, prev_acc, p.tok("s")], inc="v")
                banks.free[bmv] = tev
                tac = p.dma_acc(out[m * 128:(m + 1) * 128, c0:c0 + CH], stg1[:, 0:CH],
                                waits=[tev], slot="acc")
                prev_g_acc = tac

    final_waits = [(name, p.sems[name][1]) for name in list(p.sems.keys())]
    p.emit("sync", lambda e: e.nop(), waits=final_waits)

    @blk.sync
    def _(e):
        for f in p.q["sync"]:
            f(e)

    @blk.tensor
    def _(e):
        for f in p.q["tensor"]:
            f(e)

    @blk.vector
    def _(e):
        for f in p.q["vector"]:
            f(e)

    @blk.scalar
    def _(e):
        for f in p.q["scalar"]:
            f(e)

    @blk.gpsimd
    def _(e):
        for f in p.q["gpsimd"]:
            f(e)

    es.close()
    return tap_list


def _host_inputs(inputs, cores):
    x = np.asarray(inputs["L0_attn_hidden_states"])
    g = {}
    g["w_inproj"] = np.ascontiguousarray(inputs["L0_attn_in_proj_w"])
    cw = np.asarray(inputs["L0_attn_conv_w"])
    cd0 = np.zeros((16, 4, 128, 128), dtype=np.float32)
    idx = np.arange(128)
    for dt in range(16):
        for k in range(4):
            cd0[dt, k, idx, idx] = cw[dt * 128 + idx, k]
    g["cdg0"] = cd0.reshape(64, 128, 128)
    g["w_xproj"] = np.ascontiguousarray(inputs["L0_attn_x_proj_w"])
    g["w_dtproj"] = np.ascontiguousarray(inputs["L0_attn_dt_proj_w"])
    g["w_outproj"] = np.ascontiguousarray(inputs["L0_attn_out_proj_w"])
    g["w_gate"] = np.ascontiguousarray(inputs["L0_attn_gate_w"])
    g["w_up"] = np.ascontiguousarray(inputs["L0_attn_up_w"])
    g["w_down"] = np.ascontiguousarray(inputs["L0_attn_down_w"])
    g["w_ffn_in"] = np.ascontiguousarray(inputs["L0_ffn_W_in"])
    g["w_ffn_out"] = np.ascontiguousarray(inputs["L0_ffn_W_out"])
    w1in = np.asarray(inputs["L1_attn_W_in"])
    w1inT = np.zeros((D, E1P), dtype=np.float32)
    w1inT[:, :2048] = w1in[:2048].T
    w1inT[:, 2048:2112] = w1in[2080:2144].T
    w1inT[:, 2112:2144] = w1in[2048:2080].T
    g["w1_inT"] = w1inT
    cw1 = np.asarray(inputs["L1_attn_conv_w"]).reshape(CONVD, 4)
    cd1 = np.zeros((9, 4, 128, 128), dtype=np.float32)
    for ct in range(8):
        for k in range(4):
            rows = np.arange(128)
            cd1[ct, k, rows, rows] = cw1[ct * 128 + rows, k]
    for k in range(4):
        rows = np.arange(64, 96)
        cd1[8, k, rows, rows] = cw1[1024 + rows - 64, k]
    g["cdg1"] = cd1.reshape(36, 128, 128)
    g["w1_outT"] = np.ascontiguousarray(np.asarray(inputs["L1_attn_W_out"]).T)
    g["w1_rT"] = np.ascontiguousarray(np.asarray(inputs["L1_ffn_W_r"]).T)
    g["w1_kT"] = np.ascontiguousarray(np.asarray(inputs["L1_ffn_W_k"]).T)
    g["w1_vT"] = np.ascontiguousarray(np.asarray(inputs["L1_ffn_W_v"]).T)
    g["c_dtlnw"] = np.asarray(inputs["L0_attn_dt_ln_w"]).reshape(64, 1)
    g["c_blnw"] = np.asarray(inputs["L0_attn_b_ln_w"]).reshape(16, 1)
    g["c_bclnw"] = np.concatenate([np.asarray(inputs["L0_attn_b_ln_w"]), np.asarray(inputs["L0_attn_c_ln_w"])]).reshape(32, 1)
    g["c_clnw"] = np.asarray(inputs["L0_attn_c_ln_w"]).reshape(16, 1)
    g["c_dtprojb"] = np.ascontiguousarray(np.asarray(inputs["L0_attn_dt_proj_b"]).reshape(16, 128).T)
    alog = np.asarray(inputs["L0_attn_A_log"])
    asc = np.zeros((128, 256), dtype=np.float32)
    for n in range(16):
        for dt in range(16):
            asc[:, n * 16 + dt] = alog[dt * 128: (dt + 1) * 128, n]
    g["c_asc"] = asc
    g["c_dsk"] = np.ascontiguousarray(np.asarray(inputs["L0_attn_D_skip"]).reshape(16, 128).T)
    g["c_convb0"] = np.ascontiguousarray(np.asarray(inputs["L0_attn_conv_b"]).reshape(16, 128).T)
    g["c_premoe"] = np.ascontiguousarray(np.asarray(inputs["L0_attn_pre_moe_ln_w"]).reshape(8, 128).T)
    g["c_bin"] = np.ascontiguousarray(np.asarray(inputs["L0_ffn_b_in"]).reshape(32, 128).T)
    g["c_bout"] = np.ascontiguousarray(np.asarray(inputs["L0_ffn_b_out"]).reshape(8, 128).T)
    g["c_dtbias1"] = np.asarray(inputs["L1_attn_dt_bias"]).reshape(64, 1)
    g["c_alog1"] = np.asarray(inputs["L1_attn_A_log"]).reshape(64, 1)
    g["c_dsk1"] = np.ascontiguousarray(np.repeat(np.asarray(inputs["L1_attn_Dskip"]), 16).reshape(8, 128).T)
    g["c_lnw1"] = np.ascontiguousarray(np.asarray(inputs["L1_attn_ln_w"]).reshape(16, 128).T)
    g["c_lnb1"] = np.ascontiguousarray(np.asarray(inputs["L1_attn_ln_b"]).reshape(16, 128).T)
    g["c_mur"] = np.ascontiguousarray(np.asarray(inputs["L1_ffn_mu_r"]).reshape(8, 128).T)
    g["c_muk"] = np.ascontiguousarray(np.asarray(inputs["L1_ffn_mu_k"]).reshape(8, 128).T)
    cb1 = np.zeros((CONVDP,), dtype=np.float32)
    cb1[:1024] = np.asarray(inputs["L1_attn_conv_b"])[:1024]
    cb1[8 * 128 + 64: 8 * 128 + 96] = np.asarray(inputs["L1_attn_conv_b"])[1024:1056]
    g["c_convb1"] = np.ascontiguousarray(cb1.reshape(9, 128).T)
    g["k_ones"] = np.ones((1, 128), dtype=np.float32)
    g["k_mones"] = -np.ones((1, 128), dtype=np.float32)
    g["k_ident"] = np.eye(128, dtype=np.float32)
    c3 = np.zeros((96, 3), dtype=np.float32)
    c3[0:64, 0] = 1.0 / 64.0
    c3[64:80, 1] = 1.0 / 16.0
    c3[80:96, 2] = 1.0 / 16.0
    g["k_c3"] = c3
    b3 = np.zeros((3, 96), dtype=np.float32)
    b3[0, 0:64] = 1.0
    b3[1, 64:80] = 1.0
    b3[2, 80:96] = 1.0
    g["k_b3"] = b3
    sel8 = np.zeros((8, 128), dtype=np.float32)
    for h in range(8):
        sel8[h, h * 16:(h + 1) * 16] = 1.0
    g["k_sel8"] = sel8
    mt = np.zeros((128, 512), dtype=np.float32)
    s = np.arange(128)[:, None]
    l = np.arange(256)[None, :]
    mt[:, 0:256] = (s <= l).astype(np.float32)
    mt[:, 256:512] = ((s + 128) <= l).astype(np.float32)
    g["k_masktri"] = mt
    g["k_maskneg"] = (1.0 - mt) * (-1.0e9)
    g["k_r1024"] = np.full((128, 1), 1.0 / 1024.0, dtype=np.float32)
    g["k_r2048"] = np.full((128, 1), 1.0 / 2048.0, dtype=np.float32)
    g["k_onescol"] = np.ones((128, 1), dtype=np.float32)
    g["k_eps"] = np.tile(np.array([[1e-5, 1e-6, 1e-38]], dtype=np.float32), (128, 1))
    selhp = np.zeros((64, 1024), dtype=np.float32)
    for hg in range(8):
        for q in range(128):
            selhp[hg * 8 + q // 16, hg * 128 + q] = 1.0
    g["k_selhp"] = selhp
    insel = np.zeros((16, 1024), dtype=np.float32)
    for hp in range(8):
        for r in range(16):
            insel[r, hp * 128 + 16 * hp + r] = 1.0
    g["k_insel"] = insel
    for k in g:
        g[k] = np.ascontiguousarray(np.asarray(g[k], dtype=np.float32))

    in_maps = []
    for i in range(cores):
        m = dict(g)
        m["x0t"] = np.ascontiguousarray(np.asarray(x[i]).T.astype(np.float32))
        in_maps.append(m)
    return in_maps


def kernel(**inputs):
    in_maps = _host_inputs(inputs, N_CORES)
    nc = bass.Bass("TRN2", target_bir_lowering=False)
    build(nc, num_cores=N_CORES, phases=PHASES, taps=TAPS)
    res = run_bass_kernel_spmd(nc, in_maps, core_ids=list(range(N_CORES)))
    out = np.stack([np.asarray(r["out"]).T for r in res.results], axis=0)
    return out.astype(np.float32)


# version 07 (final)
# speedup vs baseline: 1.0126x; 1.0126x over previous
import numpy as np
from contextlib import ExitStack

import concourse.bass as bass
import concourse.mybir as mybir
from concourse.bass_utils import run_bass_kernel_spmd

F32 = mybir.dt.float32
BF16 = mybir.dt.bfloat16
ALU = mybir.AluOpType
ACT = mybir.ActivationFunctionType
AX = mybir.AxisListType.X

T = 2048
D = 1024
DIN = 2048
NST = 16
DTR = 64
DFF = 4096
CH = 512
NCH = T // CH
E1 = 2144
E1P = 2176
CONVD = 1056
CONVDP = 1152
NH = 64
HP = 16
SSDCH = 256

MAGIC = float(1.5 * 2.0 ** 23)

N_CORES = 8
ENGINES = ("sync", "tensor", "vector", "scalar", "gpsimd")

PHASES = 7
TAPS = False


class PB:

    def __init__(self, nc, es):
        self.nc = nc
        self.es = es
        self.q = {k: [] for k in ENGINES}
        self.sems = {}
        for name in ("t", "v", "s", "cc"):
            self.sems[name] = [es.enter_context(nc.semaphore(name)), 0]
        self.waited = {k: {} for k in ENGINES}
        self.psmap, self.bank_w, self.bank_r = {}, {}, {}

    def _slot(self, name):
        if name not in self.sems:
            self.sems[name] = [self.es.enter_context(self.nc.semaphore(name)), 0]
        return name

    def tok(self, sem):
        return (sem, self.sems[sem][1])

    def emit(self, eng, fn, waits=(), inc=None, n=1):
        ws = []
        for w in waits:
            if w is None:
                continue
            sname, v = w
            if v <= 0:
                continue
            if v > self.waited[eng].get(sname, 0):
                self.waited[eng][sname] = v
                ws.append((sname, v))
        if inc is not None:
            self.sems[inc][1] += n

        def g(e, ws=tuple(ws), fn=fn, inc=inc, n=n):
            for sname, v in ws:
                e.wait_ge(self.sems[sname][0], v)
            r = fn(e)
            if inc is not None:
                r.then_inc(self.sems[inc][0], n)

        self.q[eng].append(g)
        return (inc, self.sems[inc][1]) if inc is not None else None

    def dma(self, q, out, in_, waits=(), slot="bulk"):
        sem = self._slot(("q." if q == "sync" else "g.") + slot)
        return self.emit(q, lambda e, o=out, i=in_: e.dma_start(out=o, in_=i), waits, inc=sem, n=16)

    def dma_acc(self, out, in_, waits=(), slot="acc"):
        sem = self._slot("g." + slot)
        return self.emit("gpsimd", lambda e, o=out, i=in_: e.dma_start(out=o, in_=i, accum_op=ALU.add),
                         waits, inc=sem, n=16)

    def drain(self, eng):
        self.q[eng].append(lambda e: e.drain())

    DEF_SEM = {"tensor": "t", "vector": "v", "scalar": "s"}

    def set_psum(self, psums):
        self.psmap = {g.ap().tensor.name: i for i, g in enumerate(psums)}
        self.bank_w = {}
        self.bank_r = {}

    def _banks_of(self, obj):
        if isinstance(obj, bass.AP):
            n = obj.tensor.name
            if n in self.psmap:
                return (self.psmap[n],)
        return ()

    def _psum_scan(self, args, kw):
        wr, rd = set(), set()
        for i, a in enumerate(args):
            (wr if i == 0 else rd).update(self._banks_of(a))
        for a in kw.values():
            rd.update(self._banks_of(a))
        return wr, rd

    def _psum_pre(self, eng, wr, rd):
        extra = []
        for b in rd | wr:
            w = self.bank_w.get(b)
            if w is not None and w[1] != eng:
                extra.append(w[0])
        for b in wr:
            for tk, e2 in self.bank_r.get(b, ()):
                if e2 != eng:
                    extra.append(tk)
        return extra

    def _psum_post(self, eng, wr, rd, tok):
        for b in wr:
            self.bank_w[b] = (tok, eng)
            self.bank_r[b] = []
        for b in rd:
            self.bank_r.setdefault(b, []).append((tok, eng))

    def op(self, eng, method, *args, waits=(), inc=None, n=1, **kw):
        wr, rd = self._psum_scan(args, kw)
        if (wr or rd) and inc is None:
            assert eng in self.DEF_SEM, (eng, method)
            inc = self.DEF_SEM[eng]
        extra = self._psum_pre(eng, wr, rd)

        def fn(e, method=method, args=args, kw=kw):
            return getattr(e, method)(*args, **kw)
        tok = self.emit(eng, fn, waits=tuple(waits) + tuple(extra), inc=inc, n=n)
        if wr or rd:
            self._psum_post(eng, wr, rd, tok)
        return tok

    def mm(self, out_ap, pairs, waits=(), inc="t", start=True, stop=True):
        tok = None
        npairs = len(pairs)
        for i, (st, mv) in enumerate(pairs):
            last = i == npairs - 1
            tok = self.op("tensor", "matmul", out_ap, st, mv, start=(start and i == 0), stop=(stop and last),
                          waits=(waits if i == 0 else ()), inc=(inc if last else None))
        return tok


class Banks:
    def __init__(self, tensors):
        self.t = list(tensors)
        self.free = [None] * len(tensors)
        self.rr = 0

    def get(self, k=1):
        out = []
        for _ in range(k):
            i = self.rr % len(self.t)
            self.rr += 1
            out.append(i)
        return out


def build(nc, num_cores=N_CORES, phases=PHASES, taps=TAPS):
    es = ExitStack()
    p = PB(nc, es)

    def din(name, shape, dtype=F32):
        return nc.dram_tensor(name, list(shape), dtype, kind="ExternalInput").ap()

    x0t = din("x0t", [D, T])
    w_inproj = din("w_inproj", [D, 2 * DIN])
    cdg0 = din("cdg0", [64, 128, 128])
    w_xproj = din("w_xproj", [DIN, 96])
    w_dtproj = din("w_dtproj", [DTR, DIN])
    w_outproj = din("w_outproj", [DIN, D])
    w_gate = din("w_gate", [D, DFF])
    w_up = din("w_up", [D, DFF])
    w_down = din("w_down", [DFF, D])
    w_ffn_in = din("w_ffn_in", [D, DFF])
    w_ffn_out = din("w_ffn_out", [DFF, D])
    w1_inT = din("w1_inT", [D, E1P])
    cdg1 = din("cdg1", [36, 128, 128])
    w1_outT = din("w1_outT", [DIN, D])
    w1_rT = din("w1_rT", [D, D])
    w1_kT = din("w1_kT", [D, D])
    w1_vT = din("w1_vT", [D, D])
    c_dtlnw = din("c_dtlnw", [64, 1])
    c_blnw = din("c_blnw", [16, 1])
    c_clnw = din("c_clnw", [16, 1])
    c_bclnw = din("c_bclnw", [32, 1])
    c_dtprojb = din("c_dtprojb", [128, 16])
    c_asc = din("c_asc", [128, 256])
    c_dsk = din("c_dsk", [128, 16])
    c_convb0 = din("c_convb0", [128, 16])
    c_premoe = din("c_premoe", [128, 8])
    c_bin = din("c_bin", [128, 32])
    c_bout = din("c_bout", [128, 8])
    c_dtbias1 = din("c_dtbias1", [64, 1])
    c_alog1 = din("c_alog1", [64, 1])
    c_dsk1 = din("c_dsk1", [128, 8])
    c_lnw1 = din("c_lnw1", [128, 16])
    c_lnb1 = din("c_lnb1", [128, 16])
    c_mur = din("c_mur", [128, 8])
    c_muk = din("c_muk", [128, 8])
    c_convb1 = din("c_convb1", [128, 9])
    k_ones = din("k_ones", [1, 128])
    k_mones = din("k_mones", [1, 128])
    k_ident = din("k_ident", [128, 128])
    k_c3 = din("k_c3", [96, 3])
    k_b3 = din("k_b3", [3, 96])
    k_sel8 = din("k_sel8", [8, 128])
    k_masktri = din("k_masktri", [128, 512])
    k_maskneg = din("k_maskneg", [128, 512])
    k_r1024 = din("k_r1024", [128, 1])
    k_r2048 = din("k_r2048", [128, 1])
    k_onescol = din("k_onescol", [128, 1])
    k_eps = din("k_eps", [128, 3])
    k_selhp = din("k_selhp", [64, 1024])
    k_insel = din("k_insel", [16, 1024])

    out = nc.dram_tensor("out", [D, T], F32, kind="ExternalOutput").ap()

    sp_hid = nc.dram_tensor("sp_hid", [DFF, T], F32, kind="Internal").ap()
    cc1i = nc.dram_tensor("cc1i", [1, 1], F32, kind="Internal").ap()
    cc1o = nc.dram_tensor("cc1o", [1, 1], F32, kind="Internal").ap()
    cc2i = nc.dram_tensor("cc2i", [1, 1], F32, kind="Internal").ap()
    cc2o = nc.dram_tensor("cc2o", [1, 1], F32, kind="Internal").ap()

    tap_list = []
    def tap(name, shape, dtype=F32):
        t = nc.dram_tensor(name, list(shape), dtype, kind="ExternalOutput").ap()
        tap_list.append(name)
        return t

    sb = lambda name, shape, dtype=F32: es.enter_context(nc.sbuf_tensor(name, list(shape), dtype))

    bufA = sb("bufA", [128, 16384])
    bufB = sb("bufB", [128, 32768], BF16)
    bcall = sb("bcall", [128, 16384], BF16)
    hstrip = sb("hstrip", [128, 8192], BF16)
    wsl = sb("wsl", [128, 6144], BF16)
    smallA = sb("smallA", [128, 1792])
    smallB = sb("smallB", [128, 1664])
    consts = sb("consts", [128, 384])
    constb = sb("constb", [128, 768], BF16)
    colsA = sb("colsA", [128, 512])
    carry = sb("carry", [128, 256])

    bufA_bf = bufA.ap().bitcast(BF16)
    bufB_f32 = bufB.ap().bitcast(F32)
    hstrip_f32 = hstrip.ap().bitcast(F32)
    bcall_f32 = bcall.ap().bitcast(F32)

    psum = [es.enter_context(nc.psum_tensor(f"ps{i}", [128, 512], F32)) for i in range(8)]
    p.set_psum(psum)
    banks = Banks(psum[:6])

    blk = es.enter_context(nc.Block())

    ONES = consts[0:1, 0:128]
    ONES64 = consts[64:65, 0:128]
    ONES32 = consts[32:33, 0:128]
    IDENT = consts.ap()[:, 128:256]
    C3 = consts[0:96, 256:259]
    B3 = consts[0:3, 260:356]
    R1024 = consts.ap()[:, 356:357]
    R2048 = consts.ap()[:, 357:358]
    ONESCOL = consts.ap()[:, 358:359]
    IDENTB = constb.ap()[:, 0:128]
    MASKTRI = constb.ap()[:, 128:640]
    SEL8 = constb[0:8, 640:768]

    DTLNW = colsA[0:64, 0:1]
    BLNW = colsA[0:16, 1:2]
    CLNW = colsA[0:16, 2:3]
    DTPROJB = colsA.ap()[:, 3:19]
    ASC = colsA.ap()[:, 19:275]
    DSK = colsA.ap()[:, 275:291]
    CONVB0 = colsA.ap()[:, 291:307]
    PREMOE = colsA.ap()[:, 307:315]
    BIN = colsA.ap()[:, 315:347]
    BOUT = colsA.ap()[:, 347:355]
    DTBIAS1 = colsA[0:64, 355:356]
    ALOG1 = colsA[0:64, 356:357]
    DSK1 = colsA.ap()[:, 357:365]
    LNW1 = colsA.ap()[:, 365:381]
    LNB1 = colsA.ap()[:, 381:397]
    MUR = colsA.ap()[:, 397:405]
    MUK = colsA.ap()[:, 405:413]
    ONEMMUR = colsA.ap()[:, 413:421]
    ONEMMUK = colsA.ap()[:, 421:429]
    CONVB1 = colsA.ap()[:, 429:438]
    AMAX1 = colsA.ap()[:, 438:439]
    S1COL = colsA.ap()[:, 439:440]
    INV1COL = colsA.ap()[:, 440:441]
    S2COL = colsA.ap()[:, 441:442]
    AMAX2 = colsA.ap()[:, 442:443]
    COLAMAX = colsA.ap()[:, 443:475]
    COLSC = colsA.ap()[:, 475:507]
    EPS5 = colsA.ap()[:, 507:508]
    EPS6 = colsA.ap()[:, 508:509]
    EPSG = colsA.ap()[:, 509:510]

    t_sy = []
    p.dma("sync", consts[0:1, 0:128], k_ones, slot="boot")
    p.dma("sync", consts[64:65, 0:128], k_ones, slot="boot")
    p.dma("sync", consts[32:33, 0:128], k_ones, slot="boot")
    p.dma("sync", consts.ap()[:, 128:256], k_ident, slot="boot")
    p.dma("sync", consts[0:96, 256:259], k_c3, slot="boot")
    p.dma("sync", consts[0:3, 260:356], k_b3, slot="boot")
    p.dma("sync", consts.ap()[:, 356:357], k_r1024, slot="boot")
    p.dma("sync", consts.ap()[:, 357:358], k_r2048, slot="boot")
    p.dma("sync", consts.ap()[:, 358:359], k_onescol, slot="boot")
    p.dma("gpsimd", constb.ap()[:, 0:128], k_ident, slot="bootg")
    p.dma("gpsimd", constb.ap()[:, 128:640], k_masktri, slot="bootg")
    p.dma("gpsimd", constb[0:8, 640:768], k_sel8, slot="bootg")
    p.dma("sync", colsA[0:64, 0:1], c_dtlnw, slot="boot")
    p.dma("sync", colsA[0:16, 1:2], c_blnw, slot="boot")
    p.dma("sync", colsA[0:16, 2:3], c_clnw, slot="boot")
    p.dma("sync", colsA[64:96, 1:2], c_bclnw, slot="boot")
    p.dma("sync", colsA.ap()[:, 3:19], c_dtprojb, slot="boot")
    p.dma("sync", colsA.ap()[:, 19:275], c_asc, slot="boot")
    p.dma("sync", colsA.ap()[:, 275:291], c_dsk, slot="boot")
    p.dma("sync", colsA.ap()[:, 291:307], c_convb0, slot="boot")
    p.dma("sync", colsA.ap()[:, 307:315], c_premoe, slot="boot")
    p.dma("sync", colsA.ap()[:, 315:347], c_bin, slot="boot")
    p.dma("sync", colsA.ap()[:, 347:355], c_bout, slot="boot")
    p.dma("sync", colsA[0:64, 355:356], c_dtbias1, slot="boot")
    p.dma("sync", colsA[0:64, 356:357], c_alog1, slot="boot")
    p.dma("sync", colsA.ap()[:, 357:365], c_dsk1, slot="boot")
    p.dma("sync", colsA.ap()[:, 365:381], c_lnw1, slot="boot")
    p.dma("sync", colsA.ap()[:, 381:397], c_lnb1, slot="boot")
    p.dma("sync", colsA.ap()[:, 397:405], c_mur, slot="boot")
    p.dma("sync", colsA.ap()[:, 405:413], c_muk, slot="boot")
    p.dma("sync", colsA.ap()[:, 429:438], c_convb1, slot="boot")
    p.dma("sync", colsA.ap()[:, 507:510], k_eps, slot="boot")

    tok_consts = p.tok("q.boot")
    tok_constb = p.tok("g.bootg")

    p.emit("scalar", lambda e: e.activation(ASC, ASC, ACT.Exp, bias=0.0, scale=1.0), waits=[tok_consts], inc="s")
    p.emit("scalar", lambda e: e.activation(colsA[0:64, 356:357], colsA[0:64, 356:357], ACT.Exp, bias=0.0, scale=1.0), inc="s")

    p.drain("scalar")
    p.emit("vector", lambda e: e.tensor_scalar(colsA.ap()[:, 3:19], colsA.ap()[:, 3:19], -1.0, None, op0=ALU.mult),
           waits=[tok_consts], inc="v")
    p.emit("vector", lambda e: e.tensor_scalar(colsA[0:64, 355:356], colsA[0:64, 355:356], -1.0, None, op0=ALU.mult), inc="v")
    p.emit("vector", lambda e: e.tensor_scalar(ONEMMUR, MUR, -1.0, 1.0, op0=ALU.mult, op1=ALU.add), inc="v")
    p.emit("vector", lambda e: e.tensor_scalar(ONEMMUK, MUK, -1.0, 1.0, op0=ALU.mult, op1=ALU.add), inc="v")
    p.emit("vector", lambda e: e.tensor_scalar(colsA[64:80, 1:2], colsA[64:80, 1:2], -1.0, None, op0=ALU.mult), inc="v")

    p.drain("vector")
    tok_setup_v = p.tok("v")
    tok_setup_s = p.tok("s")

    def xn_t(d, c0, c1):
        return bufA.ap()[:, d * T + c0: d * T + c1]

    x0sb = bufB_f32
    def x0_t(d, c0, c1):
        return x0sb[:, d * T + c0: d * T + c1]

    tok_x0 = p.dma("sync",
                   bass.AP(x0sb.tensor, x0sb.offset, [[16384, 128], [2048, 8], [1, 2048]]),
                   x0t.rearrange("(a p) t -> p a t", p=128), slot="x0")

    redbuf = smallA.ap()[:, 0:64]
    sqst = [smallB.ap()[:, i * 512:(i + 1) * 512] for i in range(2)]
    musb = smallA[0:1, 64:576]
    rrow = smallA[0:1, 576:1088]

    def emit_layernorm(src_t, dst_t, red_off, src_tok, do_red=True):
        lastv = None
        prev_chunk = None
        for c in range(NCH):
            c0, c1 = c * CH, (c + 1) * CH
            bmu, bsq, brb, bab = banks.get(4)
            pairs_mu = [(R1024, src_t(d, c0, c1)) for d in range(8)]
            tmu = p.mm(psum[bmu][0:1, 0:CH], pairs_mu, waits=[src_tok, tok_consts, banks.free[bmu]])
            sq_mm = [prev_chunk, prev_chunk]
            for d in range(8):
                stok = p.op("scalar", "activation", sqst[d % 2], src_t(d, c0, c1), ACT.Square,
                            bias=0.0, scale=1.0, waits=[src_tok, sq_mm[d % 2]], inc="s")
                mtok = p.op("tensor", "matmul", psum[bsq][0:1, 0:CH], R1024, sqst[d % 2],
                            start=(d == 0), stop=(d == 7),
                            waits=[stok] + ([banks.free[bsq]] if d == 0 else []), inc="t")
                sq_mm[d % 2] = mtok
            tsq = sq_mm[1]
            tmu2 = p.op("vector", "tensor_copy", musb, psum[bmu][0:1, 0:CH], waits=[tmu], inc="v")
            banks.free[bmu] = tmu2
            p.drain("vector")
            tm2 = p.op("vector", "tensor_tensor", rrow, musb, musb, op=ALU.mult, inc="v")
            p.drain("vector")
            tvar = p.op("vector", "tensor_tensor", rrow, psum[bsq][0:1, 0:CH], rrow, op=ALU.subtract,
                        waits=[tsq], inc="v")
            banks.free[bsq] = tvar
            ts1 = p.op("scalar", "activation", rrow, rrow, ACT.Sqrt, bias=EPS5[0:1, 0:1], scale=1.0,
                       waits=[tvar], inc="s")
            tr = p.op("vector", "reciprocal", rrow, rrow, waits=[ts1], inc="v")
            p.drain("vector")
            trb = p.mm(psum[brb][:, 0:CH], [(ONES, rrow)], waits=[tr, banks.free[brb]])
            tab = p.mm(psum[bab][:, 0:CH], [(ONES, musb)], waits=[tmu2, tr, banks.free[bab]])
            xn_toks = []
            for d in range(8):
                t1 = p.op("vector", "tensor_tensor", sqst[d % 2], src_t(d, c0, c1), psum[bab][:, 0:CH],
                          op=ALU.subtract, waits=[trb, tab, (None if d < 2 else xn_toks[d - 2])], inc="v")
                p.drain("vector")
                t2 = p.op("vector", "tensor_tensor", dst_t(d, c0, c1), sqst[d % 2], psum[brb][:, 0:CH],
                          op=ALU.mult, waits=[tab], inc="v")
                xn_toks.append(t2)
                if do_red:
                    p.drain("vector")
                    t3 = p.op("vector", "tensor_reduce",
                              redbuf[:, red_off + c * 8 + d: red_off + c * 8 + d + 1],
                              dst_t(d, c0, c1), axis=AX, op=ALU.max, apply_absolute_value=True, inc="v")
                    lastv = t3
                else:
                    lastv = t2
            banks.free[brb] = lastv
            banks.free[bab] = lastv
            prev_chunk = xn_toks[-1]
        return lastv

    tok_xn = emit_layernorm(x0_t, xn_t, 0, tok_x0)

    am_col = smallA.ap()[:, 1664:1665]
    am1 = smallA[0:1, 1671:1672]
    t1 = p.emit("vector", lambda e: e.tensor_reduce(am_col, redbuf[:, 0:32], axis=AX, op=ALU.max), waits=[tok_xn], inc="v")
    btr = banks.get(1)[0]
    t2 = p.op("tensor", "transpose", psum[btr][0:1, 0:128], am_col, IDENT,
              waits=[t1, banks.free[btr], tok_consts], inc="t")
    t3 = p.op("vector", "tensor_reduce", am1, psum[btr][0:1, 0:128], axis=AX, op=ALU.max, waits=[t2], inc="v")
    banks.free[btr] = t3
    tok_am1_dma = p.dma("sync", cc1i, am1, waits=[t3], slot="cc1w")
    tok_cc1 = p.emit("gpsimd",
                     lambda e: e.collective_compute("AllReduce", ALU.max, replica_groups=[list(range(num_cores))],
                                                    ins=[cc1i], outs=[cc1o]),
                     waits=[tok_am1_dma], inc="cc")
    tok_am1_back = p.dma("gpsimd", am1, cc1o, waits=[tok_cc1], slot="cc1r")
    s1row = smallA[0:1, 1672:1673]
    inv1row = smallA[0:1, 1673:1674]
    t4 = p.emit("vector", lambda e: e.reciprocal(s1row, am1), waits=[tok_am1_back], inc="v")
    p.drain("vector")
    t5 = p.emit("vector", lambda e: e.tensor_scalar(s1row, s1row, 32767.0, None, op0=ALU.mult), inc="v")
    t6 = p.emit("vector", lambda e: e.tensor_scalar(inv1row, am1, 1.0 / 32767.0, None, op0=ALU.mult),
                waits=[tok_am1_back], inc="v")
    bq = banks.get(1)[0]
    t7a = p.mm(psum[bq][:, 0:1], [(ONES, am1)], waits=[tok_am1_back, banks.free[bq]])
    t7b = p.mm(psum[bq][:, 1:2], [(ONES, s1row)], waits=[t5])
    t7c = p.mm(psum[bq][:, 2:3], [(ONES, inv1row)], waits=[t6])
    t8 = p.op("vector", "tensor_copy", AMAX1, psum[bq][:, 0:1], waits=[t7a], inc="v")
    t8b = p.op("vector", "tensor_copy", S1COL, psum[bq][:, 1:2], waits=[t7b], inc="v")
    t8c = p.op("vector", "tensor_copy", INV1COL, psum[bq][:, 2:3], waits=[t7c], inc="v")
    banks.free[bq] = t8c
    tok_s1 = t8b
    tok_inv1 = t8c

    if taps:
        tp_xn = tap("tp_xn", [D, T])
        p.dma("sync", tp_xn.rearrange("(a p) t -> p a t", p=128),
              bass.AP(bufA.ap().tensor, 0, [[16384, 128], [2048, 8], [1, 2048]]), waits=[tok_xn], slot="tap")
        tp_am = tap("tp_am", [1, 1])
        p.dma("sync", tp_am, am1, waits=[tok_am1_back], slot="tap")

    tok_hid_spill = None
    tok_cc2 = None
    if phases >= 1:
        K_in = bufB.ap()
        def kin_ap(k, m0, m1):
            return bufB.ap()[:, k * 4096 + m0: k * 4096 + m1]
        wfa = hstrip_f32
        qtmp = smallB.ap()[:, 0:512]
        qsrowA = smallA[0:1, 64:576]
        cmx = smallA.ap()[:, 1666:1670]

        tok_prev_use = [None] * 8
        prev_wq2 = None
        for ms in range(8):
            m0 = ms * 512
            ldt = []
            for k in range(8):
                tok = p.dma("sync", wfa[:, k * 512:(k + 1) * 512],
                            w_ffn_in[k * 128:(k + 1) * 128, m0:m0 + 512],
                            waits=[tok_prev_use[k]], slot="wf%d" % k)
                ldt.append(tok)
            lv1 = [smallB.ap()[:, 0:512], smallB.ap()[:, 512:1024]]
            lvm = smallB.ap()[:, 1024:1536]
            p.op("vector", "tensor_tensor", lv1[0], wfa[:, 0:512], wfa[:, 512:1024], op=ALU.max,
                 waits=[ldt[0], ldt[1], prev_wq2], inc="v")
            p.op("vector", "tensor_tensor", lvm, wfa[:, 0:512], wfa[:, 512:1024], op=ALU.min, inc="v")
            for k2 in (1, 2, 3):
                c0 = k2 * 1024
                p.op("vector", "tensor_tensor", lv1[1], wfa[:, c0:c0 + 512], wfa[:, c0 + 512:c0 + 1024],
                     op=ALU.max, waits=[ldt[2 * k2], ldt[2 * k2 + 1]], inc="v")
                p.drain("vector")
                p.op("vector", "tensor_tensor", lv1[0], lv1[0], lv1[1], op=ALU.max, inc="v")
                p.drain("vector")
                p.op("vector", "tensor_tensor", lv1[1], wfa[:, c0:c0 + 512], wfa[:, c0 + 512:c0 + 1024],
                     op=ALU.min, inc="v")
                p.drain("vector")
                p.op("vector", "tensor_tensor", lvm, lvm, lv1[1], op=ALU.min, inc="v")
                p.drain("vector")
            tG = p.op("vector", "scalar_tensor_tensor", lv1[0], lvm, -1.0, lv1[0],
                      op0=ALU.mult, op1=ALU.max, inc="v")
            last_tt = None
            for j in range(4):
                bt = banks.get(1)[0]
                tt = p.op("tensor", "transpose", psum[bt][:, 0:128], lv1[0][:, j * 128:(j + 1) * 128], IDENT,
                          waits=[tG, banks.free[bt]], inc="t")
                last_tt = tt
                tr2 = p.op("vector", "tensor_reduce", cmx[:, j:j + 1], psum[bt][:, 0:128], axis=AX, op=ALU.max,
                           waits=[tt], inc="v")
                banks.free[bt] = tr2
            p.drain("vector")
            p.op("vector", "tensor_scalar", COLAMAX[:, ms * 4:ms * 4 + 4], cmx, 1e-8, None, op0=ALU.max, inc="v")
            p.drain("vector")
            p.op("vector", "tensor_scalar", COLSC[:, ms * 4:ms * 4 + 4], COLAMAX[:, ms * 4:ms * 4 + 4],
                 1.0 / 127.0, None, op0=ALU.mult, inc="v")
            p.op("vector", "reciprocal", cmx, COLAMAX[:, ms * 4:ms * 4 + 4], inc="v")
            p.drain("vector")
            t12 = p.op("vector", "tensor_scalar", cmx, cmx, 127.0, None, op0=ALU.mult, inc="v")
            t14s = []
            bt = banks.get(1)[0]
            for j in range(4):
                t13 = p.op("tensor", "transpose", psum[bt][0:1, j * 128:(j + 1) * 128], cmx[:, j:j + 1], IDENT,
                           waits=[t12] + ([banks.free[bt]] if j == 0 else []), inc="t")
                t14 = p.op("vector", "tensor_copy", qsrowA[:, j * 128:(j + 1) * 128],
                           psum[bt][0:1, j * 128:(j + 1) * 128], waits=[t13, tok_inv1], inc="v")
                t14s.append(t14)
            banks.free[bt] = t14s[-1]
            bqb = banks.get(1)[0]
            t15 = p.mm(psum[bqb][:, 0:512], [(ONES, qsrowA)], waits=t14s + [banks.free[bqb]])
            lastq = None
            for k in range(8):
                tq1 = p.op("vector", "tensor_tensor", qtmp, wfa[:, k * 512:(k + 1) * 512], psum[bqb][:, 0:512],
                           op=ALU.mult, waits=[t15, last_tt], inc="v")
                tq2 = p.op("scalar", "activation", qtmp, qtmp, ACT.Copy, bias=MAGIC, scale=1.0,
                           waits=[tq1], inc="s")
                prev_wq2 = tq2
                tq4 = p.op("vector", "tensor_scalar", kin_ap(k, m0, m0 + 512), qtmp, -MAGIC, None,
                           op0=ALU.add, waits=[tq2], inc="v")
                lastq = tq4
                tok_prev_use[k] = tq1
            banks.free[bqb] = lastq
        tok_kin = p.tok("v")

        hred = smallB.ap()[:, 1024:1152]
        hstage = [smallB.ap()[:, 1152:1664], smallB.ap()[:, 0:512]]
        qpre = [smallB.ap()[:, 0:512], smallB.ap()[:, 512:1024]]
        hred_n = 0
        tok_hstage = [[], []]
        tok_xsim_use = [None] * 8
        prev_q3 = [None, None]
        p.drain("vector")
        for c in range(NCH):
            c0 = c * CH
            xs_toks = []
            for d in range(8):
                xs = wsl.ap()[:, d * 512:(d + 1) * 512]
                tq1 = p.op("scalar", "activation", qpre[d % 2], xn_t(d, c0, c0 + CH), ACT.Copy,
                           bias=MAGIC, scale=S1COL,
                           waits=[tok_s1, tok_xn, tok_kin, tok_xsim_use[d], prev_q3[d % 2]]
                           + tok_hstage[0] + tok_hstage[1], inc="s")
                tq3 = p.op("vector", "tensor_scalar", xs, qpre[d % 2], -MAGIC, INV1COL,
                           op0=ALU.add, op1=ALU.mult, waits=[tq1, tok_inv1], inc="v")
                prev_q3[d % 2] = tq3
                xs_toks.append(tq3)
            for m in range(32):
                bmm = banks.get(1)[0]
                pairs = [(kin_ap(k, m * 128, (m + 1) * 128), wsl.ap()[:, k * 512:(k + 1) * 512]) for k in range(8)]
                tmm = p.mm(psum[bmm][:, 0:CH], pairs, waits=xs_toks + [tok_kin, banks.free[bmm]])
                tok_xsim_use = [tmm] * 8
                hs = hstage[m % 2]
                tg = p.op("scalar", "activation", hs, psum[bmm][:, 0:CH], ACT.Gelu,
                          bias=BIN[:, m:m + 1], scale=COLSC[:, m:m + 1],
                          waits=[tmm, tok_kin] + tok_hstage[m % 2], inc="s")
                banks.free[bmm] = tg
                tr = p.op("vector", "tensor_reduce", hred[:, hred_n:hred_n + 1], hs, axis=AX,
                          op=ALU.max, apply_absolute_value=True, waits=[tg], inc="v")
                hred_n += 1
                tspill = p.dma("sync", sp_hid[m * 128:(m + 1) * 128, c0:c0 + CH], hs, waits=[tg],
                               slot="hs%d" % (m % 2))
                tok_hstage[m % 2] = [tspill, tr]
        tok_hid_spill = [p.tok("q.hs0"), p.tok("q.hs1")]
        am2loc = smallA.ap()[:, 1665:1666]
        am2rowT = smallA[0:1, 1674:1675]
        t1 = p.emit("vector", lambda e: e.tensor_reduce(am2loc, hred, axis=AX, op=ALU.max), waits=[p.tok("v"), p.tok("s")], inc="v")
        bt = banks.get(1)[0]
        t2 = p.op("tensor", "transpose", psum[bt][0:1, 0:128], am2loc, IDENT,
                  waits=[t1, banks.free[bt]], inc="t")
        t3 = p.op("vector", "tensor_reduce", am2rowT[0:1, 0:1], psum[bt][0:1, 0:128],
                  axis=AX, op=ALU.max, waits=[t2], inc="v")
        banks.free[bt] = t3
        tok_am2_dma = p.dma("sync", cc2i, am2rowT[0:1, 0:1], waits=[t3], slot="cc2w")
        tok_cc2 = p.emit("gpsimd",
                         lambda e: e.collective_compute("AllReduce", ALU.max, replica_groups=[list(range(num_cores))],
                                                        ins=[cc2i], outs=[cc2o]),
                         waits=[tok_am2_dma], inc="cc")

        if taps:
            tp_h = tap("tp_hid", [DFF, CH])
            p.dma("sync", tp_h, sp_hid[:, 0:CH], waits=tok_hid_spill, slot="tap")



    if phases >= 2:
        p.drain("vector")
        p.drain("scalar")
        gate = [p.tok("v"), p.tok("s"), p.tok("t")] + (tok_hid_spill or [])
        if "q.tap" in p.sems:
            gate.append(p.tok("q.tap"))
        xc_t = lambda d: bufB.ap()[:, d * 512:(d + 1) * 512]
        zsil_t = lambda d: bufB.ap()[:, 8192 + d * 512: 8192 + (d + 1) * 512]
        mdt_t = lambda d: bufB.ap()[:, 16384 + d * 512: 16384 + (d + 1) * 512]
        dtxy_t = lambda d: bufB.ap()[:, 24576 + d * 512: 24576 + (d + 1) * 512]
        bbc_t = lambda n: bcall.ap()[:, n * 512:(n + 1) * 512]
        cbc_t = lambda n: bcall.ap()[:, 8192 + n * 512: 8192 + (n + 1) * 512]
        xnbf_t = lambda k: hstrip.ap()[:, k * 512:(k + 1) * 512]
        h_t = lambda n: hstrip.ap()[:, n * 512:(n + 1) * 512]
        cdsl_t = lambda si: bass.AP(hstrip.ap().tensor, 4096 + si * 512, [[8192, 128], [128, 4], [1, 128]])
        cdblk = lambda si, kk: bass.AP(hstrip.ap().tensor, 4096 + si * 512 + kk * 128, [[8192, 128], [1, 128]])
        wslab_blk = lambda si, k: bass.AP(wsl.ap().tensor, si * 1024 + k * 128, [[6144, 128], [1, 128]])
        xpw_blk = lambda k: bass.AP(wsl.ap().tensor, 2048 + k * 96, [[6144, 128], [1, 96]])
        dtw_blk = lambda m: bass.AP(wsl.ap().tensor, (m % 2) * 64 * 6144 + 3584 + (m // 2) * 128, [[6144, 64], [1, 128]])
        stage = wsl.ap()[:, 4608:5123]
        halo = wsl.ap()[:, 5123:5171]
        dtn0 = wsl[0:64, 5171:5683]
        dtn64 = wsl[64:128, 5171:5683]
        proj96 = smallA[0:96, 0:512]
        projBC = smallA[64:96, 0:512]
        sq96 = smallA[0:96, 512:1024]
        r3 = smallA[0:3, 1024:1536]
        BCm = smallB[64:96, 0:512]
        t1buf = smallB.ap()[:, 1024:1536]

        def selBC(row):
            return bass.AP(consts.ap().tensor, 64 * 384 + 128 + 64 + row, [[384, 32], [0, 128]])

        tok_xpw = p.dma("gpsimd", bass.AP(wsl.ap().tensor, 2048, [[6144, 128], [96, 16], [1, 96]]),
                        bass.AP(w_xproj.tensor, 0, [[96, 128], [96 * 128, 16], [1, 96]]),
                        waits=gate, slot="wxp")
        tok_dtw = []
        for hh in range(2):
            dst = bass.AP(wsl.ap().tensor, hh * 64 * 6144 + 3584, [[6144, 64], [128, 8], [1, 128]])
            srcap = bass.AP(w_dtproj.tensor, hh * 128, [[2048, 64], [256, 8], [1, 128]])
            tok_dtw.append(p.dma("gpsimd", dst, srcap, waits=gate, slot="wdt%d" % hh))
        th0 = p.op("vector", "memset", halo, 0.0, waits=gate, inc="v")
        p.drain("vector")

        slab_free = [None, None]
        cd_free = [None, None]
        prev_h_user = None
        prev_h_user_t = None
        halo_tok = [th0] * 16
        stage_free = None
        carry_tok = [None] * 16
        prev_scan_user = None
        dA_free = [None, None]

        for c in range(NCH):
            c0 = c * CH
            xnbf_toks = []
            for k in range(8):
                tkx = p.op("vector", "tensor_copy", xnbf_t(k), xn_t(k, c0, c0 + CH),
                           waits=gate + [prev_h_user, prev_h_user_t, tok_xn], inc="v")
                xnbf_toks.append(tkx)
            p.drain("vector")
            last_conv_mm = None
            for half in range(2):
                coff = 0 if half == 0 else DIN
                for m in range(16):
                    si = m % 2
                    srcap = bass.AP(w_inproj.tensor, coff + m * 128, [[4096, 128], [4096 * 128, 8], [1, 128]])
                    dstap = bass.AP(wsl.ap().tensor, si * 1024, [[6144, 128], [128, 8], [1, 128]])
                    twl = p.dma("gpsimd", dstap, srcap, waits=[slab_free[si]], slot="ws%d" % si)
                    bmm = banks.get(1)[0]
                    pairs = [(wslab_blk(si, k), xnbf_t(k)) for k in range(8)]
                    tmm = p.mm(psum[bmm][:, 0:CH], pairs, waits=xnbf_toks + [twl, banks.free[bmm]])
                    slab_free[si] = tmm
                    if half == 1:
                        tz = p.op("scalar", "activation", zsil_t(m), psum[bmm][:, 0:CH], ACT.Silu,
                                  bias=0.0, scale=1.0, waits=[tmm, prev_h_user, prev_h_user_t], inc="s")
                        banks.free[bmm] = tz
                        continue
                    thalo = p.op("vector", "tensor_copy", stage[:, 0:3], halo[:, m * 3:m * 3 + 3],
                                 waits=[halo_tok[m], stage_free], inc="v")
                    p.drain("vector")
                    tev = p.op("vector", "tensor_copy", stage[:, 3:515], psum[bmm][:, 0:CH],
                               waits=[tmm], inc="v")
                    banks.free[bmm] = tev
                    p.drain("vector")
                    thl = p.op("vector", "tensor_copy", halo[:, m * 3:m * 3 + 3], stage[:, 512:515], inc="v")
                    halo_tok[m] = thl
                    csrc = bass.AP(cdg0.tensor, m * 4 * 128 * 128, [[128, 128], [128 * 128, 4], [1, 128]])
                    tcd = p.dma("gpsimd", cdsl_t(si), csrc, waits=[cd_free[si], prev_h_user, prev_h_user_t], slot="cd%d" % si)
                    bcv = banks.get(1)[0]
                    cpairs = [(cdblk(si, kk), stage[:, kk:kk + 512]) for kk in range(4)]
                    tcv = p.mm(psum[bcv][:, 0:CH], cpairs, waits=[tcd, tev, thl, thalo, banks.free[bcv]])
                    cd_free[si] = tcv
                    stage_free = tcv
                    last_conv_mm = tcv
                    txc = p.op("scalar", "activation", xc_t(m), psum[bcv][:, 0:CH], ACT.Silu,
                               bias=CONVB0[:, m:m + 1], scale=1.0, waits=[tcv, prev_h_user, prev_h_user_t], inc="s")
                    banks.free[bcv] = txc
            tok_xc = p.tok("s")
            bxp = banks.get(1)[0]
            xpairs = [(xpw_blk(k), xc_t(k)) for k in range(16)]
            txp = p.mm(psum[bxp][0:96, 0:CH], xpairs, waits=[tok_xpw, tok_xc, banks.free[bxp]])
            tpe = p.op("vector", "tensor_copy", proj96, psum[bxp][0:96, 0:CH], waits=[txp], inc="v")
            banks.free[bxp] = tpe
            tsqp = p.op("scalar", "activation", sq96, proj96, ACT.Square, bias=0.0, scale=1.0,
                        waits=[tpe], inc="s")
            bms = banks.get(1)[0]
            tms = p.mm(psum[bms][0:3, 0:CH], [(C3, sq96)], waits=[tsqp, banks.free[bms]])
            tr3 = p.op("scalar", "activation", r3, psum[bms][0:3, 0:CH], ACT.Sqrt,
                       bias=EPS6[0:3, 0:1], scale=1.0, waits=[tms], inc="s")
            banks.free[bms] = tr3
            trc = p.op("vector", "reciprocal", r3, r3, waits=[tr3], inc="v")
            p.drain("vector")
            brb = banks.get(1)[0]
            trb = p.mm(psum[brb][0:96, 0:CH], [(B3, r3)], waits=[trc, banks.free[brb]])
            tdtn = p.op("vector", "scalar_tensor_tensor", dtn0, smallA[0:64, 0:512], DTLNW,
                        psum[brb][0:64, 0:CH], op0=ALU.mult, op1=ALU.mult, waits=[trb, tok_dtw[0], tok_dtw[1]], inc="v")
            tbcm = p.op("vector", "scalar_tensor_tensor", BCm, projBC, colsA[64:96, 1:2],
                        psum[brb][64:96, 0:CH], op0=ALU.mult, op1=ALU.mult, waits=[trb, tok_setup_v], inc="v")
            banks.free[brb] = tbcm
            tdtn64 = p.dma("gpsimd", dtn64, dtn0, waits=[tdtn], slot="dtnc")
            tln_prev = None
            for m in range(16):
                bdt = banks.get(1)[0]
                tdm = p.mm(psum[bdt][:, 0:CH], [(dtw_blk(m), dtn0 if m % 2 == 0 else dtn64)],
                           waits=[tdtn if m % 2 == 0 else tdtn64, banks.free[bdt]])
                tsg = p.op("scalar", "activation", t1buf, psum[bdt][:, 0:CH], ACT.Sigmoid,
                           bias=DTPROJB[:, m:m + 1], scale=-1.0, waits=[tdm, tln_prev, tok_setup_v], inc="s")
                banks.free[bdt] = tsg
                p.drain("scalar")
                tln = p.op("scalar", "activation", mdt_t(m), t1buf, ACT.Ln,
                           bias=EPSG, scale=1.0, waits=[prev_h_user, prev_h_user_t], inc="s")
                p.drain("scalar")
                tln_prev = tln
            tok_mdt = p.tok("s")
            dtx_toks = []
            for m in range(16):
                td = p.op("vector", "tensor_tensor", dtxy_t(m), mdt_t(m), xc_t(m), op=ALU.mult,
                          waits=[tok_mdt, tok_xc, prev_h_user, prev_h_user_t], inc="v")
                dtx_toks.append(td)
            bc_toks = []
            for n in range(16):
                bb = banks.get(1)[0]
                tb = p.mm(psum[bb][:, 0:CH], [(selBC(n), BCm)], waits=[tbcm, tok_consts, banks.free[bb]])
                te = p.op("scalar", "activation", bbc_t(n), psum[bb][:, 0:CH], ACT.Copy,
                          bias=0.0, scale=1.0, waits=[tb, prev_scan_user], inc="s")
                banks.free[bb] = te
                bc = banks.get(1)[0]
                tb2 = p.mm(psum[bc][:, 0:CH], [(selBC(16 + n), BCm)], waits=[tbcm, banks.free[bc]])
                te2 = p.op("scalar", "activation", cbc_t(n), psum[bc][:, 0:CH], ACT.Copy,
                           bias=0.0, scale=1.0, waits=[tb2, prev_scan_user], inc="s")
                banks.free[bc] = te2
                bc_toks.append((te, te2))
            bbuf = [wsl.ap()[:, 4608:5120], wsl.ap()[:, 5171:5683]]
            bbuf_free = [last_conv_mm, (tdtn64 if True else None)]
            bbuf_free[1] = p.tok("t")
            sum_evac = None
            new_h_user = None
            for dt_ in range(16):
                bsum = banks.get(1)[0]
                scan_toks = []
                for n in range(16):
                    dab = 6 + (n % 2)
                    tda = p.op("scalar", "activation", psum[dab][:, 0:CH], mdt_t(dt_), ACT.Exp,
                               scale=ASC[:, n * 16 + dt_:n * 16 + dt_ + 1], bias=0.0,
                               waits=[tok_mdt, dA_free[n % 2], tok_setup_s], inc="s")
                    tb = p.op("vector", "tensor_tensor", bbuf[n % 2], dtxy_t(dt_), bbc_t(n), op=ALU.mult,
                              waits=[dtx_toks[dt_], bc_toks[n][0], bbuf_free[n % 2]], inc="v")
                    p.drain("vector")
                    init = 0.0 if c == 0 else carry[:, dt_ * 16 + n: dt_ * 16 + n + 1]
                    tsc = p.op("vector", "tensor_tensor_scan", h_t(n), psum[dab][:, 0:CH], bbuf[n % 2],
                               init, op0=ALU.mult, op1=ALU.add,
                               waits=[tda, tb, carry_tok[dt_], prev_h_user, last_conv_mm,
                                      xnbf_toks[-1], p.tok("t") if (dt_ == 0 and n == 0) else None], inc="v")
                    p.drain("vector")
                    dA_free[n % 2] = tsc
                    bbuf_free[n % 2] = tsc
                    scan_toks.append(tsc)
                if c < NCH - 1:
                    tcar = p.op("vector", "tensor_copy", carry[:, dt_ * 16:dt_ * 16 + 16],
                                bass.AP(hstrip.ap().tensor, 511, [[8192, 128], [512, 16]]),
                                waits=scan_toks, inc="v")
                    carry_tok[dt_] = tcar
                    p.drain("vector")
                tsum = None
                for n in range(16):
                    tp_ = p.op("vector", "tensor_tensor", h_t(n), h_t(n), cbc_t(n), op=ALU.mult,
                               waits=[scan_toks[n], bc_toks[n][1], carry_tok[dt_]], inc="v")
                    p.drain("vector")
                    tsum = p.op("tensor", "matmul", psum[bsum][:, 0:CH], IDENTB, h_t(n),
                                start=(n == 0), stop=(n == 15),
                                waits=[tp_, tok_constb] + ([banks.free[bsum]] if n == 0 else []),
                                inc=("t" if n == 15 else None))
                    new_h_user = tsum
                ty1 = p.op("vector", "scalar_tensor_tensor", t1buf, xc_t(dt_), DSK[:, dt_:dt_ + 1],
                           psum[bsum][:, 0:CH], op0=ALU.mult, op1=ALU.add,
                           waits=[tsum, p.tok("s")], inc="v")
                banks.free[bsum] = ty1
                p.drain("vector")
                ty2 = p.op("vector", "tensor_tensor", dtxy_t(dt_), t1buf, zsil_t(dt_), op=ALU.mult, inc="v")
                p.drain("vector")
            prev_scan_user = p.tok("v")
            prev_h_user_next = p.tok("v")
            tok_y = p.tok("v")
            for m in range(8):
                tw1 = p.dma("gpsimd", bass.AP(wsl.ap().tensor, 0, [[6144, 128], [128, 8], [1, 128]]),
                            bass.AP(w_outproj.tensor, m * 128, [[1024, 128], [1024 * 128, 8], [1, 128]]),
                            waits=[slab_free[0]], slot="ws0")
                tw2 = p.dma("gpsimd", bass.AP(wsl.ap().tensor, 1024, [[6144, 128], [128, 8], [1, 128]]),
                            bass.AP(w_outproj.tensor, 8 * 128 * 1024 + m * 128, [[1024, 128], [1024 * 128, 8], [1, 128]]),
                            waits=[slab_free[1]], slot="ws1")
                bop = banks.get(1)[0]
                pairs = [(wslab_blk(0, k), dtxy_t(k)) for k in range(8)] +                         [(wslab_blk(1, k), dtxy_t(8 + k)) for k in range(8)]
                tmo = p.mm(psum[bop][:, 0:CH], pairs, waits=[tw1, tw2, tok_y, banks.free[bop]])
                slab_free[0] = tmo
                slab_free[1] = tmo
                thid = p.op("vector", "tensor_tensor", xn_t(m, c0, c0 + CH), psum[bop][:, 0:CH],
                            xn_t(m, c0, c0 + CH), op=ALU.add, waits=[tmo], inc="v")
                banks.free[bop] = thid
            prev_h_user = p.tok("v")
            prev_h_user_t = p.tok("t")
        tok_hid = p.tok("v")
        dA_user = p.tok("v")

        if taps:
            tp_hd = tap("tp_hidmx", [D, T])
            p.dma("sync", tp_hd.rearrange("(a p) t -> p a t", p=128),
                  bass.AP(bufA.ap().tensor, 0, [[16384, 128], [2048, 8], [1, 2048]]), waits=[tok_hid], slot="tap")


    tok_mlp_out = None
    if phases >= 3:
        p.drain("vector")
        p.drain("scalar")
        gate2 = [p.tok("v"), p.tok("s"), p.tok("t"), tok_hid]
        MCH = 1024
        hn_t = lambda d, cc: bcall.ap()[:, d * 1024 + cc * 512: d * 1024 + cc * 512 + 512]
        ff_t = lambda m, cc: bufB.ap()[:, m * 1024 + cc * 512: m * 1024 + cc * 512 + 512]
        stgb = wsl.ap()[:, 4608:5120]
        x0stg = smallB.ap()[:, 1024:1536]
        stg1 = smallA.ap()[:, 1152:1664]
        hid_t = xn_t
        prev_ff_user = None
        stgw_free = None
        tok_out_w = []
        for mc in range(2):
            m0 = mc * MCH
            hn_toks = []
            for cc in range(2):
                cb_, cs_ = m0 + cc * 512, 512
                bsq2 = banks.get(1)[0]
                sqmm = [None, None]
                for d in range(8):
                    st = p.op("scalar", "activation", sqst[d % 2], hid_t(d, cb_, cb_ + 512), ACT.Square,
                              bias=0.0, scale=1.0, waits=gate2 + [sqmm[d % 2]], inc="s")
                    mt = p.op("tensor", "matmul", psum[bsq2][0:1, 0:512], R1024, sqst[d % 2],
                              start=(d == 0), stop=(d == 7),
                              waits=[st] + ([banks.free[bsq2]] if d == 0 else []), inc="t")
                    sqmm[d % 2] = mt
                trr = p.op("scalar", "activation", rrow, psum[bsq2][0:1, 0:512], ACT.Sqrt,
                           bias=EPS6[0:1, 0:1], scale=1.0, waits=[sqmm[1]], inc="s")
                banks.free[bsq2] = trr
                trc = p.op("vector", "reciprocal", rrow, rrow, waits=[trr, gate2[0]], inc="v")
                brb2 = banks.get(1)[0]
                trb = p.mm(psum[brb2][:, 0:512], [(ONES, rrow)], waits=[trc, banks.free[brb2]])
                lastt = None
                for d in range(8):
                    th = p.op("vector", "scalar_tensor_tensor", hn_t(d, cc) if mc == 0 else hn_t(d, cc),
                              hid_t(d, cb_, cb_ + 512), PREMOE[:, d:d + 1], psum[brb2][:, 0:512],
                              op0=ALU.mult, op1=ALU.mult, waits=[trb, prev_ff_user], inc="v")
                    lastt = th
                    hn_toks.append(th)
                banks.free[brb2] = lastt
            prev_tff = None
            for m in range(32):
                twg = p.dma("gpsimd", bass.AP(wsl.ap().tensor, 0, [[6144, 128], [128, 8], [1, 128]]),
                            bass.AP(w_gate.tensor, m * 128, [[4096, 128], [4096 * 128, 8], [1, 128]]),
                            waits=[slab_free[0]], slot="ws0")
                twu = p.dma("gpsimd", bass.AP(wsl.ap().tensor, 1024, [[6144, 128], [128, 8], [1, 128]]),
                            bass.AP(w_up.tensor, m * 128, [[4096, 128], [4096 * 128, 8], [1, 128]]),
                            waits=[slab_free[1]], slot="ws1")
                lg = lu = None
                for cc in range(2):
                    bg = banks.get(1)[0]
                    lg = p.mm(psum[bg][:, 0:512], [(wslab_blk(0, k), hn_t(k, cc)) for k in range(8)],
                              waits=hn_toks + [twg, banks.free[bg]])
                    bu = banks.get(1)[0]
                    lu = p.mm(psum[bu][:, 0:512], [(wslab_blk(1, k), hn_t(k, cc)) for k in range(8)],
                              waits=[twu, banks.free[bu]])
                    tgs = p.op("scalar", "activation", stgb, psum[bg][:, 0:512], ACT.Silu,
                               bias=0.0, scale=1.0, waits=[lg, prev_ff_user, prev_tff], inc="s")
                    banks.free[bg] = tgs
                    tff = p.op("vector", "tensor_tensor", ff_t(m, cc), stgb, psum[bu][:, 0:512],
                               op=ALU.mult, waits=[tgs, lu, prev_ff_user], inc="v")
                    banks.free[bu] = tff
                    prev_tff = tff
                slab_free[0] = lg
                slab_free[1] = lu
            tok_ff = p.tok("v")
            for md in range(8):
                bks = banks.get(2)
                tdm_last = [None, None]
                for kb in range(4):
                    sl = kb % 2
                    tws = p.dma("gpsimd", bass.AP(wsl.ap().tensor, sl * 1024, [[6144, 128], [128, 8], [1, 128]]),
                                bass.AP(w_down.tensor, kb * 8 * 128 * 1024 + md * 128,
                                        [[1024, 128], [1024 * 128, 8], [1, 128]]),
                                waits=[slab_free[sl]], slot="ws%d" % sl)
                    for cc in range(2):
                        pairs = [(wslab_blk(sl, k), ff_t(kb * 8 + k, cc)) for k in range(8)]
                        tdm_last[cc] = p.mm(psum[bks[cc]][:, 0:512], pairs,
                                            waits=[tws, tok_ff] + ([banks.free[bks[cc]]] if kb == 0 else []),
                                            start=(kb == 0), stop=(kb == 3))
                    slab_free[sl] = tdm_last[1]
                for cc in range(2):
                    cb_ = m0 + cc * 512
                    tx0 = p.dma("sync", x0stg, x0t[md * 128:(md + 1) * 128, cb_:cb_ + 512],
                                waits=([prev_ff_user] if prev_ff_user is not None else gate2), slot="x0r")
                    t1 = p.op("vector", "tensor_tensor", stg1, psum[bks[cc]][:, 0:512],
                              hid_t(md, cb_, cb_ + 512), op=ALU.add,
                              waits=[tdm_last[cc], stgw_free], inc="v")
                    banks.free[bks[cc]] = t1
                    p.drain("vector")
                    t2 = p.op("vector", "tensor_tensor", stg1, stg1, x0stg, op=ALU.add, waits=[tx0], inc="v")
                    p.drain("vector")
                    tws_ = p.dma("sync", out[md * 128:(md + 1) * 128, cb_:cb_ + 512], stg1, waits=[t2], slot="outw")
                    tok_out_w.append(tws_)
                    stgw_free = tws_
                    prev_ff_user = t2
        tok_mlp_out = p.tok("q.outw")

    if phases >= 4:
        p.drain("vector")
        p.drain("scalar")
        gate3 = [p.tok("v"), p.tok("s"), p.tok("t")]
        if "q.tap" in p.sems:
            gate3.append(p.tok("q.tap"))
        am2 = smallA[0:1, 1675:1676]
        s2row = smallA[0:1, 1676:1677]
        inv2row = smallA[0:1, 1677:1678]
        tok_am2_back = p.dma("gpsimd", am2, cc2o, waits=[tok_cc2], slot="cc2r")
        tt4 = p.op("vector", "reciprocal", s2row, am2, waits=[tok_am2_back] + gate3, inc="v")
        p.drain("vector")
        tt5 = p.op("vector", "tensor_scalar", s2row, s2row, 32767.0, None, op0=ALU.mult, inc="v")
        tt6 = p.op("vector", "tensor_scalar", inv2row, am2, 1.0 / 32767.0, None, op0=ALU.mult,
                   waits=[tok_am2_back], inc="v")
        bq2 = banks.get(1)[0]
        tb1 = p.mm(psum[bq2][:, 0:1], [(ONES, s2row)], waits=[tt5, banks.free[bq2]])
        tb2 = p.mm(psum[bq2][:, 1:2], [(ONES, inv2row)], waits=[tt6])
        tc1 = p.op("vector", "tensor_copy", S2COL, psum[bq2][:, 0:1], waits=[tb1], inc="v")
        tc2 = p.op("vector", "tensor_copy", AMAX2, psum[bq2][:, 1:2], waits=[tb2], inc="v")
        banks.free[bq2] = tc2
        INV2COL = AMAX2
        ROWSC = colsA.ap()[:, 443:475]
        kout_ap = lambda k, q0, q1: bufA_bf[:, k * 1024 + q0: k * 1024 + q1]
        wfa2 = [hstrip_f32[:, 0:1024], hstrip_f32[:, 1024:2048]]
        rmaxc = smallA.ap()[:, 1670:1671]
        prev_kq = [gate3[2], gate3[2]]
        prev_kv = [None, None]
        prev_srd = None
        for k in range(32):
            sl = k % 2
            tld = p.dma("sync", wfa2[sl], w_ffn_out[k * 128:(k + 1) * 128, :], waits=[prev_kq[sl]], slot="wo%d" % sl)
            trm = p.op("vector", "tensor_reduce", rmaxc, wfa2[sl], axis=AX, op=ALU.max,
                       apply_absolute_value=True, waits=[tld, prev_srd] + gate3, inc="v")
            p.drain("vector")
            trm2 = p.op("vector", "tensor_scalar", ROWSC[:, k:k + 1], rmaxc, 1e-8, 1.0 / 127.0,
                        op0=ALU.max, op1=ALU.mult, inc="v")
            p.drain("vector")
            trm3 = p.op("vector", "tensor_scalar", rmaxc, rmaxc, 1e-8, None, op0=ALU.max, inc="v")
            p.drain("vector")
            trm4 = p.op("vector", "reciprocal", rmaxc, rmaxc, inc="v")
            p.drain("vector")
            trm5 = p.op("vector", "tensor_scalar", rmaxc, rmaxc, 127.0, None, op0=ALU.mult, inc="v")
            for q in range(2):
                tq1 = p.op("scalar", "activation", qpre[q], wfa2[sl][:, q * 512:(q + 1) * 512], ACT.Copy,
                           bias=MAGIC, scale=rmaxc, waits=[trm5, prev_kv[q]] + gate3, inc="s")
                prev_srd = tq1
                tq4 = p.op("vector", "tensor_scalar", kout_ap(k, q * 512, (q + 1) * 512), qpre[q],
                           -MAGIC, None, op0=ALU.add, waits=[tq1], inc="v")
                prev_kv[q] = tq4
                prev_kq[sl] = tq4
        p.drain("vector")
        tcmb = p.op("vector", "tensor_scalar", ROWSC, ROWSC, INV2COL, None, op0=ALU.mult, waits=[tc2], inc="v")
        tok_kout = p.tok("v")
        hs_t = lambda k: bufB.ap()[:, k * 512:(k + 1) * 512]
        hstg = [smallB.ap()[:, 0:512], smallB.ap()[:, 512:1024]]
        prev_hs_mm = None
        prev_ffn2 = None
        prev_hs_q3 = [tok_kout, tok_kout]
        for c in range(NCH):
            c0 = c * CH
            hs_toks = []
            for k in range(32):
                sl = k % 2
                tld = p.dma("sync", hstg[sl], sp_hid[k * 128:(k + 1) * 128, c0:c0 + CH],
                            waits=[prev_hs_q3[sl]], slot="hr%d" % sl)
                tq1 = p.op("scalar", "activation", hstg[sl], hstg[sl], ACT.Copy, bias=MAGIC, scale=S2COL,
                           waits=[tld, tc1, prev_hs_mm], inc="s")
                tq3 = p.op("vector", "tensor_scalar", hs_t(k), hstg[sl], -MAGIC, ROWSC[:, k:k + 1],
                           op0=ALU.add, op1=ALU.mult, waits=[tq1, tcmb], inc="v")
                prev_hs_q3[sl] = tq3
                hs_toks.append(tq3)
            for md in range(8):
                bmd = banks.get(1)[0]
                pairs = [(kout_ap(k, md * 128, (md + 1) * 128), hs_t(k)) for k in range(32)]
                tdm = p.mm(psum[bmd][:, 0:CH], pairs, waits=hs_toks + [tok_kout, banks.free[bmd]])
                prev_hs_mm = tdm
                tev = p.op("vector", "tensor_scalar", stg1, psum[bmd][:, 0:CH], BOUT[:, md:md + 1], None,
                           op0=ALU.add, waits=[tdm, tok_mlp_out, prev_ffn2], inc="v")
                banks.free[bmd] = tev
                p.drain("vector")
                tac = p.dma_acc(out[md * 128:(md + 1) * 128, c0:c0 + CH], stg1,
                                waits=[tev, tok_mlp_out], slot="acc")
                prev_ffn2 = tac
        tok_ffn2 = p.tok("g.acc")

    if phases >= 5:
        p.drain("vector")
        p.drain("scalar")
        gate4 = [p.tok("v"), p.tok("s"), p.tok("t")]
        x1sb = bufB_f32
        x1_t = lambda d, a, b: x1sb[:, d * T + a: d * T + b]
        tok_x1 = p.dma("sync",
                       bass.AP(x1sb.tensor, x1sb.offset, [[16384, 128], [2048, 8], [1, 2048]]),
                       out.rearrange("(a p) t -> p a t", p=128), waits=[tok_ffn2] + gate4, slot="x1r")
        tok_xn1 = emit_layernorm(x1_t, xn_t, 32, tok_x1, do_red=False)
        if taps:
            tp_x1 = tap("tp_x1", [D, T])
            p.dma("sync", tp_x1.rearrange("(a p) t -> p a t", p=128),
                  bass.AP(bufB_f32.tensor, bufB_f32.offset, [[16384, 128], [2048, 8], [1, 2048]]),
                  waits=[tok_x1], slot="tap")
            tp_xn1 = tap("tp_xn1", [D, T])
            p.dma("sync", tp_xn1.rearrange("(a p) t -> p a t", p=128),
                  bass.AP(bufA.ap().tensor, 0, [[16384, 128], [2048, 8], [1, 2048]]),
                  waits=[tok_xn1], slot="tap")


    if phases >= 6:
        p.drain("vector")
        p.drain("scalar")
        gate5 = [p.tok("v"), p.tok("s"), p.tok("t")]
        if "q.tap" in p.sems:
            gate5.append(p.tok("q.tap"))
        FC = 256
        xn1b_t = lambda k: bcall.ap()[:, k * 2048:(k + 1) * 2048]
        z_t = lambda d: bufB_f32[:, d * 256:(d + 1) * 256]
        y_t = lambda hg: bufB_f32[:, 2048 + hg * 256: 2048 + (hg + 1) * 256]
        xbcc_t = lambda ct: bufB.ap()[:, 8192 + ct * 256: 8192 + (ct + 1) * 256]
        xbct_t = lambda tt: bufB.ap()[:, 10496 + tt * 1152: 10496 + (tt + 1) * 1152]
        xdt_t = lambda tt: bufB.ap()[:, 12800 + tt * 1024: 12800 + (tt + 1) * 1024]
        xdec_t = lambda tt: bufB.ap()[:, 14848 + tt * 1024: 14848 + (tt + 1) * 1024]
        catn_t = lambda i: bufB.ap()[:, 16896 + i * 256: 16896 + (i + 1) * 256]
        states_ab = [bufB_f32[0:16, 10496:11520], bufB_f32[0:16, 11520:12544]]
        SIa = bufB_f32[0:16, 12544:13568]
        SI2a = bufB_f32[0:16, 13568:14592]
        cm0b = bufB[0:16, 29184:29440]
        bm0b = bufB[0:16, 29440:29696]
        cm0f = bufB_f32[0:16, 14848:15104]
        stage1 = wsl.ap()[:, 3072:3331]
        halo1 = wsl.ap()[:, 3331:3358]
        GmS = wsl.ap()[:, 3358:3870]
        WS = wsl.ap()[:, 3870:4382]
        LamS = wsl.ap()[:, 4382:4894]
        ydsb = wsl[0:16, 4894:5150]
        odsb = wsl.ap()[:, 5150:5406]
        CSELHP = hstrip.ap()[:, 0:1024]
        CINSEL = hstrip.ap()[:, 1024:2048]
        odxb = hstrip[0:64, 2048:2304]
        m1b = smallB[0:64, 0:256]
        adtb = smallB[0:64, 256:512]
        acsb = smallB[0:64, 512:768]
        sqf = [smallB.ap()[:, 1024:1280], smallB.ap()[:, 1280:1536]]
        acst = [smallA.ap()[:, 0:64], smallA.ap()[:, 64:128]]
        nacst = [smallA.ap()[:, 128:192], smallA.ap()[:, 192:256]]
        m1t = [smallA.ap()[:, 256:320], smallA.ap()[:, 320:384]]
        e2sb = smallA[0:16, 384:448]
        dect = [smallA.ap()[:, 448:512], smallA.ap()[:, 512:576]]
        mu2 = smallA[0:1, 576:832]
        v2 = smallA[0:1, 832:1088]
        ID64 = bass.AP(consts.ap().tensor, 128, [[384, 64], [1, 64]])
        onesrep = bass.AP(colsA.ap().tensor, 358, [[512, 64], [0, 256]])

        MASKNEG = hstrip_f32[:, 1152:1664]
        tok_mn = p.dma("sync", MASKNEG, k_maskneg, waits=gate5, slot="fmn")
        tok_shp = p.dma("gpsimd", CSELHP[0:64, :], k_selhp, waits=gate5, slot="fsel")
        tok_ins = p.dma("gpsimd", CINSEL[0:16, :], k_insel, waits=gate5, slot="fsel2")
        xn1b_toks = []
        for k in range(8):
            tkx = p.op("vector", "tensor_copy", xn1b_t(k), xn_t(k, 0, 2048), waits=gate5 + [tok_xn1], inc="v")
            xn1b_toks.append(tkx)
        tsi0 = p.op("vector", "memset", SIa, 0.0, waits=gate5, inc="v")
        th1 = p.op("vector", "memset", halo1, 0.0, waits=gate5, inc="v")
        p.drain("vector")

        slabf = [None, None]
        cdf = [None, None]
        stage1_free = None
        halo1_tok = [th1] * 9
        prev_acc = None
        prev_yd_acc = None
        tok_si = tsi0
        tok_e2 = None
        tok_states_prev = None
        prev_loop_v = None
        prev_loop_t = None
        prev_loop_s = None
        for c in range(8):
            tc0 = c * FC
            states_cur = states_ab[c % 2]
            states_prev = states_ab[(c + 1) % 2]
            if c > 0:
                gexp = bass.AP(smallA.ap().tensor, 384, [[1792, 16], [1, 64], [0, 16]])
                tu = p.op("vector", "tensor_tensor", SI2a, SIa, gexp, op=ALU.mult,
                          waits=[tok_e2, tok_si, tok_states_prev], inc="v")
                p.drain("vector")
                tok_si = p.op("vector", "tensor_tensor", SIa, SI2a, states_prev, op=ALU.add, inc="v")
                p.drain("vector")
            tsg_raw = None
            for m in range(17):
                si = m % 2
                srcap = bass.AP(w1_inT.tensor, m * 128, [[E1P, 128], [E1P * 128, 8], [1, 128]])
                dstap = bass.AP(wsl.ap().tensor, si * 1024, [[6144, 128], [128, 8], [1, 128]])
                twl = p.dma("gpsimd", dstap, srcap, waits=[slabf[si]], slot="fws%d" % si)
                bmm = banks.get(1)[0]
                pairs = [(wslab_blk(si, k), bcall.ap()[:, k * 2048 + tc0: k * 2048 + tc0 + FC]) for k in range(8)]
                tmm = p.mm(psum[bmm][:, 0:FC], pairs, waits=xn1b_toks + [twl, banks.free[bmm]])
                slabf[si] = tmm
                if m < 8:
                    tzv = p.op("vector", "tensor_copy", z_t(m), psum[bmm][:, 0:FC],
                               waits=[tmm, prev_loop_t, prev_loop_v], inc="v")
                    banks.free[bmm] = tzv
                    continue
                ct = m - 8
                if m == 16:
                    tsg_raw = p.op("scalar", "activation", m1b, psum[bmm][0:64, 0:FC], ACT.Sigmoid,
                                   bias=colsA[0:64, 355:356], scale=-1.0,
                                   waits=[tmm, tok_setup_s, tok_setup_v, prev_loop_v, prev_loop_s], inc="s")
                    p.drain("scalar")
                thalo = p.op("vector", "tensor_copy", stage1[:, 0:3], halo1[:, ct * 3:ct * 3 + 3],
                             waits=[halo1_tok[ct], stage1_free], inc="v")
                p.drain("vector")
                tev = p.op("vector", "tensor_copy", stage1[:, 3:259], psum[bmm][:, 0:FC],
                           waits=[tmm, (tsg_raw if m == 16 else None)], inc="v")
                banks.free[bmm] = tev
                p.drain("vector")
                thl = p.op("vector", "tensor_copy", halo1[:, ct * 3:ct * 3 + 3], stage1[:, 256:259], inc="v")
                halo1_tok[ct] = thl
                csrc = bass.AP(cdg1.tensor, ct * 4 * 128 * 128, [[128, 128], [128 * 128, 4], [1, 128]])
                cdst = bass.AP(wsl.ap().tensor, 2048 + si * 512, [[6144, 128], [128, 4], [1, 128]])
                tcd = p.dma("gpsimd", cdst, csrc, waits=[cdf[si]], slot="fcd%d" % si)
                bcv = banks.get(1)[0]
                cpairs = [(bass.AP(wsl.ap().tensor, 2048 + si * 512 + kk * 128, [[6144, 128], [1, 128]]),
                           stage1[:, kk:kk + 256]) for kk in range(4)]
                tcv = p.mm(psum[bcv][:, 0:FC], cpairs, waits=[tcd, tev, thl, thalo, banks.free[bcv]])
                cdf[si] = tcv
                stage1_free = tcv
                txb = p.op("scalar", "activation", xbcc_t(ct), psum[bcv][:, 0:FC], ACT.Silu,
                           bias=CONVB1[:, ct:ct + 1], scale=1.0,
                           waits=[tcv, prev_loop_v, prev_loop_t], inc="s")
                banks.free[bcv] = txb
            tok_xbc = p.tok("s")
            tm1 = p.op("scalar", "activation", m1b, m1b, ACT.Ln, bias=EPSG[0:64, 0:1], scale=1.0, inc="s")
            p.drain("scalar")
            tadt = p.op("vector", "tensor_scalar", adtb, m1b, colsA[0:64, 356:357], None, op0=ALU.mult,
                        waits=[tm1, prev_loop_v], inc="v")
            p.drain("vector")
            tacs = p.op("vector", "tensor_tensor_scan", acsb, onesrep, adtb, 0.0,
                        op0=ALU.mult, op1=ALU.add, waits=[tok_consts], inc="v")
            p.drain("vector")
            tacst = []
            for tt in range(2):
                btt = banks.get(1)[0]
                t_tr = p.op("tensor", "transpose", psum[btt][:, 0:64], smallB[0:64, 512 + tt * 128: 512 + (tt + 1) * 128],
                            ID64, waits=[tacs, banks.free[btt]], inc="t")
                t_e1 = p.op("vector", "tensor_copy", acst[tt], psum[btt][:, 0:64], waits=[t_tr, prev_loop_v], inc="v")
                t_e2 = p.op("vector", "tensor_scalar", nacst[tt], psum[btt][:, 0:64], -1.0, None, op0=ALU.mult,
                            waits=[t_tr], inc="v")
                banks.free[btt] = t_e2
                tacst.append((t_e1, t_e2))
                bt2 = banks.get(1)[0]
                t_tr2 = p.op("tensor", "transpose", psum[bt2][:, 0:64], smallB[0:64, tt * 128:(tt + 1) * 128],
                             ID64, waits=[tm1, tadt, banks.free[bt2]], inc="t")
                t_e3 = p.op("vector", "tensor_copy", m1t[tt], psum[bt2][:, 0:64], waits=[t_tr2], inc="v")
                banks.free[bt2] = t_e3
                tacst.append((t_e3, t_e3))
            tod = []
            for tt in range(2):
                bto = banks.get(1)[0]
                t_tr3 = p.op("tensor", "transpose", psum[bto][0:64, 0:128], acst[tt], IDENT,
                             waits=[tacst[2 * tt][0], banks.free[bto]], inc="t")
                t_oe = p.op("scalar", "activation", odxb[:, tt * 128:(tt + 1) * 128], psum[bto][0:64, 0:128],
                            ACT.Exp, bias=0.0, scale=1.0, waits=[t_tr3, prev_loop_s, prev_loop_t], inc="s")
                banks.free[bto] = t_oe
                tod.append(t_oe)
            sel127 = bass.AP(consts.ap().tensor, 128 + 127, [[384, 128], [0, 128]])
            tcsl = p.op("tensor", "matmul", psum[7][:, 0:64], sel127, acst[1], start=True, stop=True,
                        waits=[tacst[2][0] if False else tacst[2 * 1][0], tok_consts, prev_loop_v], inc="t")
            tdec = []
            for tt in range(2):
                t_d1 = p.op("vector", "tensor_tensor", dect[tt], psum[7][:, 0:64], acst[tt], op=ALU.subtract,
                            waits=[tcsl, tacst[2 * tt][0], prev_loop_s], inc="v")
                p.drain("vector")
                t_d2 = p.op("scalar", "activation", dect[tt], dect[tt], ACT.Exp, bias=0.0, scale=1.0,
                            waits=[t_d1], inc="s")
                p.drain("scalar")
                tdec.append(t_d2)
            tok_e2 = p.op("scalar", "activation", e2sb, psum[7][0:16, 0:64], ACT.Exp, bias=0.0, scale=1.0,
                          waits=[tcsl, tok_si], inc="s")
            tbm0 = p.dma("gpsimd", bm0b, bufB[64:80, 8192 + 8 * 256: 8192 + 9 * 256], waits=[tok_xbc, prev_loop_t], slot="fbm")
            tcm0 = p.dma("gpsimd", cm0b, bufB[80:96, 8192 + 8 * 256: 8192 + 9 * 256], waits=[tok_xbc, prev_loop_t], slot="fcm")
            tcm0f = p.dma("gpsimd", cm0f, bufB[80:96, 8192 + 8 * 256: 8192 + 9 * 256], waits=[tok_xbc], slot="fcm2")
            txbt_toks = []
            for ct in range(9):
                for tt in range(2):
                    btx = banks.get(1)[0]
                    pbf = psum[btx].ap().bitcast(BF16)
                    t_tr = p.op("tensor", "transpose", pbf[:, 0:128],
                                bufB[0:128, 8192 + ct * 256 + tt * 128: 8192 + ct * 256 + (tt + 1) * 128],
                                IDENTB, waits=[tok_xbc, banks.free[btx], tok_constb], inc="t")
                    t_ev = p.op("vector", "tensor_copy", bufB.ap()[:, 10496 + tt * 1152 + ct * 128: 10496 + tt * 1152 + (ct + 1) * 128],
                                pbf[:, 0:128], waits=[t_tr, prev_loop_t], inc="v")
                    banks.free[btx] = t_ev
                    txbt_toks.append(t_ev)
            txd = []
            for tt in range(2):
                m1exp = bass.AP(smallA.ap().tensor, 256 + tt * 64, [[1792, 128], [1, 64], [0, 16]])
                t_x1 = p.op("vector", "scalar_tensor_tensor", xdt_t(tt),
                            bufB.ap()[:, 10496 + tt * 1152: 10496 + tt * 1152 + 1024], -1.0, m1exp,
                            op0=ALU.mult, op1=ALU.mult, waits=txbt_toks + [tacst[2 * tt + 1][0], prev_loop_t], inc="v")
                p.drain("vector")
                dexp = bass.AP(smallA.ap().tensor, 448 + tt * 64, [[1792, 128], [1, 64], [0, 16]])
                t_x2 = p.op("vector", "tensor_tensor", xdec_t(tt), xdt_t(tt), dexp, op=ALU.mult,
                            waits=[tdec[tt]], inc="v")
                p.drain("vector")
                txd.append((t_x1, t_x2))
            st_toks = []
            for hg in range(8):
                bst = banks.get(1)[0]
                pairs = [(bufB.ap()[:, 10496 + tt * 1152 + 1088: 10496 + tt * 1152 + 1104],
                          bufB.ap()[:, 14848 + tt * 1024 + hg * 128: 14848 + tt * 1024 + (hg + 1) * 128])
                         for tt in range(2)]
                tsm = p.mm(psum[bst][0:16, 0:128], pairs, waits=[txd[0][1], txd[1][1]] + txbt_toks + [banks.free[bst]])
                sc0 = (10496 if c % 2 == 0 else 11520) + hg * 128
                tse = p.op("vector", "tensor_copy", bufB_f32[0:16, sc0:sc0 + 128],
                           psum[bst][0:16, 0:128], waits=[tsm, tok_si], inc="v")
                banks.free[bst] = tse
                st_toks.append(tse)
            tok_states_prev = st_toks[-1]
            tG = None
            for sh in range(2):
                tG = p.op("tensor", "matmul", psum[7][:, sh * 256:(sh + 1) * 256],
                          bm0b[0:16, sh * 128:(sh + 1) * 128],
                          cm0b, start=True, stop=True,
                          waits=[tbm0, tcm0, (tcsl if sh == 0 else None), p.tok("s") if sh == 0 else None], inc="t")
            tGm = p.op("vector", "tensor_tensor", GmS, psum[7][:, 0:512], MASKTRI, op=ALU.mult,
                       waits=[tG, tok_constb, prev_loop_s], inc="v")
            y_toks = []
            prev_w = None
            prev_lam = None
            prev_od = None
            prev_yd = None
            for hg in range(8):
                byo = banks.get(1)[0]
                si_st = bass.AP(bufB_f32.tensor, bufB_f32.offset + 12544 + hg * 128, [[16384, 16], [1, 128]])
                tyo = p.op("tensor", "matmul", psum[byo][:, 0:FC], si_st, cm0f, start=True, stop=True,
                           waits=[tok_si, tcm0f, banks.free[byo]], inc="t")
                bod = banks.get(1)[0]
                tob = p.op("tensor", "matmul", psum[bod][:, 0:FC], CSELHP[0:64, hg * 128:(hg + 1) * 128],
                           odxb, start=True, stop=True, waits=[tod[0], tod[1], tok_shp, banks.free[bod]], inc="t")
                tos = p.op("scalar", "activation", odsb, psum[bod][:, 0:FC], ACT.Copy, bias=0.0, scale=1.0,
                           waits=[tob, prev_od], inc="s")
                banks.free[bod] = tos
                prev_od = None
                tyv = p.op("vector", "tensor_tensor", y_t(hg), psum[byo][:, 0:FC], odsb, op=ALU.mult,
                           waits=[tyo, tos, prev_loop_t], inc="v")
                banks.free[byo] = tyv
                prev_od = tyv
                tacc = None
                for h8 in range(8):
                    h = hg * 8 + h8
                    blam = banks.get(1)[0]
                    selh = bass.AP(consts.ap().tensor, 128 + h, [[384, 64], [0, 128]])
                    acsap = bass.AP(smallB.ap().tensor, 512, [[1664, 64], [1, 256]])
                    p.op("tensor", "matmul", psum[blam][:, 0:256], selh, acsap, start=True, stop=False,
                         waits=[tacs, tok_consts, tok_mn, banks.free[blam]])
                    p.op("tensor", "matmul", psum[blam][:, 256:512], selh, acsap, start=False, stop=False)
                    tlb = p.op("tensor", "matmul", psum[blam][:, 0:512], IDENT, MASKNEG, start=False, stop=True,
                               inc="t")
                    tl1 = p.op("scalar", "activation", LamS[:, 0:256], psum[blam][:, 0:256], ACT.Exp,
                               bias=nacst[0][:, h:h + 1], scale=1.0,
                               waits=[tlb, tacst[1][1], prev_w, prev_lam], inc="s")
                    tl2 = p.op("scalar", "activation", LamS[:, 256:512], psum[blam][:, 256:512], ACT.Exp,
                               bias=nacst[1][:, h:h + 1], scale=1.0, waits=[tacst[3][1] if False else None], inc="s")
                    banks.free[blam] = tl2
                    prev_lam = tl2
                    tw = p.op("vector", "scalar_tensor_tensor", WS, LamS, 3.0e38, GmS,
                              op0=ALU.min, op1=ALU.mult, waits=[tl1, tl2, tGm, prev_w], inc="v")
                    byd = banks.get(1)[0]
                    ydpairs = [(bufB.ap()[:, 12800 + tt * 1024 + h * 16: 12800 + tt * 1024 + (h + 1) * 16],
                                WS[:, tt * 256:(tt + 1) * 256]) for tt in range(2)]
                    tyd = p.mm(psum[byd][0:16, 0:FC], ydpairs, waits=[tw, banks.free[byd]])
                    prev_w = tyd
                    tye = p.op("vector", "tensor_copy", ydsb, psum[byd][0:16, 0:FC],
                               waits=[tyd, prev_yd], inc="v")
                    banks.free[byd] = tye
                    tacc = p.op("tensor", "matmul", psum[6][:, 0:FC], CINSEL[0:16, h8 * 128:(h8 + 1) * 128],
                                ydsb, start=(h8 == 0), stop=(h8 == 7),
                                waits=[tye, tok_ins] + ([prev_yd_acc] if h8 == 0 else []), inc="t")
                    prev_yd = tacc
                ty2 = p.op("vector", "tensor_tensor", y_t(hg), y_t(hg), psum[6][:, 0:FC], op=ALU.add,
                           waits=[tacc, tyv], inc="v")
                prev_yd_acc = ty2
                p.drain("vector")
                ty3 = p.op("vector", "scalar_tensor_tensor", y_t(hg), xbcc_t(hg), DSK1[:, hg:hg + 1],
                           y_t(hg), op0=ALU.mult, op1=ALU.add, waits=[tok_xbc, tok_setup_v], inc="v")
                p.drain("vector")
                y_toks.append(ty3)
            tiles = [("y", i) for i in range(8)] + [("z", i) for i in range(8)]
            bmu2, bsq3 = banks.get(2)
            mu_mm = None
            sq_mm2 = [None, None]
            for i, (kind, d) in enumerate(tiles):
                vt = y_t(d) if kind == "y" else z_t(d)
                wv = y_toks + [prev_loop_v]
                mu_mm = p.op("tensor", "matmul", psum[bmu2][0:1, 0:FC], R2048, vt,
                             start=(i == 0), stop=(i == 15),
                             waits=wv + ([banks.free[bmu2]] if i == 0 else []), inc="t")
                tqs = p.op("scalar", "activation", sqf[i % 2], vt, ACT.Square, bias=0.0, scale=1.0,
                           waits=wv + [sq_mm2[i % 2]], inc="s")
                sq_mm2[i % 2] = p.op("tensor", "matmul", psum[bsq3][0:1, 0:FC], R2048, sqf[i % 2],
                                     start=(i == 0), stop=(i == 15),
                                     waits=[tqs] + ([banks.free[bsq3]] if i == 0 else []), inc="t")
            tmu3 = p.op("vector", "tensor_copy", mu2, psum[bmu2][0:1, 0:FC], waits=[mu_mm], inc="v")
            banks.free[bmu2] = tmu3
            p.drain("vector")
            tm23 = p.op("vector", "tensor_tensor", v2, mu2, mu2, op=ALU.mult, inc="v")
            p.drain("vector")
            tvar3 = p.op("vector", "tensor_tensor", v2, psum[bsq3][0:1, 0:FC], v2, op=ALU.subtract,
                         waits=[sq_mm2[1]], inc="v")
            banks.free[bsq3] = tvar3
            ts13 = p.op("scalar", "activation", v2, v2, ACT.Sqrt, bias=EPS5[0:1, 0:1], scale=1.0,
                        waits=[tvar3], inc="s")
            tr13 = p.op("vector", "reciprocal", v2, v2, waits=[ts13], inc="v")
            p.drain("vector")
            brb3, bab3 = banks.get(2)
            trb3 = p.mm(psum[brb3][:, 0:FC], [(ONES, v2)], waits=[tr13, banks.free[brb3]])
            tab3 = p.mm(psum[bab3][:, 0:FC], [(ONES, mu2)], waits=[tmu3, tr13, banks.free[bab3]])
            catn_toks = []
            prev_cn = None
            for i, (kind, d) in enumerate(tiles):
                vt = y_t(d) if kind == "y" else z_t(d)
                t1c = p.op("vector", "tensor_tensor", sqf[i % 2], vt, psum[bab3][:, 0:FC], op=ALU.subtract,
                           waits=[trb3, tab3, p.tok("t"), catn_toks[i - 2] if i >= 2 else None], inc="v")
                p.drain("vector")
                t2c = p.op("vector", "tensor_tensor", sqf[i % 2], sqf[i % 2], psum[brb3][:, 0:FC], op=ALU.mult,
                           waits=[tab3], inc="v")
                p.drain("vector")
                p.drain("vector")
                t3c = p.op("vector", "tensor_scalar", catn_t(i), sqf[i % 2], LNW1[:, i:i + 1], LNB1[:, i:i + 1],
                           op0=ALU.mult, op1=ALU.add, waits=[prev_loop_t, prev_cn], inc="v")
                catn_toks.append(t3c)
                prev_cn = t3c
            banks.free[brb3] = catn_toks[-1]
            banks.free[bab3] = catn_toks[-1]
            for m in range(8):
                tw1 = p.dma("gpsimd", bass.AP(wsl.ap().tensor, 0, [[6144, 128], [128, 8], [1, 128]]),
                            bass.AP(w1_outT.tensor, m * 128, [[1024, 128], [1024 * 128, 8], [1, 128]]),
                            waits=[slabf[0]], slot="fws0")
                tw2 = p.dma("gpsimd", bass.AP(wsl.ap().tensor, 1024, [[6144, 128], [128, 8], [1, 128]]),
                            bass.AP(w1_outT.tensor, 8 * 128 * 1024 + m * 128, [[1024, 128], [1024 * 128, 8], [1, 128]]),
                            waits=[slabf[1]], slot="fws1")
                bwo = banks.get(1)[0]
                pairs = [(wslab_blk(0, k), catn_t(k)) for k in range(8)] + \
                        [(wslab_blk(1, k), catn_t(8 + k)) for k in range(8)]
                tmo = p.mm(psum[bwo][:, 0:FC], pairs, waits=catn_toks + [tw1, tw2, banks.free[bwo]])
                slabf[0] = tmo
                slabf[1] = tmo
                tev2 = p.op("vector", "tensor_copy", stg1[:, 0:FC], psum[bwo][:, 0:FC],
                            waits=[tmo, prev_acc], inc="v")
                banks.free[bwo] = tev2
                tac2 = p.dma_acc(out[m * 128:(m + 1) * 128, tc0:tc0 + FC], stg1[:, 0:FC],
                                 waits=[tev2, tok_ffn2], slot="acc")
                prev_acc = tac2
            prev_loop_v = p.tok("v")
            prev_loop_t = p.tok("t")
            prev_loop_s = p.tok("s")
        tok_attn1 = p.tok("g.acc")

    if phases >= 7:
        p.drain("vector")
        p.drain("scalar")
        gate6 = [p.tok("v"), p.tok("s"), p.tok("t")]
        xr_t = lambda d: bcall.ap()[:, d * 512:(d + 1) * 512]
        xk_t = lambda d: bcall.ap()[:, 4096 + d * 512: 4096 + (d + 1) * 512]
        ka_t = lambda d: bcall.ap()[:, 8192 + d * 512: 8192 + (d + 1) * 512]
        r_t = lambda d: bufB_f32[:, d * 512:(d + 1) * 512]
        gtmp = smallB.ap()[:, 1024:1536]
        prev_g_acc = None
        prev_gmm = None
        prev_ka = None
        for c in range(NCH):
            c0 = c * CH
            mix_toks = []
            for d in range(8):
                for which, MU, OMU, dstf in ((0, MUR, ONEMMUR, xr_t), (1, MUK, ONEMMUK, xk_t)):
                    t1 = p.op("vector", "tensor_scalar", gtmp, xn_t(d, c0, c0 + CH), MU[:, d:d + 1], None,
                              op0=ALU.mult, waits=gate6 + [tok_xn1, tok_setup_v, prev_gmm], inc="v")
                    p.drain("vector")
                    if c == 0:
                        t2 = p.op("vector", "scalar_tensor_tensor", dstf(d)[:, 1:CH],
                                  xn_t(d, 0, CH - 1), OMU[:, d:d + 1], gtmp[:, 1:CH],
                                  op0=ALU.mult, op1=ALU.add, inc="v")
                        p.drain("vector")
                        t3 = p.op("vector", "tensor_copy", dstf(d)[:, 0:1], gtmp[:, 0:1], inc="v")
                        mix_toks.append(t3)
                    else:
                        t2 = p.op("vector", "scalar_tensor_tensor", dstf(d),
                                  xn_t(d, c0 - 1, c0 + CH - 1), OMU[:, d:d + 1], gtmp,
                                  op0=ALU.mult, op1=ALU.add, inc="v")
                        mix_toks.append(t2)
                    p.drain("vector")
            rk_toks = []
            for which, wt, name in ((0, w1_rT, "r"), (1, w1_kT, "k")):
                for m in range(8):
                    si = m % 2
                    twl = p.dma("gpsimd", bass.AP(wsl.ap().tensor, si * 1024, [[6144, 128], [128, 8], [1, 128]]),
                                bass.AP(wt.tensor, m * 128, [[1024, 128], [1024 * 128, 8], [1, 128]]),
                                waits=[slabf[si]], slot="fws%d" % si)
                    bmg = banks.get(1)[0]
                    src_t = xr_t if which == 0 else xk_t
                    pairs = [(wslab_blk(si, k), src_t(k)) for k in range(8)]
                    tmm = p.mm(psum[bmg][:, 0:CH], pairs, waits=mix_toks + [twl, banks.free[bmg]])
                    slabf[si] = tmm
                    prev_gmm = tmm
                    if which == 0:
                        te = p.op("scalar", "activation", r_t(m), psum[bmg][:, 0:CH], ACT.Sigmoid,
                                  bias=0.0, scale=1.0, waits=[tmm, p.tok("v")], inc="s")
                        banks.free[bmg] = te
                    else:
                        te1 = p.op("vector", "tensor_relu", gtmp, psum[bmg][:, 0:CH],
                                   waits=[tmm, p.tok("v"), prev_ka], inc="v")
                        banks.free[bmg] = te1
                        te = p.op("scalar", "activation", ka_t(m), gtmp, ACT.Square, bias=0.0, scale=1.0,
                                  waits=[te1, p.tok("s")], inc="s")
                        prev_ka = te
                        prev_gmm = te
                    rk_toks.append(te)
            for m in range(8):
                si = m % 2
                twl = p.dma("gpsimd", bass.AP(wsl.ap().tensor, si * 1024, [[6144, 128], [128, 8], [1, 128]]),
                            bass.AP(w1_vT.tensor, m * 128, [[1024, 128], [1024 * 128, 8], [1, 128]]),
                            waits=[slabf[si]], slot="fws%d" % si)
                bmv = banks.get(1)[0]
                pairs = [(wslab_blk(si, k), ka_t(k)) for k in range(8)]
                tmm = p.mm(psum[bmv][:, 0:CH], pairs, waits=rk_toks + [twl, banks.free[bmv]])
                slabf[si] = tmm
                prev_gmm = tmm
                tev = p.op("vector", "tensor_tensor", stg1[:, 0:CH], psum[bmv][:, 0:CH], r_t(m),
                           op=ALU.mult, waits=[tmm, prev_g_acc, prev_acc, p.tok("s")], inc="v")
                banks.free[bmv] = tev
                tac = p.dma_acc(out[m * 128:(m + 1) * 128, c0:c0 + CH], stg1[:, 0:CH],
                                waits=[tev], slot="acc")
                prev_g_acc = tac

    final_waits = [(name, p.sems[name][1]) for name in list(p.sems.keys())]
    p.emit("sync", lambda e: e.nop(), waits=final_waits)

    @blk.sync
    def _(e):
        for f in p.q["sync"]:
            f(e)

    @blk.tensor
    def _(e):
        for f in p.q["tensor"]:
            f(e)

    @blk.vector
    def _(e):
        for f in p.q["vector"]:
            f(e)

    @blk.scalar
    def _(e):
        for f in p.q["scalar"]:
            f(e)

    @blk.gpsimd
    def _(e):
        for f in p.q["gpsimd"]:
            f(e)

    es.close()
    return tap_list


def _host_inputs(inputs, cores):
    x = np.asarray(inputs["L0_attn_hidden_states"])
    g = {}
    g["w_inproj"] = np.ascontiguousarray(inputs["L0_attn_in_proj_w"])
    cw = np.asarray(inputs["L0_attn_conv_w"])
    cd0 = np.zeros((16, 4, 128, 128), dtype=np.float32)
    idx = np.arange(128)
    for dt in range(16):
        for k in range(4):
            cd0[dt, k, idx, idx] = cw[dt * 128 + idx, k]
    g["cdg0"] = cd0.reshape(64, 128, 128)
    g["w_xproj"] = np.ascontiguousarray(inputs["L0_attn_x_proj_w"])
    g["w_dtproj"] = np.ascontiguousarray(inputs["L0_attn_dt_proj_w"])
    g["w_outproj"] = np.ascontiguousarray(inputs["L0_attn_out_proj_w"])
    g["w_gate"] = np.ascontiguousarray(inputs["L0_attn_gate_w"])
    g["w_up"] = np.ascontiguousarray(inputs["L0_attn_up_w"])
    g["w_down"] = np.ascontiguousarray(inputs["L0_attn_down_w"])
    g["w_ffn_in"] = np.ascontiguousarray(inputs["L0_ffn_W_in"])
    g["w_ffn_out"] = np.ascontiguousarray(inputs["L0_ffn_W_out"])
    w1in = np.asarray(inputs["L1_attn_W_in"])
    w1inT = np.zeros((D, E1P), dtype=np.float32)
    w1inT[:, :2048] = w1in[:2048].T
    w1inT[:, 2048:2112] = w1in[2080:2144].T
    w1inT[:, 2112:2144] = w1in[2048:2080].T
    g["w1_inT"] = w1inT
    cw1 = np.asarray(inputs["L1_attn_conv_w"]).reshape(CONVD, 4)
    cd1 = np.zeros((9, 4, 128, 128), dtype=np.float32)
    for ct in range(8):
        for k in range(4):
            rows = np.arange(128)
            cd1[ct, k, rows, rows] = cw1[ct * 128 + rows, k]
    for k in range(4):
        rows = np.arange(64, 96)
        cd1[8, k, rows, rows] = cw1[1024 + rows - 64, k]
    g["cdg1"] = cd1.reshape(36, 128, 128)
    g["w1_outT"] = np.ascontiguousarray(np.asarray(inputs["L1_attn_W_out"]).T)
    g["w1_rT"] = np.ascontiguousarray(np.asarray(inputs["L1_ffn_W_r"]).T)
    g["w1_kT"] = np.ascontiguousarray(np.asarray(inputs["L1_ffn_W_k"]).T)
    g["w1_vT"] = np.ascontiguousarray(np.asarray(inputs["L1_ffn_W_v"]).T)
    g["c_dtlnw"] = np.asarray(inputs["L0_attn_dt_ln_w"]).reshape(64, 1)
    g["c_blnw"] = np.asarray(inputs["L0_attn_b_ln_w"]).reshape(16, 1)
    g["c_bclnw"] = np.concatenate([np.asarray(inputs["L0_attn_b_ln_w"]), np.asarray(inputs["L0_attn_c_ln_w"])]).reshape(32, 1)
    g["c_clnw"] = np.asarray(inputs["L0_attn_c_ln_w"]).reshape(16, 1)
    g["c_dtprojb"] = np.ascontiguousarray(np.asarray(inputs["L0_attn_dt_proj_b"]).reshape(16, 128).T)
    alog = np.asarray(inputs["L0_attn_A_log"])
    asc = np.zeros((128, 256), dtype=np.float32)
    for n in range(16):
        for dt in range(16):
            asc[:, n * 16 + dt] = alog[dt * 128: (dt + 1) * 128, n]
    g["c_asc"] = asc
    g["c_dsk"] = np.ascontiguousarray(np.asarray(inputs["L0_attn_D_skip"]).reshape(16, 128).T)
    g["c_convb0"] = np.ascontiguousarray(np.asarray(inputs["L0_attn_conv_b"]).reshape(16, 128).T)
    g["c_premoe"] = np.ascontiguousarray(np.asarray(inputs["L0_attn_pre_moe_ln_w"]).reshape(8, 128).T)
    g["c_bin"] = np.ascontiguousarray(np.asarray(inputs["L0_ffn_b_in"]).reshape(32, 128).T)
    g["c_bout"] = np.ascontiguousarray(np.asarray(inputs["L0_ffn_b_out"]).reshape(8, 128).T)
    g["c_dtbias1"] = np.asarray(inputs["L1_attn_dt_bias"]).reshape(64, 1)
    g["c_alog1"] = np.asarray(inputs["L1_attn_A_log"]).reshape(64, 1)
    g["c_dsk1"] = np.ascontiguousarray(np.repeat(np.asarray(inputs["L1_attn_Dskip"]), 16).reshape(8, 128).T)
    g["c_lnw1"] = np.ascontiguousarray(np.asarray(inputs["L1_attn_ln_w"]).reshape(16, 128).T)
    g["c_lnb1"] = np.ascontiguousarray(np.asarray(inputs["L1_attn_ln_b"]).reshape(16, 128).T)
    g["c_mur"] = np.ascontiguousarray(np.asarray(inputs["L1_ffn_mu_r"]).reshape(8, 128).T)
    g["c_muk"] = np.ascontiguousarray(np.asarray(inputs["L1_ffn_mu_k"]).reshape(8, 128).T)
    cb1 = np.zeros((CONVDP,), dtype=np.float32)
    cb1[:1024] = np.asarray(inputs["L1_attn_conv_b"])[:1024]
    cb1[8 * 128 + 64: 8 * 128 + 96] = np.asarray(inputs["L1_attn_conv_b"])[1024:1056]
    g["c_convb1"] = np.ascontiguousarray(cb1.reshape(9, 128).T)
    g["k_ones"] = np.ones((1, 128), dtype=np.float32)
    g["k_mones"] = -np.ones((1, 128), dtype=np.float32)
    g["k_ident"] = np.eye(128, dtype=np.float32)
    c3 = np.zeros((96, 3), dtype=np.float32)
    c3[0:64, 0] = 1.0 / 64.0
    c3[64:80, 1] = 1.0 / 16.0
    c3[80:96, 2] = 1.0 / 16.0
    g["k_c3"] = c3
    b3 = np.zeros((3, 96), dtype=np.float32)
    b3[0, 0:64] = 1.0
    b3[1, 64:80] = 1.0
    b3[2, 80:96] = 1.0
    g["k_b3"] = b3
    sel8 = np.zeros((8, 128), dtype=np.float32)
    for h in range(8):
        sel8[h, h * 16:(h + 1) * 16] = 1.0
    g["k_sel8"] = sel8
    mt = np.zeros((128, 512), dtype=np.float32)
    s = np.arange(128)[:, None]
    l = np.arange(256)[None, :]
    mt[:, 0:256] = (s <= l).astype(np.float32)
    mt[:, 256:512] = ((s + 128) <= l).astype(np.float32)
    g["k_masktri"] = mt
    g["k_maskneg"] = (1.0 - mt) * (-1.0e9)
    g["k_r1024"] = np.full((128, 1), 1.0 / 1024.0, dtype=np.float32)
    g["k_r2048"] = np.full((128, 1), 1.0 / 2048.0, dtype=np.float32)
    g["k_onescol"] = np.ones((128, 1), dtype=np.float32)
    g["k_eps"] = np.tile(np.array([[1e-5, 1e-6, 1e-38]], dtype=np.float32), (128, 1))
    selhp = np.zeros((64, 1024), dtype=np.float32)
    for hg in range(8):
        for q in range(128):
            selhp[hg * 8 + q // 16, hg * 128 + q] = 1.0
    g["k_selhp"] = selhp
    insel = np.zeros((16, 1024), dtype=np.float32)
    for hp in range(8):
        for r in range(16):
            insel[r, hp * 128 + 16 * hp + r] = 1.0
    g["k_insel"] = insel
    for k in g:
        g[k] = np.ascontiguousarray(np.asarray(g[k], dtype=np.float32))

    in_maps = []
    for i in range(cores):
        m = dict(g)
        m["x0t"] = np.ascontiguousarray(np.asarray(x[i]).T.astype(np.float32))
        in_maps.append(m)
    return in_maps


def kernel(**inputs):
    in_maps = _host_inputs(inputs, N_CORES)
    nc = bass.Bass("TRN2", target_bir_lowering=False)
    build(nc, num_cores=N_CORES, phases=PHASES, taps=TAPS)
    res = run_bass_kernel_spmd(nc, in_maps, core_ids=list(range(N_CORES)))
    out = np.stack([np.asarray(r["out"]).T for r in res.results], axis=0)
    return out.astype(np.float32)


# version 08
# speedup vs baseline: 1.0128x; 1.0003x over previous
import numpy as np
from contextlib import ExitStack

import concourse.bass as bass
import concourse.mybir as mybir
from concourse.bass_utils import run_bass_kernel_spmd

F32 = mybir.dt.float32
BF16 = mybir.dt.bfloat16
ALU = mybir.AluOpType
ACT = mybir.ActivationFunctionType
AX = mybir.AxisListType.X

T = 2048
D = 1024
DIN = 2048
NST = 16
DTR = 64
DFF = 4096
CH = 512
NCH = T // CH
E1 = 2144
E1P = 2176
CONVD = 1056
CONVDP = 1152
NH = 64
HP = 16
SSDCH = 256

MAGIC = float(1.5 * 2.0 ** 23)

N_CORES = 8
ENGINES = ("sync", "tensor", "vector", "scalar", "gpsimd")

PHASES = 7
TAPS = False


class PB:

    def __init__(self, nc, es):
        self.nc = nc
        self.es = es
        self.q = {k: [] for k in ENGINES}
        self.sems = {}
        for name in ("t", "v", "s", "cc"):
            self.sems[name] = [es.enter_context(nc.semaphore(name)), 0]
        self.waited = {k: {} for k in ENGINES}
        self.psmap, self.bank_w, self.bank_r = {}, {}, {}

    def _slot(self, name):
        if name not in self.sems:
            self.sems[name] = [self.es.enter_context(self.nc.semaphore(name)), 0]
        return name

    def tok(self, sem):
        return (sem, self.sems[sem][1])

    def emit(self, eng, fn, waits=(), inc=None, n=1):
        ws = []
        for w in waits:
            if w is None:
                continue
            sname, v = w
            if v <= 0:
                continue
            if v > self.waited[eng].get(sname, 0):
                self.waited[eng][sname] = v
                ws.append((sname, v))
        if inc is not None:
            self.sems[inc][1] += n

        def g(e, ws=tuple(ws), fn=fn, inc=inc, n=n):
            for sname, v in ws:
                e.wait_ge(self.sems[sname][0], v)
            r = fn(e)
            if inc is not None:
                r.then_inc(self.sems[inc][0], n)

        self.q[eng].append(g)
        return (inc, self.sems[inc][1]) if inc is not None else None

    def dma(self, q, out, in_, waits=(), slot="bulk"):
        sem = self._slot(("q." if q == "sync" else "g.") + slot)
        return self.emit(q, lambda e, o=out, i=in_: e.dma_start(out=o, in_=i), waits, inc=sem, n=16)

    def dma_acc(self, out, in_, waits=(), slot="acc"):
        sem = self._slot("g." + slot)
        return self.emit("gpsimd", lambda e, o=out, i=in_: e.dma_start(out=o, in_=i, accum_op=ALU.add),
                         waits, inc=sem, n=16)

    def drain(self, eng):
        self.q[eng].append(lambda e: e.drain())

    DEF_SEM = {"tensor": "t", "vector": "v", "scalar": "s"}

    def set_psum(self, psums):
        self.psmap = {g.ap().tensor.name: i for i, g in enumerate(psums)}
        self.bank_w = {}
        self.bank_r = {}

    def _banks_of(self, obj):
        if isinstance(obj, bass.AP):
            n = obj.tensor.name
            if n in self.psmap:
                return (self.psmap[n],)
        return ()

    def _psum_scan(self, args, kw):
        wr, rd = set(), set()
        for i, a in enumerate(args):
            (wr if i == 0 else rd).update(self._banks_of(a))
        for a in kw.values():
            rd.update(self._banks_of(a))
        return wr, rd

    def _psum_pre(self, eng, wr, rd):
        extra = []
        for b in rd | wr:
            w = self.bank_w.get(b)
            if w is not None and w[1] != eng:
                extra.append(w[0])
        for b in wr:
            for tk, e2 in self.bank_r.get(b, ()):
                if e2 != eng:
                    extra.append(tk)
        return extra

    def _psum_post(self, eng, wr, rd, tok):
        for b in wr:
            self.bank_w[b] = (tok, eng)
            self.bank_r[b] = []
        for b in rd:
            self.bank_r.setdefault(b, []).append((tok, eng))

    def op(self, eng, method, *args, waits=(), inc=None, n=1, **kw):
        wr, rd = self._psum_scan(args, kw)
        if (wr or rd) and inc is None:
            assert eng in self.DEF_SEM, (eng, method)
            inc = self.DEF_SEM[eng]
        extra = self._psum_pre(eng, wr, rd)

        def fn(e, method=method, args=args, kw=kw):
            return getattr(e, method)(*args, **kw)
        tok = self.emit(eng, fn, waits=tuple(waits) + tuple(extra), inc=inc, n=n)
        if wr or rd:
            self._psum_post(eng, wr, rd, tok)
        return tok

    def mm(self, out_ap, pairs, waits=(), inc="t", start=True, stop=True):
        tok = None
        npairs = len(pairs)
        for i, (st, mv) in enumerate(pairs):
            last = i == npairs - 1
            tok = self.op("tensor", "matmul", out_ap, st, mv, start=(start and i == 0), stop=(stop and last),
                          waits=(waits if i == 0 else ()), inc=(inc if last else None))
        return tok


class Banks:
    def __init__(self, tensors):
        self.t = list(tensors)
        self.free = [None] * len(tensors)
        self.rr = 0

    def get(self, k=1):
        out = []
        for _ in range(k):
            i = self.rr % len(self.t)
            self.rr += 1
            out.append(i)
        return out


def build(nc, num_cores=N_CORES, phases=PHASES, taps=TAPS):
    es = ExitStack()
    p = PB(nc, es)

    def din(name, shape, dtype=F32):
        return nc.dram_tensor(name, list(shape), dtype, kind="ExternalInput").ap()

    x0t = din("x0t", [D, T])
    w_inproj = din("w_inproj", [D, 2 * DIN])
    cdg0 = din("cdg0", [64, 128, 128])
    w_xproj = din("w_xproj", [DIN, 96])
    w_dtproj = din("w_dtproj", [DTR, DIN])
    w_outproj = din("w_outproj", [DIN, D])
    w_gate = din("w_gate", [D, DFF])
    w_up = din("w_up", [D, DFF])
    w_down = din("w_down", [DFF, D])
    w_ffn_in = din("w_ffn_in", [D, DFF])
    w_ffn_out = din("w_ffn_out", [DFF, D])
    w1_inT = din("w1_inT", [D, E1P])
    cdg1 = din("cdg1", [36, 128, 128])
    w1_outT = din("w1_outT", [DIN, D])
    w1_rT = din("w1_rT", [D, D])
    w1_kT = din("w1_kT", [D, D])
    w1_vT = din("w1_vT", [D, D])
    c_dtlnw = din("c_dtlnw", [64, 1])
    c_blnw = din("c_blnw", [16, 1])
    c_clnw = din("c_clnw", [16, 1])
    c_bclnw = din("c_bclnw", [32, 1])
    c_dtprojb = din("c_dtprojb", [128, 16])
    c_asc = din("c_asc", [128, 256])
    c_dsk = din("c_dsk", [128, 16])
    c_convb0 = din("c_convb0", [128, 16])
    c_premoe = din("c_premoe", [128, 8])
    c_bin = din("c_bin", [128, 32])
    c_bout = din("c_bout", [128, 8])
    c_dtbias1 = din("c_dtbias1", [64, 1])
    c_alog1 = din("c_alog1", [64, 1])
    c_dsk1 = din("c_dsk1", [128, 8])
    c_lnw1 = din("c_lnw1", [128, 16])
    c_lnb1 = din("c_lnb1", [128, 16])
    c_mur = din("c_mur", [128, 8])
    c_muk = din("c_muk", [128, 8])
    c_convb1 = din("c_convb1", [128, 9])
    k_ones = din("k_ones", [1, 128])
    k_mones = din("k_mones", [1, 128])
    k_ident = din("k_ident", [128, 128])
    k_c3 = din("k_c3", [96, 3])
    k_b3 = din("k_b3", [3, 96])
    k_sel8 = din("k_sel8", [8, 128])
    k_masktri = din("k_masktri", [128, 512])
    k_maskneg = din("k_maskneg", [128, 512])
    k_r1024 = din("k_r1024", [128, 1])
    k_r2048 = din("k_r2048", [128, 1])
    k_onescol = din("k_onescol", [128, 1])
    k_eps = din("k_eps", [128, 3])
    k_selhp = din("k_selhp", [64, 1024])
    k_insel = din("k_insel", [16, 1024])

    out = nc.dram_tensor("out", [D, T], F32, kind="ExternalOutput").ap()

    sp_hid = nc.dram_tensor("sp_hid", [DFF, T], F32, kind="Internal").ap()
    cc1i = nc.dram_tensor("cc1i", [1, 1], F32, kind="Internal").ap()
    cc1o = nc.dram_tensor("cc1o", [1, 1], F32, kind="Internal").ap()
    cc2i = nc.dram_tensor("cc2i", [1, 1], F32, kind="Internal").ap()
    cc2o = nc.dram_tensor("cc2o", [1, 1], F32, kind="Internal").ap()

    tap_list = []
    def tap(name, shape, dtype=F32):
        t = nc.dram_tensor(name, list(shape), dtype, kind="ExternalOutput").ap()
        tap_list.append(name)
        return t

    sb = lambda name, shape, dtype=F32: es.enter_context(nc.sbuf_tensor(name, list(shape), dtype))

    bufA = sb("bufA", [128, 16384])
    bufB = sb("bufB", [128, 32768], BF16)
    bcall = sb("bcall", [128, 16384], BF16)
    hstrip = sb("hstrip", [128, 8192], BF16)
    wsl = sb("wsl", [128, 6144], BF16)
    smallA = sb("smallA", [128, 1792])
    smallB = sb("smallB", [128, 1664])
    consts = sb("consts", [128, 384])
    constb = sb("constb", [128, 768], BF16)
    colsA = sb("colsA", [128, 512])
    carry = sb("carry", [128, 256])

    bufA_bf = bufA.ap().bitcast(BF16)
    bufB_f32 = bufB.ap().bitcast(F32)
    hstrip_f32 = hstrip.ap().bitcast(F32)
    bcall_f32 = bcall.ap().bitcast(F32)

    psum = [es.enter_context(nc.psum_tensor(f"ps{i}", [128, 512], F32)) for i in range(8)]
    p.set_psum(psum)
    banks = Banks(psum[:6])

    blk = es.enter_context(nc.Block())

    ONES = consts[0:1, 0:128]
    ONES64 = consts[64:65, 0:128]
    ONES32 = consts[32:33, 0:128]
    IDENT = consts.ap()[:, 128:256]
    C3 = consts[0:96, 256:259]
    B3 = consts[0:3, 260:356]
    R1024 = consts.ap()[:, 356:357]
    R2048 = consts.ap()[:, 357:358]
    ONESCOL = consts.ap()[:, 358:359]
    IDENTB = constb.ap()[:, 0:128]
    MASKTRI = constb.ap()[:, 128:640]
    SEL8 = constb[0:8, 640:768]

    DTLNW = colsA[0:64, 0:1]
    BLNW = colsA[0:16, 1:2]
    CLNW = colsA[0:16, 2:3]
    DTPROJB = colsA.ap()[:, 3:19]
    ASC = colsA.ap()[:, 19:275]
    DSK = colsA.ap()[:, 275:291]
    CONVB0 = colsA.ap()[:, 291:307]
    PREMOE = colsA.ap()[:, 307:315]
    BIN = colsA.ap()[:, 315:347]
    BOUT = colsA.ap()[:, 347:355]
    DTBIAS1 = colsA[0:64, 355:356]
    ALOG1 = colsA[0:64, 356:357]
    DSK1 = colsA.ap()[:, 357:365]
    LNW1 = colsA.ap()[:, 365:381]
    LNB1 = colsA.ap()[:, 381:397]
    MUR = colsA.ap()[:, 397:405]
    MUK = colsA.ap()[:, 405:413]
    ONEMMUR = colsA.ap()[:, 413:421]
    ONEMMUK = colsA.ap()[:, 421:429]
    CONVB1 = colsA.ap()[:, 429:438]
    AMAX1 = colsA.ap()[:, 438:439]
    S1COL = colsA.ap()[:, 439:440]
    INV1COL = colsA.ap()[:, 440:441]
    S2COL = colsA.ap()[:, 441:442]
    AMAX2 = colsA.ap()[:, 442:443]
    COLAMAX = colsA.ap()[:, 443:475]
    COLSC = colsA.ap()[:, 475:507]
    EPS5 = colsA.ap()[:, 507:508]
    EPS6 = colsA.ap()[:, 508:509]
    EPSG = colsA.ap()[:, 509:510]

    t_sy = []
    p.dma("sync", consts[0:1, 0:128], k_ones, slot="boot")
    p.dma("sync", consts[64:65, 0:128], k_ones, slot="boot")
    p.dma("sync", consts[32:33, 0:128], k_ones, slot="boot")
    p.dma("sync", consts.ap()[:, 128:256], k_ident, slot="boot")
    p.dma("sync", consts[0:96, 256:259], k_c3, slot="boot")
    p.dma("sync", consts[0:3, 260:356], k_b3, slot="boot")
    p.dma("sync", consts.ap()[:, 356:357], k_r1024, slot="boot")
    p.dma("sync", consts.ap()[:, 357:358], k_r2048, slot="boot")
    p.dma("sync", consts.ap()[:, 358:359], k_onescol, slot="boot")
    p.dma("gpsimd", constb.ap()[:, 0:128], k_ident, slot="bootg")
    p.dma("gpsimd", constb.ap()[:, 128:640], k_masktri, slot="bootg")
    p.dma("gpsimd", constb[0:8, 640:768], k_sel8, slot="bootg")
    p.dma("sync", colsA[0:64, 0:1], c_dtlnw, slot="boot")
    p.dma("sync", colsA[0:16, 1:2], c_blnw, slot="boot")
    p.dma("sync", colsA[0:16, 2:3], c_clnw, slot="boot")
    p.dma("sync", colsA[64:96, 1:2], c_bclnw, slot="boot")
    p.dma("sync", colsA.ap()[:, 3:19], c_dtprojb, slot="boot")
    p.dma("sync", colsA.ap()[:, 19:275], c_asc, slot="boot")
    p.dma("sync", colsA.ap()[:, 275:291], c_dsk, slot="boot")
    p.dma("sync", colsA.ap()[:, 291:307], c_convb0, slot="boot")
    p.dma("sync", colsA.ap()[:, 307:315], c_premoe, slot="boot")
    p.dma("sync", colsA.ap()[:, 315:347], c_bin, slot="boot")
    p.dma("sync", colsA.ap()[:, 347:355], c_bout, slot="boot")
    p.dma("sync", colsA[0:64, 355:356], c_dtbias1, slot="boot")
    p.dma("sync", colsA[0:64, 356:357], c_alog1, slot="boot")
    p.dma("sync", colsA.ap()[:, 357:365], c_dsk1, slot="boot")
    p.dma("sync", colsA.ap()[:, 365:381], c_lnw1, slot="boot")
    p.dma("sync", colsA.ap()[:, 381:397], c_lnb1, slot="boot")
    p.dma("sync", colsA.ap()[:, 397:405], c_mur, slot="boot")
    p.dma("sync", colsA.ap()[:, 405:413], c_muk, slot="boot")
    p.dma("sync", colsA.ap()[:, 429:438], c_convb1, slot="boot")
    p.dma("sync", colsA.ap()[:, 507:510], k_eps, slot="boot")

    tok_consts = p.tok("q.boot")
    tok_constb = p.tok("g.bootg")

    p.emit("scalar", lambda e: e.activation(ASC, ASC, ACT.Exp, bias=0.0, scale=1.0), waits=[tok_consts], inc="s")
    p.emit("scalar", lambda e: e.activation(colsA[0:64, 356:357], colsA[0:64, 356:357], ACT.Exp, bias=0.0, scale=1.0), inc="s")

    p.drain("scalar")
    p.emit("vector", lambda e: e.tensor_scalar(colsA.ap()[:, 3:19], colsA.ap()[:, 3:19], -1.0, None, op0=ALU.mult),
           waits=[tok_consts], inc="v")
    p.emit("vector", lambda e: e.tensor_scalar(colsA[0:64, 355:356], colsA[0:64, 355:356], -1.0, None, op0=ALU.mult), inc="v")
    p.emit("vector", lambda e: e.tensor_scalar(ONEMMUR, MUR, -1.0, 1.0, op0=ALU.mult, op1=ALU.add), inc="v")
    p.emit("vector", lambda e: e.tensor_scalar(ONEMMUK, MUK, -1.0, 1.0, op0=ALU.mult, op1=ALU.add), inc="v")
    p.emit("vector", lambda e: e.tensor_scalar(colsA[64:80, 1:2], colsA[64:80, 1:2], -1.0, None, op0=ALU.mult), inc="v")

    p.drain("vector")
    tok_setup_v = p.tok("v")
    tok_setup_s = p.tok("s")

    def xn_t(d, c0, c1):
        return bufA.ap()[:, d * T + c0: d * T + c1]

    x0sb = bufB_f32
    def x0_t(d, c0, c1):
        return x0sb[:, d * T + c0: d * T + c1]

    tok_x0 = p.dma("sync",
                   bass.AP(x0sb.tensor, x0sb.offset, [[16384, 128], [2048, 8], [1, 2048]]),
                   x0t.rearrange("(a p) t -> p a t", p=128), slot="x0")

    redbuf = smallA.ap()[:, 0:64]
    sqst = [smallB.ap()[:, i * 512:(i + 1) * 512] for i in range(2)]
    musb = smallA[0:1, 64:576]
    rrow = smallA[0:1, 576:1088]

    def emit_layernorm(src_t, dst_t, red_off, src_tok, do_red=True):
        lastv = None
        prev_chunk = None
        for c in range(NCH):
            c0, c1 = c * CH, (c + 1) * CH
            bmu, bsq, brb, bab = banks.get(4)
            pairs_mu = [(R1024, src_t(d, c0, c1)) for d in range(8)]
            tmu = p.mm(psum[bmu][0:1, 0:CH], pairs_mu, waits=[src_tok, tok_consts, banks.free[bmu]])
            sq_mm = [prev_chunk, prev_chunk]
            for d in range(8):
                stok = p.op("scalar", "activation", sqst[d % 2], src_t(d, c0, c1), ACT.Square,
                            bias=0.0, scale=1.0, waits=[src_tok, sq_mm[d % 2]], inc="s")
                mtok = p.op("tensor", "matmul", psum[bsq][0:1, 0:CH], R1024, sqst[d % 2],
                            start=(d == 0), stop=(d == 7),
                            waits=[stok] + ([banks.free[bsq]] if d == 0 else []), inc="t")
                sq_mm[d % 2] = mtok
            tsq = sq_mm[1]
            tmu2 = p.op("vector", "tensor_copy", musb, psum[bmu][0:1, 0:CH], waits=[tmu], inc="v")
            banks.free[bmu] = tmu2
            p.drain("vector")
            tm2 = p.op("vector", "tensor_tensor", rrow, musb, musb, op=ALU.mult, inc="v")
            p.drain("vector")
            tvar = p.op("vector", "tensor_tensor", rrow, psum[bsq][0:1, 0:CH], rrow, op=ALU.subtract,
                        waits=[tsq], inc="v")
            banks.free[bsq] = tvar
            ts1 = p.op("scalar", "activation", rrow, rrow, ACT.Sqrt, bias=EPS5[0:1, 0:1], scale=1.0,
                       waits=[tvar], inc="s")
            tr = p.op("vector", "reciprocal", rrow, rrow, waits=[ts1], inc="v")
            p.drain("vector")
            trb = p.mm(psum[brb][:, 0:CH], [(ONES, rrow)], waits=[tr, banks.free[brb]])
            tab = p.mm(psum[bab][:, 0:CH], [(ONES, musb)], waits=[tmu2, tr, banks.free[bab]])
            xn_toks = []
            for d in range(8):
                t1 = p.op("vector", "tensor_tensor", sqst[d % 2], src_t(d, c0, c1), psum[bab][:, 0:CH],
                          op=ALU.subtract, waits=[trb, tab, (None if d < 2 else xn_toks[d - 2])], inc="v")
                p.drain("vector")
                t2 = p.op("vector", "tensor_tensor", dst_t(d, c0, c1), sqst[d % 2], psum[brb][:, 0:CH],
                          op=ALU.mult, waits=[tab], inc="v")
                xn_toks.append(t2)
                if do_red:
                    p.drain("vector")
                    t3 = p.op("vector", "tensor_reduce",
                              redbuf[:, red_off + c * 8 + d: red_off + c * 8 + d + 1],
                              dst_t(d, c0, c1), axis=AX, op=ALU.max, apply_absolute_value=True, inc="v")
                    lastv = t3
                else:
                    lastv = t2
            banks.free[brb] = lastv
            banks.free[bab] = lastv
            prev_chunk = xn_toks[-1]
        return lastv

    tok_xn = emit_layernorm(x0_t, xn_t, 0, tok_x0)

    am_col = smallA.ap()[:, 1664:1665]
    am1 = smallA[0:1, 1671:1672]
    t1 = p.emit("vector", lambda e: e.tensor_reduce(am_col, redbuf[:, 0:32], axis=AX, op=ALU.max), waits=[tok_xn], inc="v")
    btr = banks.get(1)[0]
    t2 = p.op("tensor", "transpose", psum[btr][0:1, 0:128], am_col, IDENT,
              waits=[t1, banks.free[btr], tok_consts], inc="t")
    t3 = p.op("vector", "tensor_reduce", am1, psum[btr][0:1, 0:128], axis=AX, op=ALU.max, waits=[t2], inc="v")
    banks.free[btr] = t3
    tok_am1_dma = p.dma("sync", cc1i, am1, waits=[t3], slot="cc1w")
    tok_cc1 = p.emit("gpsimd",
                     lambda e: e.collective_compute("AllReduce", ALU.max, replica_groups=[list(range(num_cores))],
                                                    ins=[cc1i], outs=[cc1o]),
                     waits=[tok_am1_dma], inc="cc")
    tok_am1_back = p.dma("gpsimd", am1, cc1o, waits=[tok_cc1], slot="cc1r")
    s1row = smallA[0:1, 1672:1673]
    inv1row = smallA[0:1, 1673:1674]
    t4 = p.emit("vector", lambda e: e.reciprocal(s1row, am1), waits=[tok_am1_back], inc="v")
    p.drain("vector")
    t5 = p.emit("vector", lambda e: e.tensor_scalar(s1row, s1row, 32767.0, None, op0=ALU.mult), inc="v")
    t6 = p.emit("vector", lambda e: e.tensor_scalar(inv1row, am1, 1.0 / 32767.0, None, op0=ALU.mult),
                waits=[tok_am1_back], inc="v")
    bq = banks.get(1)[0]
    t7a = p.mm(psum[bq][:, 0:1], [(ONES, am1)], waits=[tok_am1_back, banks.free[bq]])
    t7b = p.mm(psum[bq][:, 1:2], [(ONES, s1row)], waits=[t5])
    t7c = p.mm(psum[bq][:, 2:3], [(ONES, inv1row)], waits=[t6])
    t8 = p.op("vector", "tensor_copy", AMAX1, psum[bq][:, 0:1], waits=[t7a], inc="v")
    t8b = p.op("vector", "tensor_copy", S1COL, psum[bq][:, 1:2], waits=[t7b], inc="v")
    t8c = p.op("vector", "tensor_copy", INV1COL, psum[bq][:, 2:3], waits=[t7c], inc="v")
    banks.free[bq] = t8c
    tok_s1 = t8b
    tok_inv1 = t8c

    if taps:
        tp_xn = tap("tp_xn", [D, T])
        p.dma("sync", tp_xn.rearrange("(a p) t -> p a t", p=128),
              bass.AP(bufA.ap().tensor, 0, [[16384, 128], [2048, 8], [1, 2048]]), waits=[tok_xn], slot="tap")
        tp_am = tap("tp_am", [1, 1])
        p.dma("sync", tp_am, am1, waits=[tok_am1_back], slot="tap")

    tok_hid_spill = None
    tok_cc2 = None
    if phases >= 1:
        K_in = bufB.ap()
        def kin_ap(k, m0, m1):
            return bufB.ap()[:, k * 4096 + m0: k * 4096 + m1]
        wfa = hstrip_f32
        qtmp = smallB.ap()[:, 0:512]
        qsrowA = smallA[0:1, 64:576]
        cmx = smallA.ap()[:, 1666:1670]

        tok_prev_use = [None] * 8
        prev_wq2 = None
        for ms in range(8):
            m0 = ms * 512
            ldt = []
            for k in range(8):
                tok = p.dma("sync", wfa[:, k * 512:(k + 1) * 512],
                            w_ffn_in[k * 128:(k + 1) * 128, m0:m0 + 512],
                            waits=[tok_prev_use[k]], slot="wf%d" % k)
                ldt.append(tok)
            lv1 = [smallB.ap()[:, 0:512], smallB.ap()[:, 512:1024]]
            lvm = smallB.ap()[:, 1024:1536]
            p.op("vector", "tensor_tensor", lv1[0], wfa[:, 0:512], wfa[:, 512:1024], op=ALU.max,
                 waits=[ldt[0], ldt[1], prev_wq2], inc="v")
            p.op("vector", "tensor_tensor", lvm, wfa[:, 0:512], wfa[:, 512:1024], op=ALU.min, inc="v")
            for k2 in (1, 2, 3):
                c0 = k2 * 1024
                p.op("vector", "tensor_tensor", lv1[1], wfa[:, c0:c0 + 512], wfa[:, c0 + 512:c0 + 1024],
                     op=ALU.max, waits=[ldt[2 * k2], ldt[2 * k2 + 1]], inc="v")
                p.drain("vector")
                p.op("vector", "tensor_tensor", lv1[0], lv1[0], lv1[1], op=ALU.max, inc="v")
                p.drain("vector")
                p.op("vector", "tensor_tensor", lv1[1], wfa[:, c0:c0 + 512], wfa[:, c0 + 512:c0 + 1024],
                     op=ALU.min, inc="v")
                p.drain("vector")
                p.op("vector", "tensor_tensor", lvm, lvm, lv1[1], op=ALU.min, inc="v")
                p.drain("vector")
            tG = p.op("vector", "scalar_tensor_tensor", lv1[0], lvm, -1.0, lv1[0],
                      op0=ALU.mult, op1=ALU.max, inc="v")
            last_tt = None
            for j in range(4):
                bt = banks.get(1)[0]
                tt = p.op("tensor", "transpose", psum[bt][:, 0:128], lv1[0][:, j * 128:(j + 1) * 128], IDENT,
                          waits=[tG, banks.free[bt]], inc="t")
                last_tt = tt
                tr2 = p.op("vector", "tensor_reduce", cmx[:, j:j + 1], psum[bt][:, 0:128], axis=AX, op=ALU.max,
                           waits=[tt], inc="v")
                banks.free[bt] = tr2
            p.drain("vector")
            p.op("vector", "tensor_scalar", COLAMAX[:, ms * 4:ms * 4 + 4], cmx, 1e-8, None, op0=ALU.max, inc="v")
            p.drain("vector")
            p.op("vector", "tensor_scalar", COLSC[:, ms * 4:ms * 4 + 4], COLAMAX[:, ms * 4:ms * 4 + 4],
                 1.0 / 127.0, None, op0=ALU.mult, inc="v")
            p.op("vector", "reciprocal", cmx, COLAMAX[:, ms * 4:ms * 4 + 4], inc="v")
            p.drain("vector")
            t12 = p.op("vector", "tensor_scalar", cmx, cmx, 127.0, None, op0=ALU.mult, inc="v")
            t14s = []
            bt = banks.get(1)[0]
            for j in range(4):
                t13 = p.op("tensor", "transpose", psum[bt][0:1, j * 128:(j + 1) * 128], cmx[:, j:j + 1], IDENT,
                           waits=[t12] + ([banks.free[bt]] if j == 0 else []), inc="t")
                t14 = p.op("vector", "tensor_copy", qsrowA[:, j * 128:(j + 1) * 128],
                           psum[bt][0:1, j * 128:(j + 1) * 128], waits=[t13, tok_inv1], inc="v")
                t14s.append(t14)
            banks.free[bt] = t14s[-1]
            bqb = banks.get(1)[0]
            t15 = p.mm(psum[bqb][:, 0:512], [(ONES, qsrowA)], waits=t14s + [banks.free[bqb]])
            lastq = None
            for k in range(8):
                tq1 = p.op("vector", "tensor_tensor", qtmp, wfa[:, k * 512:(k + 1) * 512], psum[bqb][:, 0:512],
                           op=ALU.mult, waits=[t15, last_tt], inc="v")
                tq2 = p.op("scalar", "activation", qtmp, qtmp, ACT.Copy, bias=MAGIC, scale=1.0,
                           waits=[tq1], inc="s")
                prev_wq2 = tq2
                tq4 = p.op("vector", "tensor_scalar", kin_ap(k, m0, m0 + 512), qtmp, -MAGIC, None,
                           op0=ALU.add, waits=[tq2], inc="v")
                lastq = tq4
                tok_prev_use[k] = tq1
            banks.free[bqb] = lastq
        tok_kin = p.tok("v")

        hred = smallB.ap()[:, 1024:1152]
        hstage = [smallB.ap()[:, 1152:1664], smallB.ap()[:, 0:512]]
        qpre = [smallB.ap()[:, 0:512], smallB.ap()[:, 512:1024]]
        hred_n = 0
        tok_hstage = [[], []]
        tok_xsim_use = [None] * 8
        prev_q3 = [None, None]
        p.drain("vector")
        for c in range(NCH):
            c0 = c * CH
            xs_toks = []
            for d in range(8):
                xs = wsl.ap()[:, d * 512:(d + 1) * 512]
                tq1 = p.op("scalar", "activation", qpre[d % 2], xn_t(d, c0, c0 + CH), ACT.Copy,
                           bias=MAGIC, scale=S1COL,
                           waits=[tok_s1, tok_xn, tok_kin, tok_xsim_use[d], prev_q3[d % 2]]
                           + tok_hstage[0] + tok_hstage[1], inc="s")
                tq3 = p.op("vector", "tensor_scalar", xs, qpre[d % 2], -MAGIC, INV1COL,
                           op0=ALU.add, op1=ALU.mult, waits=[tq1, tok_inv1], inc="v")
                prev_q3[d % 2] = tq3
                xs_toks.append(tq3)
            for m in range(32):
                bmm = banks.get(1)[0]
                pairs = [(kin_ap(k, m * 128, (m + 1) * 128), wsl.ap()[:, k * 512:(k + 1) * 512]) for k in range(8)]
                tmm = p.mm(psum[bmm][:, 0:CH], pairs, waits=xs_toks + [tok_kin, banks.free[bmm]])
                tok_xsim_use = [tmm] * 8
                hs = hstage[m % 2]
                tg = p.op("scalar", "activation", hs, psum[bmm][:, 0:CH], ACT.Gelu,
                          bias=BIN[:, m:m + 1], scale=COLSC[:, m:m + 1],
                          waits=[tmm, tok_kin] + tok_hstage[m % 2], inc="s")
                banks.free[bmm] = tg
                tr = p.op("vector", "tensor_reduce", hred[:, hred_n:hred_n + 1], hs, axis=AX,
                          op=ALU.max, apply_absolute_value=True, waits=[tg], inc="v")
                hred_n += 1
                tspill = p.dma("sync", sp_hid[m * 128:(m + 1) * 128, c0:c0 + CH], hs, waits=[tg],
                               slot="hs%d" % (m % 2))
                tok_hstage[m % 2] = [tspill, tr]
        tok_hid_spill = [p.tok("q.hs0"), p.tok("q.hs1")]
        am2loc = smallA.ap()[:, 1665:1666]
        am2rowT = smallA[0:1, 1674:1675]
        t1 = p.emit("vector", lambda e: e.tensor_reduce(am2loc, hred, axis=AX, op=ALU.max), waits=[p.tok("v"), p.tok("s")], inc="v")
        bt = banks.get(1)[0]
        t2 = p.op("tensor", "transpose", psum[bt][0:1, 0:128], am2loc, IDENT,
                  waits=[t1, banks.free[bt]], inc="t")
        t3 = p.op("vector", "tensor_reduce", am2rowT[0:1, 0:1], psum[bt][0:1, 0:128],
                  axis=AX, op=ALU.max, waits=[t2], inc="v")
        banks.free[bt] = t3
        tok_am2_dma = p.dma("sync", cc2i, am2rowT[0:1, 0:1], waits=[t3], slot="cc2w")
        tok_cc2 = p.emit("gpsimd",
                         lambda e: e.collective_compute("AllReduce", ALU.max, replica_groups=[list(range(num_cores))],
                                                        ins=[cc2i], outs=[cc2o]),
                         waits=[tok_am2_dma], inc="cc")

        if taps:
            tp_h = tap("tp_hid", [DFF, CH])
            p.dma("sync", tp_h, sp_hid[:, 0:CH], waits=tok_hid_spill, slot="tap")



    if phases >= 2:
        p.drain("vector")
        p.drain("scalar")
        gate = [p.tok("v"), p.tok("s"), p.tok("t")] + (tok_hid_spill or [])
        if "q.tap" in p.sems:
            gate.append(p.tok("q.tap"))
        xc_t = lambda d: bufB.ap()[:, d * 512:(d + 1) * 512]
        zsil_t = lambda d: bufB.ap()[:, 8192 + d * 512: 8192 + (d + 1) * 512]
        mdt_t = lambda d: bufB.ap()[:, 16384 + d * 512: 16384 + (d + 1) * 512]
        dtxy_t = lambda d: bufB.ap()[:, 24576 + d * 512: 24576 + (d + 1) * 512]
        bbc_t = lambda n: bcall.ap()[:, n * 512:(n + 1) * 512]
        cbc_t = lambda n: bcall.ap()[:, 8192 + n * 512: 8192 + (n + 1) * 512]
        xnbf_t = lambda k: hstrip.ap()[:, k * 512:(k + 1) * 512]
        h_t = lambda n: hstrip.ap()[:, n * 512:(n + 1) * 512]
        cdsl_t = lambda si: bass.AP(hstrip.ap().tensor, 4096 + si * 512, [[8192, 128], [128, 4], [1, 128]])
        cdblk = lambda si, kk: bass.AP(hstrip.ap().tensor, 4096 + si * 512 + kk * 128, [[8192, 128], [1, 128]])
        wslab_blk = lambda si, k: bass.AP(wsl.ap().tensor, si * 1024 + k * 128, [[6144, 128], [1, 128]])
        xpw_blk = lambda k: bass.AP(wsl.ap().tensor, 2048 + k * 96, [[6144, 128], [1, 96]])
        dtw_blk = lambda m: bass.AP(wsl.ap().tensor, (m % 2) * 64 * 6144 + 3584 + (m // 2) * 128, [[6144, 64], [1, 128]])
        stage = wsl.ap()[:, 4608:5123]
        halo = wsl.ap()[:, 5123:5171]
        dtn0 = wsl[0:64, 5171:5683]
        dtn64 = wsl[64:128, 5171:5683]
        proj96 = smallA[0:96, 0:512]
        projBC = smallA[64:96, 0:512]
        sq96 = smallA[0:96, 512:1024]
        r3 = smallA[0:3, 1024:1536]
        BCm = smallB[64:96, 0:512]
        t1buf = smallB.ap()[:, 1024:1536]

        def selBC(row):
            return bass.AP(consts.ap().tensor, 64 * 384 + 128 + 64 + row, [[384, 32], [0, 128]])

        tok_xpw = p.dma("gpsimd", bass.AP(wsl.ap().tensor, 2048, [[6144, 128], [96, 16], [1, 96]]),
                        bass.AP(w_xproj.tensor, 0, [[96, 128], [96 * 128, 16], [1, 96]]),
                        waits=gate, slot="wxp")
        tok_dtw = []
        for hh in range(2):
            dst = bass.AP(wsl.ap().tensor, hh * 64 * 6144 + 3584, [[6144, 64], [128, 8], [1, 128]])
            srcap = bass.AP(w_dtproj.tensor, hh * 128, [[2048, 64], [256, 8], [1, 128]])
            tok_dtw.append(p.dma("gpsimd", dst, srcap, waits=gate, slot="wdt%d" % hh))
        th0 = p.op("vector", "memset", halo, 0.0, waits=gate, inc="v")
        p.drain("vector")

        slab_free = [None, None]
        cd_free = [None, None]
        prev_h_user = None
        prev_h_user_t = None
        halo_tok = [th0] * 16
        stage_free = None
        carry_tok = [None] * 16
        prev_scan_user = None
        dA_free = [None, None]

        for c in range(NCH):
            c0 = c * CH
            xnbf_toks = []
            for k in range(8):
                tkx = p.op("vector", "tensor_copy", xnbf_t(k), xn_t(k, c0, c0 + CH),
                           waits=gate + [prev_h_user, prev_h_user_t, tok_xn], inc="v")
                xnbf_toks.append(tkx)
            p.drain("vector")
            last_conv_mm = None
            for half in range(2):
                coff = 0 if half == 0 else DIN
                for m in range(16):
                    si = m % 2
                    srcap = bass.AP(w_inproj.tensor, coff + m * 128, [[4096, 128], [4096 * 128, 8], [1, 128]])
                    dstap = bass.AP(wsl.ap().tensor, si * 1024, [[6144, 128], [128, 8], [1, 128]])
                    twl = p.dma("gpsimd", dstap, srcap, waits=[slab_free[si]], slot="ws%d" % si)
                    bmm = banks.get(1)[0]
                    pairs = [(wslab_blk(si, k), xnbf_t(k)) for k in range(8)]
                    tmm = p.mm(psum[bmm][:, 0:CH], pairs, waits=xnbf_toks + [twl, banks.free[bmm]])
                    slab_free[si] = tmm
                    if half == 1:
                        tz = p.op("scalar", "activation", zsil_t(m), psum[bmm][:, 0:CH], ACT.Silu,
                                  bias=0.0, scale=1.0, waits=[tmm, prev_h_user, prev_h_user_t], inc="s")
                        banks.free[bmm] = tz
                        continue
                    thalo = p.op("vector", "tensor_copy", stage[:, 0:3], halo[:, m * 3:m * 3 + 3],
                                 waits=[halo_tok[m], stage_free], inc="v")
                    p.drain("vector")
                    tev = p.op("vector", "tensor_copy", stage[:, 3:515], psum[bmm][:, 0:CH],
                               waits=[tmm], inc="v")
                    banks.free[bmm] = tev
                    p.drain("vector")
                    thl = p.op("vector", "tensor_copy", halo[:, m * 3:m * 3 + 3], stage[:, 512:515], inc="v")
                    halo_tok[m] = thl
                    csrc = bass.AP(cdg0.tensor, m * 4 * 128 * 128, [[128, 128], [128 * 128, 4], [1, 128]])
                    tcd = p.dma("gpsimd", cdsl_t(si), csrc, waits=[cd_free[si], prev_h_user, prev_h_user_t], slot="cd%d" % si)
                    bcv = banks.get(1)[0]
                    cpairs = [(cdblk(si, kk), stage[:, kk:kk + 512]) for kk in range(4)]
                    tcv = p.mm(psum[bcv][:, 0:CH], cpairs, waits=[tcd, tev, thl, thalo, banks.free[bcv]])
                    cd_free[si] = tcv
                    stage_free = tcv
                    last_conv_mm = tcv
                    txc = p.op("scalar", "activation", xc_t(m), psum[bcv][:, 0:CH], ACT.Silu,
                               bias=CONVB0[:, m:m + 1], scale=1.0, waits=[tcv, prev_h_user, prev_h_user_t], inc="s")
                    banks.free[bcv] = txc
            tok_xc = p.tok("s")
            bxp = banks.get(1)[0]
            xpairs = [(xpw_blk(k), xc_t(k)) for k in range(16)]
            txp = p.mm(psum[bxp][0:96, 0:CH], xpairs, waits=[tok_xpw, tok_xc, banks.free[bxp]])
            tpe = p.op("vector", "tensor_copy", proj96, psum[bxp][0:96, 0:CH], waits=[txp], inc="v")
            banks.free[bxp] = tpe
            tsqp = p.op("scalar", "activation", sq96, proj96, ACT.Square, bias=0.0, scale=1.0,
                        waits=[tpe], inc="s")
            bms = banks.get(1)[0]
            tms = p.mm(psum[bms][0:3, 0:CH], [(C3, sq96)], waits=[tsqp, banks.free[bms]])
            tr3 = p.op("scalar", "activation", r3, psum[bms][0:3, 0:CH], ACT.Sqrt,
                       bias=EPS6[0:3, 0:1], scale=1.0, waits=[tms], inc="s")
            banks.free[bms] = tr3
            trc = p.op("vector", "reciprocal", r3, r3, waits=[tr3], inc="v")
            p.drain("vector")
            brb = banks.get(1)[0]
            trb = p.mm(psum[brb][0:96, 0:CH], [(B3, r3)], waits=[trc, banks.free[brb]])
            tdtn = p.op("vector", "scalar_tensor_tensor", dtn0, smallA[0:64, 0:512], DTLNW,
                        psum[brb][0:64, 0:CH], op0=ALU.mult, op1=ALU.mult, waits=[trb, tok_dtw[0], tok_dtw[1]], inc="v")
            tbcm = p.op("vector", "scalar_tensor_tensor", BCm, projBC, colsA[64:96, 1:2],
                        psum[brb][64:96, 0:CH], op0=ALU.mult, op1=ALU.mult, waits=[trb, tok_setup_v], inc="v")
            banks.free[brb] = tbcm
            tdtn64 = p.dma("gpsimd", dtn64, dtn0, waits=[tdtn], slot="dtnc")
            tln_prev = None
            for m in range(16):
                bdt = banks.get(1)[0]
                tdm = p.mm(psum[bdt][:, 0:CH], [(dtw_blk(m), dtn0 if m % 2 == 0 else dtn64)],
                           waits=[tdtn if m % 2 == 0 else tdtn64, banks.free[bdt]])
                tsg = p.op("scalar", "activation", t1buf, psum[bdt][:, 0:CH], ACT.Sigmoid,
                           bias=DTPROJB[:, m:m + 1], scale=-1.0, waits=[tdm, tln_prev, tok_setup_v], inc="s")
                banks.free[bdt] = tsg
                p.drain("scalar")
                tln = p.op("scalar", "activation", mdt_t(m), t1buf, ACT.Ln,
                           bias=EPSG, scale=1.0, waits=[prev_h_user, prev_h_user_t], inc="s")
                p.drain("scalar")
                tln_prev = tln
            tok_mdt = p.tok("s")
            dtx_toks = []
            for m in range(16):
                td = p.op("vector", "tensor_tensor", dtxy_t(m), mdt_t(m), xc_t(m), op=ALU.mult,
                          waits=[tok_mdt, tok_xc, prev_h_user, prev_h_user_t], inc="v")
                dtx_toks.append(td)
            bc_toks = []
            for n in range(16):
                bb = banks.get(1)[0]
                tb = p.mm(psum[bb][:, 0:CH], [(selBC(n), BCm)], waits=[tbcm, tok_consts, banks.free[bb]])
                te = p.op("scalar", "activation", bbc_t(n), psum[bb][:, 0:CH], ACT.Copy,
                          bias=0.0, scale=1.0, waits=[tb, prev_scan_user], inc="s")
                banks.free[bb] = te
                bc = banks.get(1)[0]
                tb2 = p.mm(psum[bc][:, 0:CH], [(selBC(16 + n), BCm)], waits=[tbcm, banks.free[bc]])
                te2 = p.op("scalar", "activation", cbc_t(n), psum[bc][:, 0:CH], ACT.Copy,
                           bias=0.0, scale=1.0, waits=[tb2, prev_scan_user], inc="s")
                banks.free[bc] = te2
                bc_toks.append((te, te2))
            bbuf = [wsl.ap()[:, 4608:5120], wsl.ap()[:, 5171:5683]]
            bbuf_free = [last_conv_mm, (tdtn64 if True else None)]
            bbuf_free[1] = p.tok("t")
            sum_evac = None
            new_h_user = None
            for dt_ in range(16):
                bsum = banks.get(1)[0]
                scan_toks = []
                for n in range(16):
                    dab = 6 + (n % 2)
                    tda = p.op("scalar", "activation", psum[dab][:, 0:CH], mdt_t(dt_), ACT.Exp,
                               scale=ASC[:, n * 16 + dt_:n * 16 + dt_ + 1], bias=0.0,
                               waits=[tok_mdt, dA_free[n % 2], tok_setup_s], inc="s")
                    tb = p.op("vector", "tensor_tensor", bbuf[n % 2], dtxy_t(dt_), bbc_t(n), op=ALU.mult,
                              waits=[dtx_toks[dt_], bc_toks[n][0], bbuf_free[n % 2]], inc="v")
                    p.drain("vector")
                    init = 0.0 if c == 0 else carry[:, dt_ * 16 + n: dt_ * 16 + n + 1]
                    tsc = p.op("vector", "tensor_tensor_scan", h_t(n), psum[dab][:, 0:CH], bbuf[n % 2],
                               init, op0=ALU.mult, op1=ALU.add,
                               waits=[tda, tb, carry_tok[dt_], prev_h_user, last_conv_mm,
                                      xnbf_toks[-1], p.tok("t") if (dt_ == 0 and n == 0) else None], inc="v")
                    p.drain("vector")
                    dA_free[n % 2] = tsc
                    bbuf_free[n % 2] = tsc
                    scan_toks.append(tsc)
                if c < NCH - 1:
                    tcar = p.op("vector", "tensor_copy", carry[:, dt_ * 16:dt_ * 16 + 16],
                                bass.AP(hstrip.ap().tensor, 511, [[8192, 128], [512, 16]]),
                                waits=scan_toks, inc="v")
                    carry_tok[dt_] = tcar
                    p.drain("vector")
                tsum = None
                for n in range(16):
                    tp_ = p.op("vector", "tensor_tensor", h_t(n), h_t(n), cbc_t(n), op=ALU.mult,
                               waits=[scan_toks[n], bc_toks[n][1], carry_tok[dt_]], inc="v")
                    p.drain("vector")
                    tsum = p.op("tensor", "matmul", psum[bsum][:, 0:CH], IDENTB, h_t(n),
                                start=(n == 0), stop=(n == 15),
                                waits=[tp_, tok_constb] + ([banks.free[bsum]] if n == 0 else []),
                                inc=("t" if n == 15 else None))
                    new_h_user = tsum
                ty1 = p.op("vector", "scalar_tensor_tensor", t1buf, xc_t(dt_), DSK[:, dt_:dt_ + 1],
                           psum[bsum][:, 0:CH], op0=ALU.mult, op1=ALU.add,
                           waits=[tsum, p.tok("s")], inc="v")
                banks.free[bsum] = ty1
                p.drain("vector")
                ty2 = p.op("vector", "tensor_tensor", dtxy_t(dt_), t1buf, zsil_t(dt_), op=ALU.mult, inc="v")
                p.drain("vector")
            prev_scan_user = p.tok("v")
            prev_h_user_next = p.tok("v")
            tok_y = p.tok("v")
            for m in range(8):
                tw1 = p.dma("gpsimd", bass.AP(wsl.ap().tensor, 0, [[6144, 128], [128, 8], [1, 128]]),
                            bass.AP(w_outproj.tensor, m * 128, [[1024, 128], [1024 * 128, 8], [1, 128]]),
                            waits=[slab_free[0]], slot="ws0")
                tw2 = p.dma("gpsimd", bass.AP(wsl.ap().tensor, 1024, [[6144, 128], [128, 8], [1, 128]]),
                            bass.AP(w_outproj.tensor, 8 * 128 * 1024 + m * 128, [[1024, 128], [1024 * 128, 8], [1, 128]]),
                            waits=[slab_free[1]], slot="ws1")
                bop = banks.get(1)[0]
                pairs = [(wslab_blk(0, k), dtxy_t(k)) for k in range(8)] +                         [(wslab_blk(1, k), dtxy_t(8 + k)) for k in range(8)]
                tmo = p.mm(psum[bop][:, 0:CH], pairs, waits=[tw1, tw2, tok_y, banks.free[bop]])
                slab_free[0] = tmo
                slab_free[1] = tmo
                thid = p.op("vector", "tensor_tensor", xn_t(m, c0, c0 + CH), psum[bop][:, 0:CH],
                            xn_t(m, c0, c0 + CH), op=ALU.add, waits=[tmo], inc="v")
                banks.free[bop] = thid
            prev_h_user = p.tok("v")
            prev_h_user_t = p.tok("t")
        tok_hid = p.tok("v")
        dA_user = p.tok("v")

        if taps:
            tp_hd = tap("tp_hidmx", [D, T])
            p.dma("sync", tp_hd.rearrange("(a p) t -> p a t", p=128),
                  bass.AP(bufA.ap().tensor, 0, [[16384, 128], [2048, 8], [1, 2048]]), waits=[tok_hid], slot="tap")


    tok_mlp_out = None
    if phases >= 3:
        p.drain("vector")
        p.drain("scalar")
        gate2 = [p.tok("v"), p.tok("s"), p.tok("t"), tok_hid]
        MCH = 1024
        hn_t = lambda d, cc: bcall.ap()[:, d * 1024 + cc * 512: d * 1024 + cc * 512 + 512]
        ff_t = lambda m, cc: bufB.ap()[:, m * 1024 + cc * 512: m * 1024 + cc * 512 + 512]
        stgb = wsl.ap()[:, 4608:5120]
        x0stg = smallB.ap()[:, 1024:1536]
        stg1 = smallA.ap()[:, 1152:1664]
        hid_t = xn_t
        prev_ff_user = None
        stgw_free = None
        tok_out_w = []
        for mc in range(2):
            m0 = mc * MCH
            hn_toks = []
            for cc in range(2):
                cb_, cs_ = m0 + cc * 512, 512
                bsq2 = banks.get(1)[0]
                sqmm = [None, None]
                for d in range(8):
                    st = p.op("scalar", "activation", sqst[d % 2], hid_t(d, cb_, cb_ + 512), ACT.Square,
                              bias=0.0, scale=1.0, waits=gate2 + [sqmm[d % 2]], inc="s")
                    mt = p.op("tensor", "matmul", psum[bsq2][0:1, 0:512], R1024, sqst[d % 2],
                              start=(d == 0), stop=(d == 7),
                              waits=[st] + ([banks.free[bsq2]] if d == 0 else []), inc="t")
                    sqmm[d % 2] = mt
                trr = p.op("scalar", "activation", rrow, psum[bsq2][0:1, 0:512], ACT.Sqrt,
                           bias=EPS6[0:1, 0:1], scale=1.0, waits=[sqmm[1]], inc="s")
                banks.free[bsq2] = trr
                trc = p.op("vector", "reciprocal", rrow, rrow, waits=[trr, gate2[0]], inc="v")
                brb2 = banks.get(1)[0]
                trb = p.mm(psum[brb2][:, 0:512], [(ONES, rrow)], waits=[trc, banks.free[brb2]])
                lastt = None
                for d in range(8):
                    th = p.op("vector", "scalar_tensor_tensor", hn_t(d, cc) if mc == 0 else hn_t(d, cc),
                              hid_t(d, cb_, cb_ + 512), PREMOE[:, d:d + 1], psum[brb2][:, 0:512],
                              op0=ALU.mult, op1=ALU.mult, waits=[trb, prev_ff_user], inc="v")
                    lastt = th
                    hn_toks.append(th)
                banks.free[brb2] = lastt
            prev_tff = None
            for m in range(32):
                twg = p.dma("gpsimd", bass.AP(wsl.ap().tensor, 0, [[6144, 128], [128, 8], [1, 128]]),
                            bass.AP(w_gate.tensor, m * 128, [[4096, 128], [4096 * 128, 8], [1, 128]]),
                            waits=[slab_free[0]], slot="ws0")
                twu = p.dma("gpsimd", bass.AP(wsl.ap().tensor, 1024, [[6144, 128], [128, 8], [1, 128]]),
                            bass.AP(w_up.tensor, m * 128, [[4096, 128], [4096 * 128, 8], [1, 128]]),
                            waits=[slab_free[1]], slot="ws1")
                lg = lu = None
                for cc in range(2):
                    bg = banks.get(1)[0]
                    lg = p.mm(psum[bg][:, 0:512], [(wslab_blk(0, k), hn_t(k, cc)) for k in range(8)],
                              waits=hn_toks + [twg, banks.free[bg]])
                    bu = banks.get(1)[0]
                    lu = p.mm(psum[bu][:, 0:512], [(wslab_blk(1, k), hn_t(k, cc)) for k in range(8)],
                              waits=[twu, banks.free[bu]])
                    tgs = p.op("scalar", "activation", stgb, psum[bg][:, 0:512], ACT.Silu,
                               bias=0.0, scale=1.0, waits=[lg, prev_ff_user, prev_tff], inc="s")
                    banks.free[bg] = tgs
                    tff = p.op("vector", "tensor_tensor", ff_t(m, cc), stgb, psum[bu][:, 0:512],
                               op=ALU.mult, waits=[tgs, lu, prev_ff_user], inc="v")
                    banks.free[bu] = tff
                    prev_tff = tff
                slab_free[0] = lg
                slab_free[1] = lu
            tok_ff = p.tok("v")
            for md in range(8):
                bks = banks.get(2)
                tdm_last = [None, None]
                for kb in range(4):
                    sl = kb % 2
                    tws = p.dma("gpsimd", bass.AP(wsl.ap().tensor, sl * 1024, [[6144, 128], [128, 8], [1, 128]]),
                                bass.AP(w_down.tensor, kb * 8 * 128 * 1024 + md * 128,
                                        [[1024, 128], [1024 * 128, 8], [1, 128]]),
                                waits=[slab_free[sl]], slot="ws%d" % sl)
                    for cc in range(2):
                        pairs = [(wslab_blk(sl, k), ff_t(kb * 8 + k, cc)) for k in range(8)]
                        tdm_last[cc] = p.mm(psum[bks[cc]][:, 0:512], pairs,
                                            waits=[tws, tok_ff] + ([banks.free[bks[cc]]] if kb == 0 else []),
                                            start=(kb == 0), stop=(kb == 3))
                    slab_free[sl] = tdm_last[1]
                for cc in range(2):
                    cb_ = m0 + cc * 512
                    tx0 = p.dma("sync", x0stg, x0t[md * 128:(md + 1) * 128, cb_:cb_ + 512],
                                waits=([prev_ff_user] if prev_ff_user is not None else gate2), slot="x0r")
                    t1 = p.op("vector", "tensor_tensor", stg1, psum[bks[cc]][:, 0:512],
                              hid_t(md, cb_, cb_ + 512), op=ALU.add,
                              waits=[tdm_last[cc], stgw_free], inc="v")
                    banks.free[bks[cc]] = t1
                    p.drain("vector")
                    t2 = p.op("vector", "tensor_tensor", stg1, stg1, x0stg, op=ALU.add, waits=[tx0], inc="v")
                    p.drain("vector")
                    tws_ = p.dma("sync", out[md * 128:(md + 1) * 128, cb_:cb_ + 512], stg1, waits=[t2], slot="outw")
                    tok_out_w.append(tws_)
                    stgw_free = tws_
                    prev_ff_user = t2
        tok_mlp_out = p.tok("q.outw")

    if phases >= 4:
        p.drain("vector")
        p.drain("scalar")
        gate3 = [p.tok("v"), p.tok("s"), p.tok("t")]
        if "q.tap" in p.sems:
            gate3.append(p.tok("q.tap"))
        am2 = smallA[0:1, 1675:1676]
        s2row = smallA[0:1, 1676:1677]
        inv2row = smallA[0:1, 1677:1678]
        tok_am2_back = p.dma("gpsimd", am2, cc2o, waits=[tok_cc2], slot="cc2r")
        tt4 = p.op("vector", "reciprocal", s2row, am2, waits=[tok_am2_back] + gate3, inc="v")
        p.drain("vector")
        tt5 = p.op("vector", "tensor_scalar", s2row, s2row, 32767.0, None, op0=ALU.mult, inc="v")
        tt6 = p.op("vector", "tensor_scalar", inv2row, am2, 1.0 / 32767.0, None, op0=ALU.mult,
                   waits=[tok_am2_back], inc="v")
        bq2 = banks.get(1)[0]
        tb1 = p.mm(psum[bq2][:, 0:1], [(ONES, s2row)], waits=[tt5, banks.free[bq2]])
        tb2 = p.mm(psum[bq2][:, 1:2], [(ONES, inv2row)], waits=[tt6])
        tc1 = p.op("vector", "tensor_copy", S2COL, psum[bq2][:, 0:1], waits=[tb1], inc="v")
        tc2 = p.op("vector", "tensor_copy", AMAX2, psum[bq2][:, 1:2], waits=[tb2], inc="v")
        banks.free[bq2] = tc2
        INV2COL = AMAX2
        ROWSC = colsA.ap()[:, 443:475]
        kout_ap = lambda k, q0, q1: bufA_bf[:, k * 1024 + q0: k * 1024 + q1]
        wfa2 = [hstrip_f32[:, 0:1024], hstrip_f32[:, 1024:2048]]
        rmaxc = smallA.ap()[:, 1670:1671]
        prev_kq = [gate3[2], gate3[2]]
        prev_kv = [None, None]
        prev_srd = None
        for k in range(32):
            sl = k % 2
            tld = p.dma("sync", wfa2[sl], w_ffn_out[k * 128:(k + 1) * 128, :], waits=[prev_kq[sl]], slot="wo%d" % sl)
            trm = p.op("vector", "tensor_reduce", rmaxc, wfa2[sl], axis=AX, op=ALU.max,
                       apply_absolute_value=True, waits=[tld, prev_srd] + gate3, inc="v")
            p.drain("vector")
            trm2 = p.op("vector", "tensor_scalar", ROWSC[:, k:k + 1], rmaxc, 1e-8, 1.0 / 127.0,
                        op0=ALU.max, op1=ALU.mult, inc="v")
            p.drain("vector")
            trm3 = p.op("vector", "tensor_scalar", rmaxc, rmaxc, 1e-8, None, op0=ALU.max, inc="v")
            p.drain("vector")
            trm4 = p.op("vector", "reciprocal", rmaxc, rmaxc, inc="v")
            p.drain("vector")
            trm5 = p.op("vector", "tensor_scalar", rmaxc, rmaxc, 127.0, None, op0=ALU.mult, inc="v")
            for q in range(2):
                tq1 = p.op("scalar", "activation", qpre[q], wfa2[sl][:, q * 512:(q + 1) * 512], ACT.Copy,
                           bias=MAGIC, scale=rmaxc, waits=[trm5, prev_kv[q]] + gate3, inc="s")
                prev_srd = tq1
                tq4 = p.op("vector", "tensor_scalar", kout_ap(k, q * 512, (q + 1) * 512), qpre[q],
                           -MAGIC, None, op0=ALU.add, waits=[tq1], inc="v")
                prev_kv[q] = tq4
                prev_kq[sl] = tq4
        p.drain("vector")
        tcmb = p.op("vector", "tensor_scalar", ROWSC, ROWSC, INV2COL, None, op0=ALU.mult, waits=[tc2], inc="v")
        tok_kout = p.tok("v")
        hs_t = lambda k: bufB.ap()[:, k * 512:(k + 1) * 512]
        hstg = [smallB.ap()[:, 0:512], smallB.ap()[:, 512:1024]]
        prev_hs_mm = None
        prev_ffn2 = None
        prev_hs_q3 = [tok_kout, tok_kout]
        for c in range(NCH):
            c0 = c * CH
            hs_toks = []
            for k in range(32):
                sl = k % 2
                tld = p.dma("sync", hstg[sl], sp_hid[k * 128:(k + 1) * 128, c0:c0 + CH],
                            waits=[prev_hs_q3[sl]], slot="hr%d" % sl)
                tq1 = p.op("scalar", "activation", hstg[sl], hstg[sl], ACT.Copy, bias=MAGIC, scale=S2COL,
                           waits=[tld, tc1, prev_hs_mm], inc="s")
                tq3 = p.op("vector", "tensor_scalar", hs_t(k), hstg[sl], -MAGIC, ROWSC[:, k:k + 1],
                           op0=ALU.add, op1=ALU.mult, waits=[tq1, tcmb], inc="v")
                prev_hs_q3[sl] = tq3
                hs_toks.append(tq3)
            for md in range(8):
                bmd = banks.get(1)[0]
                pairs = [(kout_ap(k, md * 128, (md + 1) * 128), hs_t(k)) for k in range(32)]
                tdm = p.mm(psum[bmd][:, 0:CH], pairs, waits=hs_toks + [tok_kout, banks.free[bmd]])
                prev_hs_mm = tdm
                tev = p.op("vector", "tensor_scalar", stg1, psum[bmd][:, 0:CH], BOUT[:, md:md + 1], None,
                           op0=ALU.add, waits=[tdm, tok_mlp_out, prev_ffn2], inc="v")
                banks.free[bmd] = tev
                p.drain("vector")
                tac = p.dma_acc(out[md * 128:(md + 1) * 128, c0:c0 + CH], stg1,
                                waits=[tev, tok_mlp_out], slot="acc")
                prev_ffn2 = tac
        tok_ffn2 = p.tok("g.acc")

    if phases >= 5:
        p.drain("vector")
        p.drain("scalar")
        gate4 = [p.tok("v"), p.tok("s"), p.tok("t")]
        x1sb = bufB_f32
        x1_t = lambda d, a, b: x1sb[:, d * T + a: d * T + b]
        tok_x1 = p.dma("sync",
                       bass.AP(x1sb.tensor, x1sb.offset, [[16384, 128], [2048, 8], [1, 2048]]),
                       out.rearrange("(a p) t -> p a t", p=128), waits=[tok_ffn2] + gate4, slot="x1r")
        tok_xn1 = emit_layernorm(x1_t, xn_t, 32, tok_x1, do_red=False)
        if taps:
            tp_x1 = tap("tp_x1", [D, T])
            p.dma("sync", tp_x1.rearrange("(a p) t -> p a t", p=128),
                  bass.AP(bufB_f32.tensor, bufB_f32.offset, [[16384, 128], [2048, 8], [1, 2048]]),
                  waits=[tok_x1], slot="tap")
            tp_xn1 = tap("tp_xn1", [D, T])
            p.dma("sync", tp_xn1.rearrange("(a p) t -> p a t", p=128),
                  bass.AP(bufA.ap().tensor, 0, [[16384, 128], [2048, 8], [1, 2048]]),
                  waits=[tok_xn1], slot="tap")


    if phases >= 6:
        p.drain("vector")
        p.drain("scalar")
        gate5 = [p.tok("v"), p.tok("s"), p.tok("t")]
        if "q.tap" in p.sems:
            gate5.append(p.tok("q.tap"))
        FC = 256
        xn1b_t = lambda k: bcall.ap()[:, k * 2048:(k + 1) * 2048]
        z_t = lambda d: bufB_f32[:, d * 256:(d + 1) * 256]
        y_t = lambda hg: bufB_f32[:, 2048 + hg * 256: 2048 + (hg + 1) * 256]
        xbcc_t = lambda ct: bufB.ap()[:, 8192 + ct * 256: 8192 + (ct + 1) * 256]
        xbct_t = lambda tt: bufB.ap()[:, 10496 + tt * 1152: 10496 + (tt + 1) * 1152]
        xdt_t = lambda tt: bufB.ap()[:, 12800 + tt * 1024: 12800 + (tt + 1) * 1024]
        xdec_t = lambda tt: bufB.ap()[:, 14848 + tt * 1024: 14848 + (tt + 1) * 1024]
        catn_t = lambda i: bufB.ap()[:, 16896 + i * 256: 16896 + (i + 1) * 256]
        states_ab = [bufB_f32[0:16, 10496:11520], bufB_f32[0:16, 11520:12544]]
        SIa = bufB_f32[0:16, 12544:13568]
        SI2a = bufB_f32[0:16, 13568:14592]
        cm0b = bufB[0:16, 29184:29440]
        bm0b = bufB[0:16, 29440:29696]
        cm0f = bufB_f32[0:16, 14848:15104]
        stage1 = wsl.ap()[:, 3072:3331]
        halo1 = wsl.ap()[:, 3331:3358]
        GmS = wsl.ap()[:, 3358:3870]
        WS = wsl.ap()[:, 3870:4382]
        LamS = wsl.ap()[:, 4382:4894]
        ydsb = wsl[0:16, 4894:5150]
        odsb = wsl.ap()[:, 5150:5406]
        CSELHP = hstrip.ap()[:, 0:1024]
        CINSEL = hstrip.ap()[:, 1024:2048]
        odxb = hstrip[0:64, 2048:2304]
        m1b = smallB[0:64, 0:256]
        adtb = smallB[0:64, 256:512]
        acsb = smallB[0:64, 512:768]
        sqf = [smallB.ap()[:, 1024:1280], smallB.ap()[:, 1280:1536]]
        acst = [smallA.ap()[:, 0:64], smallA.ap()[:, 64:128]]
        nacst = [smallA.ap()[:, 128:192], smallA.ap()[:, 192:256]]
        m1t = [smallA.ap()[:, 256:320], smallA.ap()[:, 320:384]]
        e2sb = smallA[0:16, 384:448]
        dect = [smallA.ap()[:, 448:512], smallA.ap()[:, 512:576]]
        mu2 = smallA[0:1, 576:832]
        v2 = smallA[0:1, 832:1088]
        ID64 = bass.AP(consts.ap().tensor, 128, [[384, 64], [1, 64]])
        onesrep = bass.AP(colsA.ap().tensor, 358, [[512, 64], [0, 256]])

        MASKNEG = hstrip_f32[:, 1152:1664]
        tok_mn = p.dma("sync", MASKNEG, k_maskneg, waits=gate5, slot="fmn")
        tok_shp = p.dma("gpsimd", CSELHP[0:64, :], k_selhp, waits=gate5, slot="fsel")
        tok_ins = p.dma("gpsimd", CINSEL[0:16, :], k_insel, waits=gate5, slot="fsel2")
        xn1b_toks = []
        for k in range(8):
            tkx = p.op("vector", "tensor_copy", xn1b_t(k), xn_t(k, 0, 2048), waits=gate5 + [tok_xn1], inc="v")
            xn1b_toks.append(tkx)
        tsi0 = p.op("vector", "memset", SIa, 0.0, waits=gate5, inc="v")
        th1 = p.op("vector", "memset", halo1, 0.0, waits=gate5, inc="v")
        p.drain("vector")

        slabf = [None, None]
        cdf = [None, None]
        stage1_free = None
        halo1_tok = [th1] * 9
        prev_acc = None
        prev_yd_acc = None
        tok_si = tsi0
        tok_e2 = None
        tok_states_prev = None
        prev_loop_v = None
        prev_loop_t = None
        prev_loop_s = None
        zt_rd = [[] for _ in range(8)]
        xbcc_rd = [[] for _ in range(9)]
        m1b_rd = []
        for c in range(8):
            zt_rd_new = [[] for _ in range(8)]
            xbcc_rd_new = [[] for _ in range(9)]
            m1b_rd_new = []
            tc0 = c * FC
            states_cur = states_ab[c % 2]
            states_prev = states_ab[(c + 1) % 2]
            if c > 0:
                gexp = bass.AP(smallA.ap().tensor, 384, [[1792, 16], [1, 64], [0, 16]])
                tu = p.op("vector", "tensor_tensor", SI2a, SIa, gexp, op=ALU.mult,
                          waits=[tok_e2, tok_si, tok_states_prev], inc="v")
                p.drain("vector")
                tok_si = p.op("vector", "tensor_tensor", SIa, SI2a, states_prev, op=ALU.add, inc="v")
                p.drain("vector")
            tsg_raw = None
            for m in range(17):
                si = m % 2
                srcap = bass.AP(w1_inT.tensor, m * 128, [[E1P, 128], [E1P * 128, 8], [1, 128]])
                dstap = bass.AP(wsl.ap().tensor, si * 1024, [[6144, 128], [128, 8], [1, 128]])
                twl = p.dma("gpsimd", dstap, srcap, waits=[slabf[si]], slot="fws%d" % si)
                bmm = banks.get(1)[0]
                pairs = [(wslab_blk(si, k), bcall.ap()[:, k * 2048 + tc0: k * 2048 + tc0 + FC]) for k in range(8)]
                tmm = p.mm(psum[bmm][:, 0:FC], pairs, waits=xn1b_toks + [twl, banks.free[bmm]])
                slabf[si] = tmm
                if m < 8:
                    tzv = p.op("vector", "tensor_copy", z_t(m), psum[bmm][:, 0:FC],
                               waits=[tmm] + zt_rd[m], inc="v")
                    banks.free[bmm] = tzv
                    continue
                ct = m - 8
                if m == 16:
                    tsg_raw = p.op("scalar", "activation", m1b, psum[bmm][0:64, 0:FC], ACT.Sigmoid,
                                   bias=colsA[0:64, 355:356], scale=-1.0,
                                   waits=[tmm, tok_setup_s, tok_setup_v] + m1b_rd, inc="s")
                    p.drain("scalar")
                thalo = p.op("vector", "tensor_copy", stage1[:, 0:3], halo1[:, ct * 3:ct * 3 + 3],
                             waits=[halo1_tok[ct], stage1_free], inc="v")
                p.drain("vector")
                tev = p.op("vector", "tensor_copy", stage1[:, 3:259], psum[bmm][:, 0:FC],
                           waits=[tmm, (tsg_raw if m == 16 else None)], inc="v")
                banks.free[bmm] = tev
                p.drain("vector")
                thl = p.op("vector", "tensor_copy", halo1[:, ct * 3:ct * 3 + 3], stage1[:, 256:259], inc="v")
                halo1_tok[ct] = thl
                csrc = bass.AP(cdg1.tensor, ct * 4 * 128 * 128, [[128, 128], [128 * 128, 4], [1, 128]])
                cdst = bass.AP(wsl.ap().tensor, 2048 + si * 512, [[6144, 128], [128, 4], [1, 128]])
                tcd = p.dma("gpsimd", cdst, csrc, waits=[cdf[si]], slot="fcd%d" % si)
                bcv = banks.get(1)[0]
                cpairs = [(bass.AP(wsl.ap().tensor, 2048 + si * 512 + kk * 128, [[6144, 128], [1, 128]]),
                           stage1[:, kk:kk + 256]) for kk in range(4)]
                tcv = p.mm(psum[bcv][:, 0:FC], cpairs, waits=[tcd, tev, thl, thalo, banks.free[bcv]])
                cdf[si] = tcv
                stage1_free = tcv
                txb = p.op("scalar", "activation", xbcc_t(ct), psum[bcv][:, 0:FC], ACT.Silu,
                           bias=CONVB1[:, ct:ct + 1], scale=1.0,
                           waits=[tcv] + xbcc_rd[ct], inc="s")
                banks.free[bcv] = txb
            tok_xbc = p.tok("s")
            tm1 = p.op("scalar", "activation", m1b, m1b, ACT.Ln, bias=EPSG[0:64, 0:1], scale=1.0, inc="s")
            p.drain("scalar")
            tadt = p.op("vector", "tensor_scalar", adtb, m1b, colsA[0:64, 356:357], None, op0=ALU.mult,
                        waits=[tm1, prev_loop_v], inc="v")
            m1b_rd_new.append(tadt)
            p.drain("vector")
            tacs = p.op("vector", "tensor_tensor_scan", acsb, onesrep, adtb, 0.0,
                        op0=ALU.mult, op1=ALU.add, waits=[tok_consts], inc="v")
            p.drain("vector")
            tacst = []
            for tt in range(2):
                btt = banks.get(1)[0]
                t_tr = p.op("tensor", "transpose", psum[btt][:, 0:64], smallB[0:64, 512 + tt * 128: 512 + (tt + 1) * 128],
                            ID64, waits=[tacs, banks.free[btt]], inc="t")
                t_e1 = p.op("vector", "tensor_copy", acst[tt], psum[btt][:, 0:64], waits=[t_tr, prev_loop_v], inc="v")
                t_e2 = p.op("vector", "tensor_scalar", nacst[tt], psum[btt][:, 0:64], -1.0, None, op0=ALU.mult,
                            waits=[t_tr], inc="v")
                banks.free[btt] = t_e2
                tacst.append((t_e1, t_e2))
                bt2 = banks.get(1)[0]
                t_tr2 = p.op("tensor", "transpose", psum[bt2][:, 0:64], smallB[0:64, tt * 128:(tt + 1) * 128],
                             ID64, waits=[tm1, tadt, banks.free[bt2]], inc="t")
                m1b_rd_new.append(t_tr2)
                t_e3 = p.op("vector", "tensor_copy", m1t[tt], psum[bt2][:, 0:64], waits=[t_tr2], inc="v")
                banks.free[bt2] = t_e3
                tacst.append((t_e3, t_e3))
            tod = []
            for tt in range(2):
                bto = banks.get(1)[0]
                t_tr3 = p.op("tensor", "transpose", psum[bto][0:64, 0:128], acst[tt], IDENT,
                             waits=[tacst[2 * tt][0], banks.free[bto]], inc="t")
                t_oe = p.op("scalar", "activation", odxb[:, tt * 128:(tt + 1) * 128], psum[bto][0:64, 0:128],
                            ACT.Exp, bias=0.0, scale=1.0, waits=[t_tr3, prev_loop_s, prev_loop_t], inc="s")
                banks.free[bto] = t_oe
                tod.append(t_oe)
            sel127 = bass.AP(consts.ap().tensor, 128 + 127, [[384, 128], [0, 128]])
            tcsl = p.op("tensor", "matmul", psum[7][:, 0:64], sel127, acst[1], start=True, stop=True,
                        waits=[tacst[2][0] if False else tacst[2 * 1][0], tok_consts, prev_loop_v], inc="t")
            tdec = []
            for tt in range(2):
                t_d1 = p.op("vector", "tensor_tensor", dect[tt], psum[7][:, 0:64], acst[tt], op=ALU.subtract,
                            waits=[tcsl, tacst[2 * tt][0], prev_loop_s], inc="v")
                p.drain("vector")
                t_d2 = p.op("scalar", "activation", dect[tt], dect[tt], ACT.Exp, bias=0.0, scale=1.0,
                            waits=[t_d1], inc="s")
                p.drain("scalar")
                tdec.append(t_d2)
            tok_e2 = p.op("scalar", "activation", e2sb, psum[7][0:16, 0:64], ACT.Exp, bias=0.0, scale=1.0,
                          waits=[tcsl, tok_si], inc="s")
            tbm0 = p.dma("gpsimd", bm0b, bufB[64:80, 8192 + 8 * 256: 8192 + 9 * 256], waits=[tok_xbc, prev_loop_t], slot="fbm")
            tcm0 = p.dma("gpsimd", cm0b, bufB[80:96, 8192 + 8 * 256: 8192 + 9 * 256], waits=[tok_xbc, prev_loop_t], slot="fcm")
            tcm0f = p.dma("gpsimd", cm0f, bufB[80:96, 8192 + 8 * 256: 8192 + 9 * 256], waits=[tok_xbc], slot="fcm2")
            xbcc_rd_new[8] += [tbm0, tcm0, tcm0f]
            txbt_toks = []
            for ct in range(9):
                for tt in range(2):
                    btx = banks.get(1)[0]
                    pbf = psum[btx].ap().bitcast(BF16)
                    t_tr = p.op("tensor", "transpose", pbf[:, 0:128],
                                bufB[0:128, 8192 + ct * 256 + tt * 128: 8192 + ct * 256 + (tt + 1) * 128],
                                IDENTB, waits=[tok_xbc, banks.free[btx], tok_constb], inc="t")
                    xbcc_rd_new[ct].append(t_tr)
                    t_ev = p.op("vector", "tensor_copy", bufB.ap()[:, 10496 + tt * 1152 + ct * 128: 10496 + tt * 1152 + (ct + 1) * 128],
                                pbf[:, 0:128], waits=[t_tr, prev_loop_t], inc="v")
                    banks.free[btx] = t_ev
                    txbt_toks.append(t_ev)
            txd = []
            for tt in range(2):
                m1exp = bass.AP(smallA.ap().tensor, 256 + tt * 64, [[1792, 128], [1, 64], [0, 16]])
                t_x1 = p.op("vector", "scalar_tensor_tensor", xdt_t(tt),
                            bufB.ap()[:, 10496 + tt * 1152: 10496 + tt * 1152 + 1024], -1.0, m1exp,
                            op0=ALU.mult, op1=ALU.mult, waits=txbt_toks + [tacst[2 * tt + 1][0], prev_loop_t], inc="v")
                p.drain("vector")
                dexp = bass.AP(smallA.ap().tensor, 448 + tt * 64, [[1792, 128], [1, 64], [0, 16]])
                t_x2 = p.op("vector", "tensor_tensor", xdec_t(tt), xdt_t(tt), dexp, op=ALU.mult,
                            waits=[tdec[tt]], inc="v")
                p.drain("vector")
                txd.append((t_x1, t_x2))
            st_toks = []
            for hg in range(8):
                bst = banks.get(1)[0]
                pairs = [(bufB.ap()[:, 10496 + tt * 1152 + 1088: 10496 + tt * 1152 + 1104],
                          bufB.ap()[:, 14848 + tt * 1024 + hg * 128: 14848 + tt * 1024 + (hg + 1) * 128])
                         for tt in range(2)]
                tsm = p.mm(psum[bst][0:16, 0:128], pairs, waits=[txd[0][1], txd[1][1]] + txbt_toks + [banks.free[bst]])
                sc0 = (10496 if c % 2 == 0 else 11520) + hg * 128
                tse = p.op("vector", "tensor_copy", bufB_f32[0:16, sc0:sc0 + 128],
                           psum[bst][0:16, 0:128], waits=[tsm, tok_si], inc="v")
                banks.free[bst] = tse
                st_toks.append(tse)
            tok_states_prev = st_toks[-1]
            tG = None
            for sh in range(2):
                tG = p.op("tensor", "matmul", psum[7][:, sh * 256:(sh + 1) * 256],
                          bm0b[0:16, sh * 128:(sh + 1) * 128],
                          cm0b, start=True, stop=True,
                          waits=[tbm0, tcm0, (tcsl if sh == 0 else None), p.tok("s") if sh == 0 else None], inc="t")
            tGm = p.op("vector", "tensor_tensor", GmS, psum[7][:, 0:512], MASKTRI, op=ALU.mult,
                       waits=[tG, tok_constb, prev_loop_s], inc="v")
            y_toks = []
            prev_w = None
            prev_lam = None
            prev_od = None
            prev_yd = None
            for hg in range(8):
                byo = banks.get(1)[0]
                si_st = bass.AP(bufB_f32.tensor, bufB_f32.offset + 12544 + hg * 128, [[16384, 16], [1, 128]])
                tyo = p.op("tensor", "matmul", psum[byo][:, 0:FC], si_st, cm0f, start=True, stop=True,
                           waits=[tok_si, tcm0f, banks.free[byo]], inc="t")
                bod = banks.get(1)[0]
                tob = p.op("tensor", "matmul", psum[bod][:, 0:FC], CSELHP[0:64, hg * 128:(hg + 1) * 128],
                           odxb, start=True, stop=True, waits=[tod[0], tod[1], tok_shp, banks.free[bod]], inc="t")
                tos = p.op("scalar", "activation", odsb, psum[bod][:, 0:FC], ACT.Copy, bias=0.0, scale=1.0,
                           waits=[tob, prev_od], inc="s")
                banks.free[bod] = tos
                prev_od = None
                tyv = p.op("vector", "tensor_tensor", y_t(hg), psum[byo][:, 0:FC], odsb, op=ALU.mult,
                           waits=[tyo, tos, prev_loop_t], inc="v")
                banks.free[byo] = tyv
                prev_od = tyv
                tacc = None
                for h8 in range(8):
                    h = hg * 8 + h8
                    blam = banks.get(1)[0]
                    selh = bass.AP(consts.ap().tensor, 128 + h, [[384, 64], [0, 128]])
                    acsap = bass.AP(smallB.ap().tensor, 512, [[1664, 64], [1, 256]])
                    p.op("tensor", "matmul", psum[blam][:, 0:256], selh, acsap, start=True, stop=False,
                         waits=[tacs, tok_consts, tok_mn, banks.free[blam]])
                    p.op("tensor", "matmul", psum[blam][:, 256:512], selh, acsap, start=False, stop=False)
                    tlb = p.op("tensor", "matmul", psum[blam][:, 0:512], IDENT, MASKNEG, start=False, stop=True,
                               inc="t")
                    tl1 = p.op("scalar", "activation", LamS[:, 0:256], psum[blam][:, 0:256], ACT.Exp,
                               bias=nacst[0][:, h:h + 1], scale=1.0,
                               waits=[tlb, tacst[1][1], prev_w, prev_lam], inc="s")
                    tl2 = p.op("scalar", "activation", LamS[:, 256:512], psum[blam][:, 256:512], ACT.Exp,
                               bias=nacst[1][:, h:h + 1], scale=1.0, waits=[tacst[3][1] if False else None], inc="s")
                    banks.free[blam] = tl2
                    prev_lam = tl2
                    tw = p.op("vector", "scalar_tensor_tensor", WS, LamS, 3.0e38, GmS,
                              op0=ALU.min, op1=ALU.mult, waits=[tl1, tl2, tGm, prev_w], inc="v")
                    byd = banks.get(1)[0]
                    ydpairs = [(bufB.ap()[:, 12800 + tt * 1024 + h * 16: 12800 + tt * 1024 + (h + 1) * 16],
                                WS[:, tt * 256:(tt + 1) * 256]) for tt in range(2)]
                    tyd = p.mm(psum[byd][0:16, 0:FC], ydpairs, waits=[tw, banks.free[byd]])
                    prev_w = tyd
                    tye = p.op("vector", "tensor_copy", ydsb, psum[byd][0:16, 0:FC],
                               waits=[tyd, prev_yd], inc="v")
                    banks.free[byd] = tye
                    tacc = p.op("tensor", "matmul", psum[6][:, 0:FC], CINSEL[0:16, h8 * 128:(h8 + 1) * 128],
                                ydsb, start=(h8 == 0), stop=(h8 == 7),
                                waits=[tye, tok_ins] + ([prev_yd_acc] if h8 == 0 else []), inc="t")
                    prev_yd = tacc
                ty2 = p.op("vector", "tensor_tensor", y_t(hg), y_t(hg), psum[6][:, 0:FC], op=ALU.add,
                           waits=[tacc, tyv], inc="v")
                prev_yd_acc = ty2
                p.drain("vector")
                ty3 = p.op("vector", "scalar_tensor_tensor", y_t(hg), xbcc_t(hg), DSK1[:, hg:hg + 1],
                           y_t(hg), op0=ALU.mult, op1=ALU.add, waits=[tok_xbc, tok_setup_v], inc="v")
                p.drain("vector")
                y_toks.append(ty3)
                xbcc_rd_new[hg].append(ty3)
            tiles = [("y", i) for i in range(8)] + [("z", i) for i in range(8)]
            bmu2, bsq3 = banks.get(2)
            mu_mm = None
            sq_mm2 = [None, None]
            for i, (kind, d) in enumerate(tiles):
                vt = y_t(d) if kind == "y" else z_t(d)
                wv = y_toks + [prev_loop_v]
                mu_mm = p.op("tensor", "matmul", psum[bmu2][0:1, 0:FC], R2048, vt,
                             start=(i == 0), stop=(i == 15),
                             waits=wv + ([banks.free[bmu2]] if i == 0 else []), inc="t")
                tqs = p.op("scalar", "activation", sqf[i % 2], vt, ACT.Square, bias=0.0, scale=1.0,
                           waits=wv + [sq_mm2[i % 2]], inc="s")
                if kind == "z":
                    zt_rd_new[d] += [mu_mm, tqs]
                sq_mm2[i % 2] = p.op("tensor", "matmul", psum[bsq3][0:1, 0:FC], R2048, sqf[i % 2],
                                     start=(i == 0), stop=(i == 15),
                                     waits=[tqs] + ([banks.free[bsq3]] if i == 0 else []), inc="t")
            tmu3 = p.op("vector", "tensor_copy", mu2, psum[bmu2][0:1, 0:FC], waits=[mu_mm], inc="v")
            banks.free[bmu2] = tmu3
            p.drain("vector")
            tm23 = p.op("vector", "tensor_tensor", v2, mu2, mu2, op=ALU.mult, inc="v")
            p.drain("vector")
            tvar3 = p.op("vector", "tensor_tensor", v2, psum[bsq3][0:1, 0:FC], v2, op=ALU.subtract,
                         waits=[sq_mm2[1]], inc="v")
            banks.free[bsq3] = tvar3
            ts13 = p.op("scalar", "activation", v2, v2, ACT.Sqrt, bias=EPS5[0:1, 0:1], scale=1.0,
                        waits=[tvar3], inc="s")
            tr13 = p.op("vector", "reciprocal", v2, v2, waits=[ts13], inc="v")
            p.drain("vector")
            brb3, bab3 = banks.get(2)
            trb3 = p.mm(psum[brb3][:, 0:FC], [(ONES, v2)], waits=[tr13, banks.free[brb3]])
            tab3 = p.mm(psum[bab3][:, 0:FC], [(ONES, mu2)], waits=[tmu3, tr13, banks.free[bab3]])
            catn_toks = []
            prev_cn = None
            for i, (kind, d) in enumerate(tiles):
                vt = y_t(d) if kind == "y" else z_t(d)
                t1c = p.op("vector", "tensor_tensor", sqf[i % 2], vt, psum[bab3][:, 0:FC], op=ALU.subtract,
                           waits=[trb3, tab3, p.tok("t"), catn_toks[i - 2] if i >= 2 else None], inc="v")
                p.drain("vector")
                t2c = p.op("vector", "tensor_tensor", sqf[i % 2], sqf[i % 2], psum[brb3][:, 0:FC], op=ALU.mult,
                           waits=[tab3], inc="v")
                p.drain("vector")
                p.drain("vector")
                t3c = p.op("vector", "tensor_scalar", catn_t(i), sqf[i % 2], LNW1[:, i:i + 1], LNB1[:, i:i + 1],
                           op0=ALU.mult, op1=ALU.add, waits=[prev_loop_t, prev_cn], inc="v")
                catn_toks.append(t3c)
                prev_cn = t3c
            banks.free[brb3] = catn_toks[-1]
            banks.free[bab3] = catn_toks[-1]
            for m in range(8):
                tw1 = p.dma("gpsimd", bass.AP(wsl.ap().tensor, 0, [[6144, 128], [128, 8], [1, 128]]),
                            bass.AP(w1_outT.tensor, m * 128, [[1024, 128], [1024 * 128, 8], [1, 128]]),
                            waits=[slabf[0]], slot="fws0")
                tw2 = p.dma("gpsimd", bass.AP(wsl.ap().tensor, 1024, [[6144, 128], [128, 8], [1, 128]]),
                            bass.AP(w1_outT.tensor, 8 * 128 * 1024 + m * 128, [[1024, 128], [1024 * 128, 8], [1, 128]]),
                            waits=[slabf[1]], slot="fws1")
                bwo = banks.get(1)[0]
                pairs = [(wslab_blk(0, k), catn_t(k)) for k in range(8)] + \
                        [(wslab_blk(1, k), catn_t(8 + k)) for k in range(8)]
                tmo = p.mm(psum[bwo][:, 0:FC], pairs, waits=catn_toks + [tw1, tw2, banks.free[bwo]])
                slabf[0] = tmo
                slabf[1] = tmo
                tev2 = p.op("vector", "tensor_copy", stg1[:, 0:FC], psum[bwo][:, 0:FC],
                            waits=[tmo, prev_acc], inc="v")
                banks.free[bwo] = tev2
                tac2 = p.dma_acc(out[m * 128:(m + 1) * 128, tc0:tc0 + FC], stg1[:, 0:FC],
                                 waits=[tev2, tok_ffn2], slot="acc")
                prev_acc = tac2
            prev_loop_v = p.tok("v")
            prev_loop_t = p.tok("t")
            prev_loop_s = p.tok("s")
            zt_rd, xbcc_rd, m1b_rd = zt_rd_new, xbcc_rd_new, m1b_rd_new
        tok_attn1 = p.tok("g.acc")

    if phases >= 7:
        p.drain("vector")
        p.drain("scalar")
        gate6 = [p.tok("v"), p.tok("s"), p.tok("t")]
        xr_t = lambda d: bcall.ap()[:, d * 512:(d + 1) * 512]
        xk_t = lambda d: bcall.ap()[:, 4096 + d * 512: 4096 + (d + 1) * 512]
        ka_t = lambda d: bcall.ap()[:, 8192 + d * 512: 8192 + (d + 1) * 512]
        r_t = lambda d: bufB_f32[:, d * 512:(d + 1) * 512]
        gtmp = smallB.ap()[:, 1024:1536]
        prev_g_acc = None
        prev_gmm = None
        prev_ka = None
        for c in range(NCH):
            c0 = c * CH
            mix_toks = []
            for d in range(8):
                for which, MU, OMU, dstf in ((0, MUR, ONEMMUR, xr_t), (1, MUK, ONEMMUK, xk_t)):
                    t1 = p.op("vector", "tensor_scalar", gtmp, xn_t(d, c0, c0 + CH), MU[:, d:d + 1], None,
                              op0=ALU.mult, waits=gate6 + [tok_xn1, tok_setup_v, prev_gmm], inc="v")
                    p.drain("vector")
                    if c == 0:
                        t2 = p.op("vector", "scalar_tensor_tensor", dstf(d)[:, 1:CH],
                                  xn_t(d, 0, CH - 1), OMU[:, d:d + 1], gtmp[:, 1:CH],
                                  op0=ALU.mult, op1=ALU.add, inc="v")
                        p.drain("vector")
                        t3 = p.op("vector", "tensor_copy", dstf(d)[:, 0:1], gtmp[:, 0:1], inc="v")
                        mix_toks.append(t3)
                    else:
                        t2 = p.op("vector", "scalar_tensor_tensor", dstf(d),
                                  xn_t(d, c0 - 1, c0 + CH - 1), OMU[:, d:d + 1], gtmp,
                                  op0=ALU.mult, op1=ALU.add, inc="v")
                        mix_toks.append(t2)
                    p.drain("vector")
            rk_toks = []
            for which, wt, name in ((0, w1_rT, "r"), (1, w1_kT, "k")):
                for m in range(8):
                    si = m % 2
                    twl = p.dma("gpsimd", bass.AP(wsl.ap().tensor, si * 1024, [[6144, 128], [128, 8], [1, 128]]),
                                bass.AP(wt.tensor, m * 128, [[1024, 128], [1024 * 128, 8], [1, 128]]),
                                waits=[slabf[si]], slot="fws%d" % si)
                    bmg = banks.get(1)[0]
                    src_t = xr_t if which == 0 else xk_t
                    pairs = [(wslab_blk(si, k), src_t(k)) for k in range(8)]
                    tmm = p.mm(psum[bmg][:, 0:CH], pairs, waits=mix_toks + [twl, banks.free[bmg]])
                    slabf[si] = tmm
                    prev_gmm = tmm
                    if which == 0:
                        te = p.op("scalar", "activation", r_t(m), psum[bmg][:, 0:CH], ACT.Sigmoid,
                                  bias=0.0, scale=1.0, waits=[tmm, p.tok("v")], inc="s")
                        banks.free[bmg] = te
                    else:
                        te1 = p.op("vector", "tensor_relu", gtmp, psum[bmg][:, 0:CH],
                                   waits=[tmm, p.tok("v"), prev_ka], inc="v")
                        banks.free[bmg] = te1
                        te = p.op("scalar", "activation", ka_t(m), gtmp, ACT.Square, bias=0.0, scale=1.0,
                                  waits=[te1, p.tok("s")], inc="s")
                        prev_ka = te
                        prev_gmm = te
                    rk_toks.append(te)
            for m in range(8):
                si = m % 2
                twl = p.dma("gpsimd", bass.AP(wsl.ap().tensor, si * 1024, [[6144, 128], [128, 8], [1, 128]]),
                            bass.AP(w1_vT.tensor, m * 128, [[1024, 128], [1024 * 128, 8], [1, 128]]),
                            waits=[slabf[si]], slot="fws%d" % si)
                bmv = banks.get(1)[0]
                pairs = [(wslab_blk(si, k), ka_t(k)) for k in range(8)]
                tmm = p.mm(psum[bmv][:, 0:CH], pairs, waits=rk_toks + [twl, banks.free[bmv]])
                slabf[si] = tmm
                prev_gmm = tmm
                tev = p.op("vector", "tensor_tensor", stg1[:, 0:CH], psum[bmv][:, 0:CH], r_t(m),
                           op=ALU.mult, waits=[tmm, prev_g_acc, prev_acc, p.tok("s")], inc="v")
                banks.free[bmv] = tev
                tac = p.dma_acc(out[m * 128:(m + 1) * 128, c0:c0 + CH], stg1[:, 0:CH],
                                waits=[tev], slot="acc")
                prev_g_acc = tac

    final_waits = [(name, p.sems[name][1]) for name in list(p.sems.keys())]
    p.emit("sync", lambda e: e.nop(), waits=final_waits)

    @blk.sync
    def _(e):
        for f in p.q["sync"]:
            f(e)

    @blk.tensor
    def _(e):
        for f in p.q["tensor"]:
            f(e)

    @blk.vector
    def _(e):
        for f in p.q["vector"]:
            f(e)

    @blk.scalar
    def _(e):
        for f in p.q["scalar"]:
            f(e)

    @blk.gpsimd
    def _(e):
        for f in p.q["gpsimd"]:
            f(e)

    es.close()
    return tap_list


def _host_inputs(inputs, cores):
    x = np.asarray(inputs["L0_attn_hidden_states"])
    g = {}
    g["w_inproj"] = np.ascontiguousarray(inputs["L0_attn_in_proj_w"])
    cw = np.asarray(inputs["L0_attn_conv_w"])
    cd0 = np.zeros((16, 4, 128, 128), dtype=np.float32)
    idx = np.arange(128)
    for dt in range(16):
        for k in range(4):
            cd0[dt, k, idx, idx] = cw[dt * 128 + idx, k]
    g["cdg0"] = cd0.reshape(64, 128, 128)
    g["w_xproj"] = np.ascontiguousarray(inputs["L0_attn_x_proj_w"])
    g["w_dtproj"] = np.ascontiguousarray(inputs["L0_attn_dt_proj_w"])
    g["w_outproj"] = np.ascontiguousarray(inputs["L0_attn_out_proj_w"])
    g["w_gate"] = np.ascontiguousarray(inputs["L0_attn_gate_w"])
    g["w_up"] = np.ascontiguousarray(inputs["L0_attn_up_w"])
    g["w_down"] = np.ascontiguousarray(inputs["L0_attn_down_w"])
    g["w_ffn_in"] = np.ascontiguousarray(inputs["L0_ffn_W_in"])
    g["w_ffn_out"] = np.ascontiguousarray(inputs["L0_ffn_W_out"])
    w1in = np.asarray(inputs["L1_attn_W_in"])
    w1inT = np.zeros((D, E1P), dtype=np.float32)
    w1inT[:, :2048] = w1in[:2048].T
    w1inT[:, 2048:2112] = w1in[2080:2144].T
    w1inT[:, 2112:2144] = w1in[2048:2080].T
    g["w1_inT"] = w1inT
    cw1 = np.asarray(inputs["L1_attn_conv_w"]).reshape(CONVD, 4)
    cd1 = np.zeros((9, 4, 128, 128), dtype=np.float32)
    for ct in range(8):
        for k in range(4):
            rows = np.arange(128)
            cd1[ct, k, rows, rows] = cw1[ct * 128 + rows, k]
    for k in range(4):
        rows = np.arange(64, 96)
        cd1[8, k, rows, rows] = cw1[1024 + rows - 64, k]
    g["cdg1"] = cd1.reshape(36, 128, 128)
    g["w1_outT"] = np.ascontiguousarray(np.asarray(inputs["L1_attn_W_out"]).T)
    g["w1_rT"] = np.ascontiguousarray(np.asarray(inputs["L1_ffn_W_r"]).T)
    g["w1_kT"] = np.ascontiguousarray(np.asarray(inputs["L1_ffn_W_k"]).T)
    g["w1_vT"] = np.ascontiguousarray(np.asarray(inputs["L1_ffn_W_v"]).T)
    g["c_dtlnw"] = np.asarray(inputs["L0_attn_dt_ln_w"]).reshape(64, 1)
    g["c_blnw"] = np.asarray(inputs["L0_attn_b_ln_w"]).reshape(16, 1)
    g["c_bclnw"] = np.concatenate([np.asarray(inputs["L0_attn_b_ln_w"]), np.asarray(inputs["L0_attn_c_ln_w"])]).reshape(32, 1)
    g["c_clnw"] = np.asarray(inputs["L0_attn_c_ln_w"]).reshape(16, 1)
    g["c_dtprojb"] = np.ascontiguousarray(np.asarray(inputs["L0_attn_dt_proj_b"]).reshape(16, 128).T)
    alog = np.asarray(inputs["L0_attn_A_log"])
    asc = np.zeros((128, 256), dtype=np.float32)
    for n in range(16):
        for dt in range(16):
            asc[:, n * 16 + dt] = alog[dt * 128: (dt + 1) * 128, n]
    g["c_asc"] = asc
    g["c_dsk"] = np.ascontiguousarray(np.asarray(inputs["L0_attn_D_skip"]).reshape(16, 128).T)
    g["c_convb0"] = np.ascontiguousarray(np.asarray(inputs["L0_attn_conv_b"]).reshape(16, 128).T)
    g["c_premoe"] = np.ascontiguousarray(np.asarray(inputs["L0_attn_pre_moe_ln_w"]).reshape(8, 128).T)
    g["c_bin"] = np.ascontiguousarray(np.asarray(inputs["L0_ffn_b_in"]).reshape(32, 128).T)
    g["c_bout"] = np.ascontiguousarray(np.asarray(inputs["L0_ffn_b_out"]).reshape(8, 128).T)
    g["c_dtbias1"] = np.asarray(inputs["L1_attn_dt_bias"]).reshape(64, 1)
    g["c_alog1"] = np.asarray(inputs["L1_attn_A_log"]).reshape(64, 1)
    g["c_dsk1"] = np.ascontiguousarray(np.repeat(np.asarray(inputs["L1_attn_Dskip"]), 16).reshape(8, 128).T)
    g["c_lnw1"] = np.ascontiguousarray(np.asarray(inputs["L1_attn_ln_w"]).reshape(16, 128).T)
    g["c_lnb1"] = np.ascontiguousarray(np.asarray(inputs["L1_attn_ln_b"]).reshape(16, 128).T)
    g["c_mur"] = np.ascontiguousarray(np.asarray(inputs["L1_ffn_mu_r"]).reshape(8, 128).T)
    g["c_muk"] = np.ascontiguousarray(np.asarray(inputs["L1_ffn_mu_k"]).reshape(8, 128).T)
    cb1 = np.zeros((CONVDP,), dtype=np.float32)
    cb1[:1024] = np.asarray(inputs["L1_attn_conv_b"])[:1024]
    cb1[8 * 128 + 64: 8 * 128 + 96] = np.asarray(inputs["L1_attn_conv_b"])[1024:1056]
    g["c_convb1"] = np.ascontiguousarray(cb1.reshape(9, 128).T)
    g["k_ones"] = np.ones((1, 128), dtype=np.float32)
    g["k_mones"] = -np.ones((1, 128), dtype=np.float32)
    g["k_ident"] = np.eye(128, dtype=np.float32)
    c3 = np.zeros((96, 3), dtype=np.float32)
    c3[0:64, 0] = 1.0 / 64.0
    c3[64:80, 1] = 1.0 / 16.0
    c3[80:96, 2] = 1.0 / 16.0
    g["k_c3"] = c3
    b3 = np.zeros((3, 96), dtype=np.float32)
    b3[0, 0:64] = 1.0
    b3[1, 64:80] = 1.0
    b3[2, 80:96] = 1.0
    g["k_b3"] = b3
    sel8 = np.zeros((8, 128), dtype=np.float32)
    for h in range(8):
        sel8[h, h * 16:(h + 1) * 16] = 1.0
    g["k_sel8"] = sel8
    mt = np.zeros((128, 512), dtype=np.float32)
    s = np.arange(128)[:, None]
    l = np.arange(256)[None, :]
    mt[:, 0:256] = (s <= l).astype(np.float32)
    mt[:, 256:512] = ((s + 128) <= l).astype(np.float32)
    g["k_masktri"] = mt
    g["k_maskneg"] = (1.0 - mt) * (-1.0e9)
    g["k_r1024"] = np.full((128, 1), 1.0 / 1024.0, dtype=np.float32)
    g["k_r2048"] = np.full((128, 1), 1.0 / 2048.0, dtype=np.float32)
    g["k_onescol"] = np.ones((128, 1), dtype=np.float32)
    g["k_eps"] = np.tile(np.array([[1e-5, 1e-6, 1e-38]], dtype=np.float32), (128, 1))
    selhp = np.zeros((64, 1024), dtype=np.float32)
    for hg in range(8):
        for q in range(128):
            selhp[hg * 8 + q // 16, hg * 128 + q] = 1.0
    g["k_selhp"] = selhp
    insel = np.zeros((16, 1024), dtype=np.float32)
    for hp in range(8):
        for r in range(16):
            insel[r, hp * 128 + 16 * hp + r] = 1.0
    g["k_insel"] = insel
    for k in g:
        g[k] = np.ascontiguousarray(np.asarray(g[k], dtype=np.float32))

    in_maps = []
    for i in range(cores):
        m = dict(g)
        m["x0t"] = np.ascontiguousarray(np.asarray(x[i]).T.astype(np.float32))
        in_maps.append(m)
    return in_maps


def kernel(**inputs):
    in_maps = _host_inputs(inputs, N_CORES)
    nc = bass.Bass("TRN2", target_bir_lowering=False)
    build(nc, num_cores=N_CORES, phases=PHASES, taps=TAPS)
    res = run_bass_kernel_spmd(nc, in_maps, core_ids=list(range(N_CORES)))
    out = np.stack([np.asarray(r["out"]).T for r in res.results], axis=0)
    return out.astype(np.float32)
